# version 11
# baseline (speedup 1.0000x reference)
import math
from contextlib import ExitStack

import numpy as np
import concourse.bass as bass
import concourse.mybir as mybir
from concourse.bass_utils import run_bass_kernel_spmd

F32 = mybir.dt.float32
BF16 = mybir.dt.bfloat16
AF = mybir.ActivationFunctionType
ALU = mybir.AluOpType
AX = mybir.AxisListType

T = 4096
C = 1024
NCH = 8
NTCH = 32
NT = 8
H = 16
HD = 64
NE = 8
HFF = 2048


class Sems:

    def __init__(self, nc, stack, names):
        self.nc = nc
        self.h = {}
        self.v = {}
        self.marks = {}
        for n in names:
            self.h[n] = stack.enter_context(nc.semaphore(n))
            self.v[n] = 0

    def inc(self, name, inst, n=1):
        inst.then_inc(self.h[name], n)
        self.v[name] += n
        return self.v[name]

    def base(self, *names):
        return {n: self.v[n] for n in names}


SEM_NAMES = ["ld0", "ld1", "tns", "vec", "scl", "sto", "cast", "aux"]



def emit_common(nc, D, S):
    ident = D["cm_ident"]
    ones_bf = D["cm_ones_bf"]
    onescol = D["cm_onescol"]
    with nc.Block() as block:
        @block.gpsimd
        def _(g):
            g.memset(ident[:, :], 0.0)
            g.affine_select(out=ident[:, :], in_=ident[:, :], compare_op=ALU.not_equal,
                            fill=1.0, base=0, pattern=[[-1, 128]], channel_multiplier=1)
            g.memset(onescol[:, :], 1.0)
            g.memset(ones_bf[:, :], 1.0)
            g.memset(D["cm_eps"][:, :], 1e-5)
            g.memset(D["cm_onescol_bf"][:, :], 1.0)
            g.memset(D["cm_ident_bf"][:, :], 0.0)
            g.affine_select(out=D["cm_ident_bf"][:, :], in_=D["cm_ident_bf"][:, :],
                            compare_op=ALU.not_equal, fill=1.0, base=0,
                            pattern=[[-1, 128]], channel_multiplier=1)


COMMON_SBUF = {
    "cm_ident": ([128, 128], F32),
    "cm_ident_bf": ([128, 128], BF16),
    "cm_ones_bf": ([128, 1], BF16),
    "cm_onescol": ([128, 128], F32),
    "cm_onescol_bf": ([128, 128], BF16),
    "cm_eps": ([1, 1], F32),
}



def emit_entry(nc, D, S):
    x_in, scr = D["x_in"], D["scr_out"]
    ident = D["cm_ident"]
    with ExitStack() as ctx:
        xt = [ctx.enter_context(nc.sbuf_tensor(f"en_xt{i}", [128, C], F32)) for i in range(2)]
        xf = [ctx.enter_context(nc.sbuf_tensor(f"en_xf{c}", [128, T], F32)) for c in range(NCH)]
        ps = [ctx.enter_context(nc.psum_tensor(f"en_ps{i}", [128, 512], F32)) for i in range(4)]
        B = S.base("ld0", "tns", "vec", "sto")

        with nc.Block() as block:
            @block.sync
            def _(sync):
                for tc in range(NTCH):
                    if tc >= 2:
                        sync.wait_ge(S.h["tns"], B["tns"] + 2 * (tc - 1))
                    S.inc("ld0", sync.dma_start(out=xt[tc % 2][:, :],
                                                in_=x_in[tc * 128:(tc + 1) * 128, :]), 16)

            @block.tensor
            def _(t):
                for tc in range(NTCH):
                    t.wait_ge(S.h["ld0"], B["ld0"] + 16 * (tc + 1))
                    if tc >= 2:
                        t.wait_ge(S.h["vec"], B["vec"] + 2 * (tc - 1))
                    for c in range(NCH):
                        bank = ps[(tc % 2) * 2 + c // 4]
                        inst = t.transpose(bank[:, (c % 4) * 128:(c % 4 + 1) * 128],
                                           xt[tc % 2][:, c * 128:(c + 1) * 128], ident[:, :])
                        if c in (3, 7):
                            S.inc("tns", inst, 1)

            @block.vector
            def _(v):
                for tc in range(NTCH):
                    for half in range(2):
                        v.wait_ge(S.h["tns"], B["tns"] + 2 * tc + half + 1)
                        bank = ps[(tc % 2) * 2 + half]
                        for cc in range(4):
                            c = half * 4 + cc
                            inst = v.tensor_copy(xf[c][:, tc * 128:(tc + 1) * 128],
                                                 bank[:, cc * 128:(cc + 1) * 128])
                            if cc == 3:
                                S.inc("vec", inst, 1)

        with nc.Block() as block:
            @block.sync
            def _(sync):
                for c in range(NCH):
                    S.inc("sto", sync.dma_start(out=scr[c * 128:(c + 1) * 128, :], in_=xf[c][:, :]), 16)
                sync.wait_ge(S.h["sto"], S.v["sto"])


def emit_exit(nc, D, S):
    scr, x_out = D["scr_in"], D["x_out"]
    ident = D["cm_ident"]
    with ExitStack() as ctx:
        xf = [ctx.enter_context(nc.sbuf_tensor(f"ex_xf{c}", [128, T], F32)) for c in range(NCH)]
        xt = [ctx.enter_context(nc.sbuf_tensor(f"ex_xt{i}", [128, C], F32)) for i in range(2)]
        ps = [ctx.enter_context(nc.psum_tensor(f"ex_ps{i}", [128, 512], F32)) for i in range(4)]
        B = S.base("ld0", "tns", "vec", "sto")

        with nc.Block() as block:
            @block.sync
            def _(sync):
                for c in range(NCH):
                    S.inc("ld0", sync.dma_start(out=xf[c][:, :], in_=scr[c * 128:(c + 1) * 128, :]), 16)

        with nc.Block() as block:
            @block.tensor
            def _(t):
                t.wait_ge(S.h["ld0"], B["ld0"] + 128)
                for tc in range(NTCH):
                    if tc >= 2:
                        t.wait_ge(S.h["sto"], B["sto"] + 16 * (tc - 1))
                    for c in range(NCH):
                        bank = ps[(tc % 2) * 2 + c // 4]
                        inst = t.transpose(bank[:, (c % 4) * 128:(c % 4 + 1) * 128],
                                           xf[c][:, tc * 128:(tc + 1) * 128], ident[:, :])
                        if c in (3, 7):
                            S.inc("tns", inst, 1)

            @block.vector
            def _(v):
                for tc in range(NTCH):
                    for half in range(2):
                        v.wait_ge(S.h["tns"], B["tns"] + 2 * tc + half + 1)
                        bank = ps[(tc % 2) * 2 + half]
                        for cc in range(4):
                            c = half * 4 + cc
                            inst = v.tensor_copy(xt[tc % 2][:, c * 128:(c + 1) * 128],
                                                 bank[:, cc * 128:(cc + 1) * 128])
                            if c == 7:
                                S.inc("vec", inst, 1)

            @block.sync
            def _(sync):
                for tc in range(NTCH):
                    sync.wait_ge(S.h["vec"], B["vec"] + tc + 1)
                    S.inc("sto", sync.dma_start(out=x_out[tc * 128:(tc + 1) * 128, :],
                                                in_=xt[tc % 2][:, :]), 16)
                sync.wait_ge(S.h["sto"], S.v["sto"])




def emit_prep_l0ffn(nc, D, S):
    with nc.Block() as block:
        @block.gpsimd
        def _(g):
            S.marks["wr_bf"] = S.inc("cast", g.dma_start(out=D["wr_bf"], in_=D["W_r"]), 16)
            S.marks["wk_bf"] = S.inc("cast", g.dma_start(out=D["wk_bf"], in_=D["W_k"]), 16)
            S.marks["wv_bf"] = S.inc("cast", g.dma_start(out=D["wv_bf"], in_=D["W_v"]), 16)


def _emit_ln_stats(nc, D, S, ctx, scr, a_sb, b_sb, tag=""):
    ones_bf = D["cm_ones_bf"]
    TH = T // 2
    with ExitStack() as c2:
        xs32 = [c2.enter_context(nc.sbuf_tensor(f"{tag}ls_x{i}", [128, TH], F32)) for i in range(2)]
        xb = c2.enter_context(nc.sbuf_tensor(f"{tag}ls_xb", [128, NCH * TH], BF16))
        s1_sb = c2.enter_context(nc.sbuf_tensor(f"{tag}ls_s1", [1, T], F32))
        s2_sb = c2.enter_context(nc.sbuf_tensor(f"{tag}ls_s2", [1, T], F32))
        tmp = c2.enter_context(nc.sbuf_tensor(f"{tag}ls_tmp", [1, T], F32))
        tmp2 = c2.enter_context(nc.sbuf_tensor(f"{tag}ls_tmp2", [1, 512], F32))
        ps_s = [c2.enter_context(nc.psum_tensor(f"{tag}ls_ps{i}", [1, 512], F32)) for i in range(2)]
        B = S.base("ld0", "tns", "vec", "scl")
        cp = lambda h, c: B["scl"] + 16 * h + (c + 1)
        sq = lambda h, c: B["scl"] + 16 * h + 8 + (c + 1)
        g1 = lambda h, ts: B["tns"] + 8 * h + (ts + 1)
        g2 = lambda h, ts: B["tns"] + 8 * h + 4 + (ts + 1)
        e1 = lambda h, ts: B["vec"] + 8 * h + (ts + 1)
        e2 = lambda h, ts: B["vec"] + 8 * h + 4 + (ts + 1)

        with nc.Block() as block:
            @block.sync
            def _(sync):
                for h2 in range(2):
                    for c in range(NCH):
                        if h2 * NCH + c >= 2:
                            pc = h2 * NCH + c - 2
                            sync.wait_ge(S.h["scl"], cp(pc // NCH, pc % NCH))
                        S.inc("ld0", sync.dma_start(
                            out=xs32[c % 2][:, :],
                            in_=scr[c * 128:(c + 1) * 128, h2 * TH:(h2 + 1) * TH]), 16)

            @block.scalar
            def _(s):
                for h2 in range(2):
                    if h2 > 0:
                        s.wait_ge(S.h["tns"], g2(h2 - 1, 3))
                    for c in range(NCH):
                        s.wait_ge(S.h["ld0"], B["ld0"] + 16 * (h2 * NCH + c + 1))
                        S.inc("scl", s.copy(xb[:, c * TH:(c + 1) * TH], xs32[c % 2][:, :]), 1)
                    s.wait_ge(S.h["tns"], g1(h2, 3))
                    for c in range(NCH):
                        S.inc("scl", s.activation(xb[:, c * TH:(c + 1) * TH],
                                                  xb[:, c * TH:(c + 1) * TH], AF.Square), 1)

            @block.tensor
            def _(t):
                for h2 in range(2):
                    t.wait_ge(S.h["scl"], cp(h2, 7))
                    for ts_ in range(4):
                        if h2 > 0 or ts_ >= 2:
                            pt = h2 * 8 + ts_ - 2
                            t.wait_ge(S.h["vec"], B["vec"] + pt + 1)
                        for c in range(NCH):
                            inst = t.matmul(ps_s[ts_ % 2][:, :], ones_bf[:, :],
                                            xb[:, c * TH + ts_ * 512: c * TH + (ts_ + 1) * 512],
                                            start=(c == 0), stop=(c == NCH - 1))
                        S.inc("tns", inst, 1)
                    t.wait_ge(S.h["scl"], sq(h2, 7))
                    for ts_ in range(4):
                        pt = h2 * 8 + 4 + ts_ - 2
                        t.wait_ge(S.h["vec"], B["vec"] + pt + 1)
                        for c in range(NCH):
                            inst = t.matmul(ps_s[ts_ % 2][:, :], ones_bf[:, :],
                                            xb[:, c * TH + ts_ * 512: c * TH + (ts_ + 1) * 512],
                                            start=(c == 0), stop=(c == NCH - 1))
                        S.inc("tns", inst, 1)

            @block.vector
            def _(v):
                for h2 in range(2):
                    for ts_ in range(4):
                        v.wait_ge(S.h["tns"], g1(h2, ts_))
                        S.inc("vec", v.tensor_copy(
                            s1_sb[:, h2 * TH + ts_ * 512: h2 * TH + (ts_ + 1) * 512],
                            ps_s[ts_ % 2][:, :]), 1)
                    for ts_ in range(4):
                        v.wait_ge(S.h["tns"], g2(h2, ts_))
                        S.inc("vec", v.tensor_copy(
                            s2_sb[:, h2 * TH + ts_ * 512: h2 * TH + (ts_ + 1) * 512],
                            ps_s[ts_ % 2][:, :]), 1)
                for ts_ in range(8):
                    sl = slice(ts_ * 512, (ts_ + 1) * 512)
                    v.tensor_scalar(tmp[:, sl], s1_sb[:, sl], 1.0 / C, None, ALU.mult)
                    v.tensor_tensor(tmp2[:, :], tmp[:, sl], tmp[:, sl], ALU.mult)
                    inst = v.scalar_tensor_tensor(s2_sb[:, sl], s2_sb[:, sl], 1.0 / C,
                                                  tmp2[:, :], ALU.mult, ALU.subtract)
                S.inc("vec", inst, 1)

            @block.scalar
            def _(s):
                s.wait_ge(S.h["vec"], B["vec"] + 17)
                s.activation(s2_sb[:, :], s2_sb[:, :], AF.Ln, bias=D["cm_eps"][0:1, 0:1])
                S.inc("scl", s.activation(a_sb[:, :], s2_sb[:, :], AF.Exp, scale=-0.5), 1)

            @block.vector
            def _(v):
                v.wait_ge(S.h["scl"], B["scl"] + 33)
                S.inc("vec", v.scalar_tensor_tensor(b_sb[:, :], tmp[:, :], -1.0,
                                                    a_sb[:, :], ALU.mult, ALU.mult), 1)


def emit_l0ffn(nc, D, S, tag="f1"):
    scr_in, scr_out = D["scr_in"], D["scr_out"]
    ident, onescol = D["cm_ident"], D["cm_onescol"]
    with ExitStack() as ctx:
        wrT = ctx.enter_context(nc.sbuf_tensor(f"{tag}_wrT", [128, NCH * C], BF16))
        wkT = ctx.enter_context(nc.sbuf_tensor(f"{tag}_wkT", [128, NCH * C], BF16))
        wvT = ctx.enter_context(nc.sbuf_tensor(f"{tag}_wvT", [128, NCH * C], BF16))
        mus = ctx.enter_context(nc.sbuf_tensor(f"{tag}_mus", [1, 2 * C], F32))
        muT = ctx.enter_context(nc.sbuf_tensor(f"{tag}_muT", [128, 16], F32))
        a_sb = ctx.enter_context(nc.sbuf_tensor(f"{tag}_a", [1, T], F32))
        b_sb = ctx.enter_context(nc.sbuf_tensor(f"{tag}_b", [1, T], F32))

        with ExitStack() as c2:
            ps_mu = c2.enter_context(nc.psum_tensor(f"{tag}_psmu", [128, 16], F32))
            B = S.base("ld1", "tns", "cast")
            with nc.Block() as block:
                @block.sync
                def _(sync):
                    sync.wait_ge(S.h["cast"], S.marks["wr_bf"])
                    for c in range(NCH):
                        S.inc("ld1", sync.dma_start(out=wrT[:, c * C:(c + 1) * C],
                                                    in_=D["wr_bf"][:, c * 128:(c + 1) * 128],
                                                    transpose=True), 16)
                    sync.wait_ge(S.h["cast"], S.marks["wk_bf"])
                    for c in range(NCH):
                        S.inc("ld1", sync.dma_start(out=wkT[:, c * C:(c + 1) * C],
                                                    in_=D["wk_bf"][:, c * 128:(c + 1) * 128],
                                                    transpose=True), 16)
                    sync.wait_ge(S.h["cast"], S.marks["wv_bf"])
                    for c in range(NCH):
                        S.inc("ld1", sync.dma_start(out=wvT[:, c * C:(c + 1) * C],
                                                    in_=D["wv_bf"][:, c * 128:(c + 1) * 128],
                                                    transpose=True), 16)
                    S.inc("ld1", sync.dma_start(out=mus[:, 0:C], in_=D["mu_r"]), 16)
                    S.inc("ld1", sync.dma_start(out=mus[:, C:2 * C], in_=D["mu_k"]), 16)

                @block.tensor
                def _(t):
                    t.wait_ge(S.h["ld1"], S.v["ld1"])
                    for j in range(16):
                        t.transpose(ps_mu[:, j:j + 1], mus[0:1, j * 128:(j + 1) * 128],
                                    ident[0:1, 0:1])

            with nc.Block() as block:
                @block.vector
                def _(v):
                    v.tensor_copy(muT[:, :], ps_mu[:, :])

        _emit_ln_stats(nc, D, S, ctx, scr_in, a_sb, b_sb, tag)

        with ExitStack() as c2:
            xs = c2.enter_context(nc.sbuf_tensor(f"{tag}_xs", [128, 2 * NCH * 512], F32))
            xnb = c2.enter_context(nc.sbuf_tensor(f"{tag}_xn", [128, NCH * 513], BF16))
            xr = c2.enter_context(nc.sbuf_tensor(f"{tag}_xr", [128, NCH * 512], BF16))
            xk = c2.enter_context(nc.sbuf_tensor(f"{tag}_xk", [128, NCH * 512], BF16))
            rsb = c2.enter_context(nc.sbuf_tensor(f"{tag}_r", [128, NCH * 512], BF16))
            krelu = c2.enter_context(nc.sbuf_tensor(f"{tag}_kr", [128, 2 * 512], BF16))
            kact = c2.enter_context(nc.sbuf_tensor(f"{tag}_ka", [128, NCH * 512], BF16))
            dbuf = c2.enter_context(nc.sbuf_tensor(f"{tag}_d", [128, 512], F32))
            tmpf = c2.enter_context(nc.sbuf_tensor(f"{tag}_tf", [128, 512], F32))
            abx = c2.enter_context(nc.sbuf_tensor(f"{tag}_abx", [128, 1024], F32))
            ps_ab = c2.enter_context(nc.psum_tensor(f"{tag}_psab", [128, 1024], F32))
            psA = [c2.enter_context(nc.psum_tensor(f"{tag}_psA{i}", [128, 512], F32)) for i in range(2)]

            B = S.base("ld0", "tns", "vec", "scl", "sto")
            ld_done = lambda i: B["ld0"] + 128 * (i + 1)
            ab_mm = lambda i: B["tns"] + 25 * i + 1
            r_mm = lambda i, o: B["tns"] + 25 * i + 1 + (o + 1)
            k_mm = lambda i, o: B["tns"] + 25 * i + 9 + (o + 1)
            v_mm = lambda i, o: B["tns"] + 25 * i + 17 + (o + 1)
            sig = lambda i, o: B["scl"] + 16 * i + (o + 1)
            relu = lambda i, o: B["scl"] + 16 * i + 8 + (o + 1)
            ab_evac = lambda i: B["vec"] + 18 * i + 1
            mix_rdy = lambda i: B["vec"] + 18 * i + 2
            kact_v = lambda i, o: B["vec"] + 18 * i + 2 + (o + 1)
            upd_v = lambda i, o: B["vec"] + 18 * i + 10 + (o + 1)
            sto_done = lambda i: B["sto"] + 128 * (i + 1)

            def xsv(i, c):
                return xs[:, ((i % 2) * NCH + c) * 512:((i % 2) * NCH + c) * 512 + 512]

            with nc.Block() as block:
                @block.sync
                def _(sync):
                    for i in range(2):
                        for c in range(NCH):
                            S.inc("ld0", sync.dma_start(
                                out=xsv(i, c),
                                in_=scr_in[c * 128:(c + 1) * 128, i * 512:(i + 1) * 512]), 16)
                    for i in range(NT):
                        for o in range(NCH):
                            sync.wait_ge(S.h["vec"], upd_v(i, o))
                            S.inc("sto", sync.dma_start(
                                out=scr_out[o * 128:(o + 1) * 128, i * 512:(i + 1) * 512],
                                in_=xsv(i, o)), 16)
                        if i + 2 < NT:
                            sync.wait_ge(S.h["sto"], sto_done(i))
                            for c in range(NCH):
                                S.inc("ld0", sync.dma_start(
                                    out=xsv(i + 2, c),
                                    in_=scr_in[c * 128:(c + 1) * 128,
                                               (i + 2) * 512:(i + 3) * 512]), 16)
                    sync.wait_ge(S.h["sto"], sto_done(NT - 1))

                @block.tensor
                def _(t):
                    for i in range(NT):
                        if i >= 1:
                            t.wait_ge(S.h["vec"], ab_evac(i - 1))
                        t.matmul(ps_ab[:, 0:512], onescol[0:1, :], a_sb[:, i * 512:(i + 1) * 512],
                                 start=True, stop=True)
                        inst = t.matmul(ps_ab[:, 512:1024], onescol[0:1, :], b_sb[:, i * 512:(i + 1) * 512],
                                        start=True, stop=True)
                        S.inc("tns", inst, 1)
                        t.wait_ge(S.h["vec"], mix_rdy(i))
                        for o in range(NCH):
                            if o >= 2:
                                t.wait_ge(S.h["scl"], sig(i, o - 2))
                            elif i > 0:
                                t.wait_ge(S.h["vec"], upd_v(i - 1, o + 6))
                            for c in range(NCH):
                                inst = t.matmul(psA[o % 2][:, :],
                                                wrT[:, c * C + o * 128: c * C + (o + 1) * 128],
                                                xr[:, c * 512:(c + 1) * 512],
                                                start=(c == 0), stop=(c == NCH - 1))
                            S.inc("tns", inst, 1)
                        for o in range(NCH):
                            if o < 2:
                                t.wait_ge(S.h["scl"], sig(i, o + 6))
                            else:
                                t.wait_ge(S.h["vec"], kact_v(i, o - 2))
                            for c in range(NCH):
                                inst = t.matmul(psA[o % 2][:, :],
                                                wkT[:, c * C + o * 128: c * C + (o + 1) * 128],
                                                xk[:, c * 512:(c + 1) * 512],
                                                start=(c == 0), stop=(c == NCH - 1))
                            S.inc("tns", inst, 1)
                        t.wait_ge(S.h["vec"], kact_v(i, 7))
                        for o in range(NCH):
                            if o >= 2:
                                t.wait_ge(S.h["vec"], upd_v(i, o - 2))
                            for c in range(NCH):
                                inst = t.matmul(psA[o % 2][:, :],
                                                wvT[:, c * C + o * 128: c * C + (o + 1) * 128],
                                                kact[:, c * 512:(c + 1) * 512],
                                                start=(c == 0), stop=(c == NCH - 1))
                            S.inc("tns", inst, 1)

                @block.scalar
                def _(s):
                    for i in range(NT):
                        for o in range(NCH):
                            s.wait_ge(S.h["tns"], r_mm(i, o))
                            S.inc("scl", s.activation(rsb[:, o * 512:(o + 1) * 512],
                                                      psA[o % 2][:, :], AF.Sigmoid), 1)
                        for o in range(NCH):
                            s.wait_ge(S.h["tns"], k_mm(i, o))
                            if o >= 2:
                                s.wait_ge(S.h["vec"], kact_v(i, o - 2))
                            S.inc("scl", s.activation(krelu[:, (o % 2) * 512:(o % 2 + 1) * 512],
                                                      psA[o % 2][:, :], AF.Relu), 1)

                @block.vector
                def _(v):
                    for i in range(NT):
                        v.wait_ge(S.h["tns"], ab_mm(i))
                        v.tensor_copy(abx[:, 0:512], ps_ab[:, 0:512])
                        S.inc("vec", v.tensor_copy(abx[:, 512:1024], ps_ab[:, 512:1024]), 1)
                        v.wait_ge(S.h["ld0"], ld_done(i))
                        for c in range(NCH):
                            if i == 0:
                                v.memset(xnb[:, c * 513:c * 513 + 1], 0.0)
                            else:
                                v.tensor_copy(xnb[:, c * 513:c * 513 + 1],
                                              xnb[:, c * 513 + 512:c * 513 + 513])
                        for c in range(NCH):
                            v.tensor_tensor(tmpf[:, :], xsv(i, c), abx[:, 0:512], ALU.mult)
                            v.tensor_tensor(xnb[:, c * 513 + 1:c * 513 + 513], tmpf[:, :],
                                            abx[:, 512:1024], ALU.add)
                            xn_new = xnb[:, c * 513 + 1:c * 513 + 513]
                            xn_sh = xnb[:, c * 513:c * 513 + 512]
                            v.tensor_tensor(dbuf[:, :], xn_new, xn_sh, ALU.subtract)
                            v.scalar_tensor_tensor(xr[:, c * 512:(c + 1) * 512], dbuf[:, :],
                                                   muT[:, c:c + 1], xn_sh, ALU.mult, ALU.add)
                            inst = v.scalar_tensor_tensor(xk[:, c * 512:(c + 1) * 512], dbuf[:, :],
                                                          muT[:, 8 + c:8 + c + 1], xn_sh,
                                                          ALU.mult, ALU.add)
                        S.inc("vec", inst, 1)
                        for o in range(NCH):
                            v.wait_ge(S.h["scl"], relu(i, o))
                            S.inc("vec", v.tensor_tensor(kact[:, o * 512:(o + 1) * 512],
                                                         psA[o % 2][:, :],
                                                         krelu[:, (o % 2) * 512:(o % 2 + 1) * 512],
                                                         ALU.mult), 1)
                        for o in range(NCH):
                            v.wait_ge(S.h["tns"], v_mm(i, o))
                            v.tensor_tensor(tmpf[:, :], psA[o % 2][:, :],
                                            rsb[:, o * 512:(o + 1) * 512], ALU.mult)
                            S.inc("vec", v.scalar_tensor_tensor(xsv(i, o), tmpf[:, :], 0.5,
                                                                xsv(i, o), ALU.mult, ALU.add), 1)

            assert S.v["tns"] == B["tns"] + 25 * NT, (S.v["tns"], B["tns"])
            assert S.v["vec"] == B["vec"] + 18 * NT
            assert S.v["scl"] == B["scl"] + 16 * NT
            assert S.v["ld0"] == B["ld0"] + 128 * NT
            assert S.v["sto"] == B["sto"] + 128 * NT



ATTN_W = (("WQ", "wq_bf"), ("WK", "wk_bf2"), ("WV", "wv_bf2"), ("WO", "wo_bf"))


def emit_prep_l0attn(nc, D, S):
    with nc.Block() as block:
        @block.gpsimd
        def _(g):
            for src_name, dst in ATTN_W:
                S.marks[dst] = S.inc("cast", g.dma_start(out=D[dst], in_=D[src_name]), 16)


def emit_l0attn(nc, D, S):
    scr_in, scr_out = D["scr_in"], D["scr_out"]
    ident, onescol = D["cm_ident"], D["cm_onescol"]
    ident_bf = D["cm_ident_bf"]
    q_scr, k_scr, v_scr, a_scr = D["q_scr"], D["k_scr"], D["v_scr"], D["a_scr"]

    with ExitStack() as ctx:
        a_sb = ctx.enter_context(nc.sbuf_tensor("at_a", [1, T], F32))
        b_sb = ctx.enter_context(nc.sbuf_tensor("at_b", [1, T], F32))
        _emit_ln_stats(nc, D, S, ctx, scr_in, a_sb, b_sb, "at")

        wq_s = ctx.enter_context(nc.sbuf_tensor("at_wq", [128, NCH * C], BF16))
        wk_s = ctx.enter_context(nc.sbuf_tensor("at_wk", [128, NCH * C], BF16))
        wv_s = ctx.enter_context(nc.sbuf_tensor("at_wv", [128, NCH * C], BF16))
        wo_s = ctx.enter_context(nc.sbuf_tensor("at_wo", [128, NCH * C], BF16))
        bos = ctx.enter_context(nc.sbuf_tensor("at_bos", [1, C], F32))
        boT = ctx.enter_context(nc.sbuf_tensor("at_boT", [128, 8], F32))
        with ExitStack() as c2:
            ps_bo = c2.enter_context(nc.psum_tensor("at_psbo", [128, 8], F32))
            with nc.Block() as block:
                @block.sync
                def _(sync):
                    sync.wait_ge(S.h["cast"], S.marks["wo_bf"])
                    for (_, wname), buf in zip(ATTN_W, (wq_s, wk_s, wv_s, wo_s)):
                        for c in range(NCH):
                            S.inc("ld1", sync.dma_start(
                                out=buf[:, c * C:(c + 1) * C],
                                in_=D[wname][c * 128:(c + 1) * 128, :]), 16)
                    S.inc("ld1", sync.dma_start(out=bos[:, :], in_=D["bo"]), 16)

                @block.tensor
                def _(t):
                    t.wait_ge(S.h["ld1"], S.v["ld1"])
                    for j in range(8):
                        t.transpose(ps_bo[:, j:j + 1], bos[0:1, j * 128:(j + 1) * 128],
                                    ident[0:1, 0:1])
            with nc.Block() as block:
                @block.vector
                def _(v):
                    v.tensor_copy(boT[:, :], ps_bo[:, :])

        with ExitStack() as c2:
            xs = c2.enter_context(nc.sbuf_tensor("at_xs", [128, NCH * 512], F32))
            xn = c2.enter_context(nc.sbuf_tensor("at_xn", [128, NCH * 512], BF16))
            abx = c2.enter_context(nc.sbuf_tensor("at_abx", [128, 1024], F32))
            qt = c2.enter_context(nc.sbuf_tensor("at_qt", [128, NCH * 512], BF16))
            kt = c2.enter_context(nc.sbuf_tensor("at_kt", [128, NCH * 512], BF16))
            vt = c2.enter_context(nc.sbuf_tensor("at_vt", [128, 2 * 1024], BF16))
            tmpf = c2.enter_context(nc.sbuf_tensor("at_tmpf", [128, 512], F32))
            ps_ab = c2.enter_context(nc.psum_tensor("at_psab", [128, 1024], F32))
            psA = [c2.enter_context(nc.psum_tensor(f"at_psA{i}", [128, 512], F32)) for i in range(2)]
            psV = [c2.enter_context(nc.psum_tensor(f"at_psV{i}", [128, 1024], F32)) for i in range(2)]

            B = S.base("ld0", "tns", "vec", "scl", "sto")
            ld_done = lambda i: B["ld0"] + 128 * (i + 1)
            ab_mm = lambda i: B["tns"] + 21 * i + 1
            q_mm = lambda i, o: B["tns"] + 21 * i + 1 + (o + 1)
            k_mm = lambda i, o: B["tns"] + 21 * i + 9 + (o + 1)
            v_mm = lambda i, u: B["tns"] + 21 * i + 17 + (u + 1)
            qe = lambda i, o: B["scl"] + 16 * i + (o + 1)
            ke = lambda i, o: B["scl"] + 16 * i + 8 + (o + 1)
            ab_evac = lambda i: B["vec"] + 6 * i + 1
            apl = lambda i: B["vec"] + 6 * i + 2
            ve = lambda i, u: B["vec"] + 6 * i + 2 + (u + 1)
            sto_done = lambda i: B["sto"] + 320 * (i + 1)

            def xsv(c):
                return xs[:, c * 512:(c + 1) * 512]

            with nc.Block() as block:
                @block.sync
                def _(sync):
                    for c in range(NCH):
                        S.inc("ld0", sync.dma_start(
                            out=xsv(c),
                            in_=scr_in[c * 128:(c + 1) * 128, 0:512]), 16)
                    for i in range(NT):
                        if i + 1 < NT:
                            sync.wait_ge(S.h["vec"], apl(i))
                            for c in range(NCH):
                                S.inc("ld0", sync.dma_start(
                                    out=xsv(c),
                                    in_=scr_in[c * 128:(c + 1) * 128,
                                               (i + 1) * 512:(i + 2) * 512]), 16)
                        for o in range(NCH):
                            sync.wait_ge(S.h["scl"], qe(i, o))
                            S.inc("sto", sync.dma_start(
                                out=q_scr[o * 128:(o + 1) * 128, i * 512:(i + 1) * 512],
                                in_=qt[:, o * 512:(o + 1) * 512]), 16)
                        for o in range(NCH):
                            sync.wait_ge(S.h["scl"], ke(i, o))
                            S.inc("sto", sync.dma_start(
                                out=k_scr[o * 128:(o + 1) * 128, i * 512:(i + 1) * 512],
                                in_=kt[:, o * 512:(o + 1) * 512]), 16)
                        for u in range(4):
                            sync.wait_ge(S.h["vec"], ve(i, u))
                            S.inc("sto", sync.dma_start(
                                out=v_scr[i * 512 + u * 128: i * 512 + (u + 1) * 128, :],
                                in_=vt[:, (u % 2) * 1024:(u % 2 + 1) * 1024]), 16)
                    sync.wait_ge(S.h["sto"], sto_done(NT - 1))

                @block.tensor
                def _(t):
                    for i in range(NT):
                        if i >= 1:
                            t.wait_ge(S.h["vec"], ab_evac(i - 1))
                        t.matmul(ps_ab[:, 0:512], onescol[0:1, :], a_sb[:, i * 512:(i + 1) * 512],
                                 start=True, stop=True)
                        inst = t.matmul(ps_ab[:, 512:1024], onescol[0:1, :],
                                        b_sb[:, i * 512:(i + 1) * 512], start=True, stop=True)
                        S.inc("tns", inst, 1)
                        t.wait_ge(S.h["vec"], apl(i))
                        for o in range(NCH):
                            if o >= 2:
                                t.wait_ge(S.h["scl"], qe(i, o - 2))
                            elif i > 0:
                                t.wait_ge(S.h["scl"], ke(i - 1, o + 6))
                            for c in range(NCH):
                                inst = t.matmul(psA[o % 2][:, :],
                                                wq_s[:, c * C + o * 128: c * C + (o + 1) * 128],
                                                xn[:, c * 512:(c + 1) * 512],
                                                start=(c == 0), stop=(c == NCH - 1))
                            S.inc("tns", inst, 1)
                        for o in range(NCH):
                            t.wait_ge(S.h["scl"], qe(i, o + 6) if o < 2 else ke(i, o - 2))
                            for c in range(NCH):
                                inst = t.matmul(psA[o % 2][:, :],
                                                wk_s[:, c * C + o * 128: c * C + (o + 1) * 128],
                                                xn[:, c * 512:(c + 1) * 512],
                                                start=(c == 0), stop=(c == NCH - 1))
                            S.inc("tns", inst, 1)
                        for u in range(4):
                            if u >= 2:
                                t.wait_ge(S.h["vec"], ve(i, u - 2))
                            elif i > 0:
                                t.wait_ge(S.h["vec"], ve(i - 1, u + 2))
                            for c in range(NCH):
                                for nn in range(2):
                                    inst = t.matmul(
                                        psV[u % 2][:, nn * 512:(nn + 1) * 512],
                                        xn[:, c * 512 + u * 128: c * 512 + (u + 1) * 128],
                                        wv_s[:, c * C + nn * 512: c * C + (nn + 1) * 512],
                                        start=(c == 0), stop=(c == NCH - 1))
                            S.inc("tns", inst, 1)

                @block.scalar
                def _(s):
                    for i in range(NT):
                        for o in range(NCH):
                            s.wait_ge(S.h["tns"], q_mm(i, o))
                            if i > 0:
                                s.wait_ge(S.h["sto"], B["sto"] + 320 * i - 320 + (o + 1) * 16)
                            S.inc("scl", s.copy(qt[:, o * 512:(o + 1) * 512], psA[o % 2][:, :]), 1)
                        for o in range(NCH):
                            s.wait_ge(S.h["tns"], k_mm(i, o))
                            if i > 0:
                                s.wait_ge(S.h["sto"], B["sto"] + 320 * i - 192 + (o + 1) * 16)
                            S.inc("scl", s.copy(kt[:, o * 512:(o + 1) * 512], psA[o % 2][:, :]), 1)

                @block.vector
                def _(v):
                    for i in range(NT):
                        v.wait_ge(S.h["tns"], ab_mm(i))
                        v.tensor_copy(abx[:, 0:512], ps_ab[:, 0:512])
                        S.inc("vec", v.tensor_copy(abx[:, 512:1024], ps_ab[:, 512:1024]), 1)
                        v.wait_ge(S.h["ld0"], ld_done(i))
                        for c in range(NCH):
                            v.tensor_tensor(tmpf[:, :], xsv(c), abx[:, 0:512], ALU.mult)
                            inst = v.tensor_tensor(xn[:, c * 512:(c + 1) * 512], tmpf[:, :],
                                                   abx[:, 512:1024], ALU.add)
                        S.inc("vec", inst, 1)
                        for u in range(4):
                            v.wait_ge(S.h["tns"], v_mm(i, u))
                            if i > 0 or u >= 2:
                                pi, pu = (i, u - 2) if u >= 2 else (i - 1, u + 2)
                                v.wait_ge(S.h["sto"], B["sto"] + 320 * pi + 256 + (pu + 1) * 16)
                            S.inc("vec", v.tensor_copy(vt[:, (u % 2) * 1024:(u % 2 + 1) * 1024],
                                                       psV[u % 2][:, :]), 1)

            assert S.v["tns"] == B["tns"] + 21 * NT
            assert S.v["vec"] == B["vec"] + 6 * NT
            assert S.v["scl"] == B["scl"] + 16 * NT
            assert S.v["sto"] == B["sto"] + 320 * NT
        A_store_done = S.v["sto"]

        with ExitStack() as c2:
            qh = [c2.enter_context(nc.sbuf_tensor(f"at_qh{i}", [64, T], BF16)) for i in range(2)]
            kh = [c2.enter_context(nc.sbuf_tensor(f"at_kh{i}", [64, T], BF16)) for i in range(2)]
            vh = [c2.enter_context(nc.sbuf_tensor(f"at_vh{i}", [128, 32 * 65], BF16)) for i in range(2)]
            pb = [c2.enter_context(nc.sbuf_tensor(f"at_p{i}", [128, 1024], BF16)) for i in range(4)]
            attn = c2.enter_context(nc.sbuf_tensor("at_attn", [64, 1024], BF16))
            recs = c2.enter_context(nc.sbuf_tensor("at_recs", [1, 512], F32))
            recx = c2.enter_context(nc.sbuf_tensor("at_recx", [64, 512], F32))
            atok = c2.enter_context(nc.sbuf_tensor("at_atok", [128, 512], BF16))
            mask = c2.enter_context(nc.sbuf_tensor("at_mask", [128, 1920], F32))
            ps_sc = [c2.enter_context(nc.psum_tensor(f"at_sc{i}", [128, 1024], F32)) for i in range(2)]
            ps_at = [c2.enter_context(nc.psum_tensor(f"at_at{i}", [65, 512], F32)) for i in range(2)]
            ps_rec = c2.enter_context(nc.psum_tensor("at_rec", [64, 512], F32))
            ps_trb = c2.enter_context(nc.psum_tensor("at_tr", [128, 128], BF16))
            ps_tr = [ps_trb[:, 0:64], ps_trb[:, 64:128]]

            with nc.Block() as block:
                @block.gpsimd
                def _(g):
                    g.memset(mask[:, :], 1.0)
                    g.affine_select(out=mask[:, :], in_=mask[:, :], compare_op=ALU.is_ge,
                                    fill=0.0, base=-896, pattern=[[1, 1920]], channel_multiplier=-1)
                    g.memset(vh[0][:, :], 1.0)
                    g.memset(vh[1][:, :], 1.0)

            B = S.base("ld0", "tns", "vec", "scl", "sto")
            KV = [8 * (qb + 1) for qb in range(4)]

            sc_t = {}; pv_t = {}; exp_t = {}; msk_t = {}; rec_t = {}; rex_t = {}
            rcx_t = {}; attn_t = {}; trm_t = {}; tre_t = {}
            tv = B["tns"]; sv = B["scl"]; vv = B["vec"]
            for h in range(H):
                for qb in range(4):
                    nk = KV[qb]
                    for j in range(nk // 2):
                        tv += 1; sc_t[(h, qb, 2 * j)] = tv
                        tv += 1; sc_t[(h, qb, 2 * j + 1)] = tv
                        if j >= 1:
                            tv += 2; pv_t[(h, qb, 2 * j - 2)] = tv
                            tv += 2; pv_t[(h, qb, 2 * j - 1)] = tv
                    tv += 2; pv_t[(h, qb, nk - 2)] = tv
                    tv += 2; pv_t[(h, qb, nk - 1)] = tv
                    for nn in range(2):
                        tv += 1; rex_t[(h, qb, nn)] = tv
                    for j in range(8):
                        tv += 1; trm_t[(h, qb, j)] = tv
                    for kc in range(nk):
                        sv += 1; exp_t[(h, qb, kc)] = sv
                    for kc in range(qb * 8, nk):
                        vv += 1; msk_t[(h, qb, kc)] = vv
                    for nn in range(2):
                        vv += 1; rec_t[(h, qb, nn)] = vv
                        vv += 1; rcx_t[(h, qb, nn)] = vv
                        vv += 1; attn_t[(h, qb, nn)] = vv
                    for j in range(8):
                        vv += 1; tre_t[(h, qb, j)] = vv
            T_END, S_END, V_END = tv, sv, vv

            def prev_exp(h, qb, kc):
                idx = kc - 2
                while idx < 0:
                    if qb > 0:
                        qb -= 1; idx += KV[qb]
                    elif h > 0:
                        h -= 1; qb = 3; idx += KV[3]
                    else:
                        return None
                return exp_t[(h, qb, idx)]

            def prev_pv(h, qb, kc):
                idx = kc - 4
                while idx < 0:
                    if qb > 0:
                        qb -= 1; idx += KV[qb]
                    elif h > 0:
                        h -= 1; qb = 3; idx += KV[3]
                    else:
                        return None
                return pv_t[(h, qb, idx)]

            with nc.Block() as block:
                @block.sync
                def _(sync):
                    sync.wait_ge(S.h["sto"], A_store_done)

                    def head_loads(h):
                        S.inc("ld0", sync.dma_start(out=qh[h % 2][:, :],
                                                    in_=q_scr[h * 64:(h + 1) * 64, :]), 16)
                        S.inc("ld0", sync.dma_start(out=kh[h % 2][:, :],
                                                    in_=k_scr[h * 64:(h + 1) * 64, :]), 16)
                        S.inc("ld0", sync.dma_start(
                            out=vh[h % 2].rearrange("p (k w) -> p k w", w=65)[:, :, 0:64],
                            in_=v_scr[:, h * 64:(h + 1) * 64].rearrange("(k p) w -> p k w", p=128)), 16)

                    head_loads(0)
                    head_loads(1)
                    for h in range(H):
                        for qb in range(4):
                            sync.wait_ge(S.h["vec"], tre_t[(h, qb, 7)])
                            S.inc("sto", sync.dma_start(
                                out=a_scr[(h * T + qb * 1024):(h * T + (qb + 1) * 1024), :]
                                    .rearrange("(j p) w -> p j w", p=128),
                                in_=atok.rearrange("p (j w) -> p j w", w=64)), 16)
                        if h + 2 < H:
                            sync.wait_ge(S.h["tns"], pv_t[(h, 3, KV[3] - 1)])
                            head_loads(h + 2)
                    sync.wait_ge(S.h["sto"], S.v["sto"])

                @block.tensor
                def _(t):
                    for h in range(H):
                        t.wait_ge(S.h["ld0"], B["ld0"] + 48 * (h + 1))
                        for qb in range(4):
                            nk = KV[qb]

                            def emit_sc(kc):
                                pe = prev_exp(h, qb, kc)
                                if pe is not None:
                                    t.wait_ge(S.h["scl"], pe)
                                for nn in range(2):
                                    inst = t.matmul(
                                        ps_sc[kc % 2][:, nn * 512:(nn + 1) * 512],
                                        kh[h % 2][:, kc * 128:(kc + 1) * 128],
                                        qh[h % 2][:, qb * 1024 + nn * 512: qb * 1024 + (nn + 1) * 512],
                                        start=True, stop=True)
                                S.inc("tns", inst, 1)

                            def emit_pv(pkc):
                                t.wait_ge(S.h["scl"], exp_t[(h, qb, pkc)])
                                if pkc >= qb * 8:
                                    t.wait_ge(S.h["vec"], msk_t[(h, qb, pkc)])
                                for nn in range(2):
                                    inst = t.matmul(ps_at[nn][:, :],
                                                    vh[h % 2][:, pkc * 65:(pkc + 1) * 65],
                                                    pb[pkc % 4][:, nn * 512:(nn + 1) * 512],
                                                    start=(pkc == 0), stop=(pkc == nk - 1))
                                    S.inc("tns", inst, 1)

                            for j in range(nk // 2):
                                emit_sc(2 * j)
                                emit_sc(2 * j + 1)
                                if j >= 1:
                                    emit_pv(2 * j - 2)
                                    emit_pv(2 * j - 1)
                            emit_pv(nk - 2)
                            emit_pv(nk - 1)
                            for nn in range(2):
                                t.wait_ge(S.h["vec"], rec_t[(h, qb, nn)])
                                if nn == 1:
                                    t.wait_ge(S.h["vec"], rcx_t[(h, qb, 0)])
                                S.inc("tns", t.matmul(ps_rec[:, :], onescol[0:1, 0:64],
                                                      recs[:, :], start=True, stop=True), 1)
                            t.wait_ge(S.h["vec"], attn_t[(h, qb, 1)])
                            for j in range(8):
                                if j >= 2:
                                    t.wait_ge(S.h["vec"], tre_t[(h, qb, j - 2)])
                                elif (h, qb) != (0, 0):
                                    ph, pqb = (h, qb - 1) if qb > 0 else (h - 1, 3)
                                    t.wait_ge(S.h["vec"], tre_t[(ph, pqb, j + 6)])
                                S.inc("tns", t.transpose(ps_tr[j % 2],
                                                         attn[:, j * 128:(j + 1) * 128],
                                                         ident_bf[0:64, 0:64]), 1)

                @block.scalar
                def _(s):
                    for h in range(H):
                        for qb in range(4):
                            nk = KV[qb]
                            for kc in range(nk):
                                s.wait_ge(S.h["tns"], sc_t[(h, qb, kc)])
                                ppv = prev_pv(h, qb, kc)
                                if ppv is not None:
                                    s.wait_ge(S.h["tns"], ppv)
                                S.inc("scl", s.activation(pb[kc % 4][:, :], ps_sc[kc % 2][:, :],
                                                          AF.Exp, scale=0.125), 1)

                @block.vector
                def _(v):
                    for h in range(H):
                        for qb in range(4):
                            nk = KV[qb]
                            for kc in range(qb * 8, nk):
                                v.wait_ge(S.h["scl"], exp_t[(h, qb, kc)])
                                dd = kc * 128 - qb * 1024
                                S.inc("vec", v.tensor_tensor(
                                    pb[kc % 4][:, :], pb[kc % 4][:, :],
                                    mask[:, 896 - dd:896 - dd + 1024], ALU.mult), 1)
                            for nn in range(2):
                                v.wait_ge(S.h["tns"], pv_t[(h, qb, nk - 1)])
                                S.inc("vec", v.reciprocal(recs[:, :], ps_at[nn][64:65, :]), 1)
                                v.wait_ge(S.h["tns"], rex_t[(h, qb, nn)])
                                S.inc("vec", v.tensor_copy(recx[:, :], ps_rec[:, :]), 1)
                                S.inc("vec", v.tensor_tensor(attn[:, nn * 512:(nn + 1) * 512],
                                                             ps_at[nn][0:64, :], recx[:, :],
                                                             ALU.mult), 1)
                            for j in range(8):
                                v.wait_ge(S.h["tns"], trm_t[(h, qb, j)])
                                if (h, qb) != (0, 0) and j == 0:
                                    n_prev = h * 4 + qb
                                    v.wait_ge(S.h["sto"], B["sto"] + n_prev * 16)
                                S.inc("vec", v.tensor_copy(atok[:, j * 64:(j + 1) * 64],
                                                           ps_tr[j % 2]), 1)

            assert S.v["tns"] == T_END, (S.v["tns"], T_END)
            assert S.v["scl"] == S_END
            assert S.v["vec"] == V_END
        B_store_done = S.v["sto"]

        with ExitStack() as c2:
            xs2 = [c2.enter_context(nc.sbuf_tensor(f"atc_xs{j}", [128, NCH * 512], F32))
                   for j in range(2)]
            at2 = [c2.enter_context(nc.sbuf_tensor(f"atc_at{j}", [128, NCH * 512], BF16))
                   for j in range(2)]
            psA = [c2.enter_context(nc.psum_tensor(f"atc_ps{i}", [128, 512], F32)) for i in range(2)]
            a_view = a_scr.rearrange("(s g) w -> s (g w)", g=16)

            B = S.base("ld0", "tns", "vec", "sto")
            ld_done = lambda i: B["ld0"] + 256 * (i + 1)
            wo_mm = lambda i, o: B["tns"] + 8 * i + (o + 1)
            upd = lambda i, o: B["vec"] + 8 * i + (o + 1)
            sto_done = lambda i: B["sto"] + 128 * (i + 1)

            with nc.Block() as block:
                @block.sync
                def _(sync):
                    sync.wait_ge(S.h["sto"], B_store_done)
                    for i in range(NT):
                        if i == 0:
                            todo = [0, 1]
                        elif i + 1 < NT:
                            todo = [i + 1]
                        else:
                            todo = []
                        for j in todo:
                            if j >= 2:
                                sync.wait_ge(S.h["sto"], sto_done(j - 2))
                                sync.wait_ge(S.h["tns"], wo_mm(j - 2, NCH - 1))
                            for c in range(NCH):
                                S.inc("ld0", sync.dma_start(
                                    out=at2[j % 2][:, c * 512:(c + 1) * 512],
                                    in_=a_view[j * 512:(j + 1) * 512, c * 128:(c + 1) * 128],
                                    transpose=True), 16)
                            for c in range(NCH):
                                S.inc("ld0", sync.dma_start(
                                    out=xs2[j % 2][:, c * 512:(c + 1) * 512],
                                    in_=scr_in[c * 128:(c + 1) * 128, j * 512:(j + 1) * 512]), 16)
                        for o in range(NCH):
                            sync.wait_ge(S.h["vec"], upd(i, o))
                            S.inc("sto", sync.dma_start(
                                out=scr_out[o * 128:(o + 1) * 128, i * 512:(i + 1) * 512],
                                in_=xs2[i % 2][:, o * 512:(o + 1) * 512]), 16)
                    sync.wait_ge(S.h["sto"], sto_done(NT - 1))

                @block.tensor
                def _(t):
                    for i in range(NT):
                        t.wait_ge(S.h["ld0"], B["ld0"] + 256 * (i + 1))
                        for o in range(NCH):
                            if o >= 2:
                                t.wait_ge(S.h["vec"], upd(i, o - 2))
                            elif i > 0:
                                t.wait_ge(S.h["vec"], upd(i - 1, o + 6))
                            for c in range(NCH):
                                inst = t.matmul(psA[o % 2][:, :],
                                                wo_s[:, c * C + o * 128: c * C + (o + 1) * 128],
                                                at2[i % 2][:, c * 512:(c + 1) * 512],
                                                start=(c == 0), stop=(c == NCH - 1))
                            S.inc("tns", inst, 1)

                @block.vector
                def _(v):
                    for i in range(NT):
                        for o in range(NCH):
                            v.wait_ge(S.h["tns"], wo_mm(i, o))
                            S.inc("vec", v.scalar_tensor_tensor(
                                xs2[i % 2][:, o * 512:(o + 1) * 512], psA[o % 2][:, :],
                                boT[:, o:o + 1], xs2[i % 2][:, o * 512:(o + 1) * 512],
                                ALU.add, ALU.add), 1)

            assert S.v["tns"] == B["tns"] + 8 * NT
            assert S.v["vec"] == B["vec"] + 8 * NT




def emit_prep_l1attn(nc, D, S):
    with nc.Block() as block:
        @block.gpsimd
        def _(g):
            S.marks["wqkv_bf"] = S.inc("cast", g.dma_start(out=D["wqkv_bf"], in_=D["WQKV"]), 16)
            S.marks["wproj_bf"] = S.inc("cast", g.dma_start(out=D["wproj_bf"], in_=D["WPROJ"]), 16)


def emit_l1attn(nc, D, S):
    scr_in, scr_out = D["scr_in"], D["scr_out"]
    ident, onescol, ones_bf = D["cm_ident"], D["cm_onescol"], D["cm_ones_bf"]

    with ExitStack() as ctx:
        a_sb = ctx.enter_context(nc.sbuf_tensor("la_a", [1, T], F32))
        b_sb = ctx.enter_context(nc.sbuf_tensor("la_b", [1, T], F32))
        _emit_ln_stats(nc, D, S, ctx, scr_in, a_sb, b_sb, "la")

        wqkvT = ctx.enter_context(nc.sbuf_tensor("la_wqkv", [128, NCH * 3072], BF16))
        wprojT = ctx.enter_context(nc.sbuf_tensor("la_wproj", [128, NCH * C], BF16))
        bps = ctx.enter_context(nc.sbuf_tensor("la_bps", [1, C], F32))
        bpT = ctx.enter_context(nc.sbuf_tensor("la_bpT", [128, 8], F32))
        ones2 = ctx.enter_context(nc.sbuf_tensor("la_ones2", [128, 2], BF16))
        ind2 = ctx.enter_context(nc.sbuf_tensor("la_ind2", [2, 128], BF16))
        ctx_s = ctx.enter_context(nc.sbuf_tensor("la_ctx", [128, 512], BF16))
        S_sb = ctx.enter_context(nc.sbuf_tensor("la_S", [1, 1024], F32))
        srecT = ctx.enter_context(nc.sbuf_tensor("la_srT", [128, 8], F32))

        with ExitStack() as c2:
            ps_bp = c2.enter_context(nc.psum_tensor("la_psbp", [128, 8], F32))
            with nc.Block() as block:
                @block.sync
                def _(sync):
                    sync.wait_ge(S.h["cast"], S.marks["wqkv_bf"])
                    for c in range(NCH):
                        S.inc("ld1", sync.dma_start(out=wqkvT[:, c * 3072:(c + 1) * 3072],
                                                    in_=D["wqkv_bf"][:, c * 128:(c + 1) * 128],
                                                    transpose=True), 16)
                    sync.wait_ge(S.h["cast"], S.marks["wproj_bf"])
                    for c in range(NCH):
                        S.inc("ld1", sync.dma_start(out=wprojT[:, c * C:(c + 1) * C],
                                                    in_=D["wproj_bf"][:, c * 128:(c + 1) * 128],
                                                    transpose=True), 16)
                    S.inc("ld1", sync.dma_start(out=bps[:, :], in_=D["b_proj"]), 16)

                @block.gpsimd
                def _(g):
                    g.memset(ones2[:, :], 0.0)
                    g.memset(ones2[0:64, 0:1], 1.0)
                    g.memset(ones2[64:128, 1:2], 1.0)
                    g.memset(ind2[:, :], 1.0)
                    g.affine_select(out=ind2[:, :], in_=ind2[:, :], compare_op=ALU.is_ge,
                                    fill=0.0, base=0, pattern=[[1, 128]], channel_multiplier=-64)
                    g.affine_select(out=ind2[:, :], in_=ind2[:, :], compare_op=ALU.is_ge,
                                    fill=0.0, base=63, pattern=[[-1, 128]], channel_multiplier=64)
                    g.memset(S_sb[:, :], 0.0)

                @block.tensor
                def _(t):
                    t.wait_ge(S.h["ld1"], S.v["ld1"])
                    for j in range(8):
                        t.transpose(ps_bp[:, j:j + 1], bps[0:1, j * 128:(j + 1) * 128],
                                    ident[0:1, 0:1])
            with nc.Block() as block:
                @block.vector
                def _(v):
                    v.tensor_copy(bpT[:, :], ps_bp[:, :])

        with ExitStack() as c2:
            xs = c2.enter_context(nc.sbuf_tensor("la_xs", [128, NCH * 512], F32))
            xn = c2.enter_context(nc.sbuf_tensor("la_xn", [128, NCH * 512], BF16))
            abx = c2.enter_context(nc.sbuf_tensor("la_abx", [128, 1024], F32))
            tmpf = c2.enter_context(nc.sbuf_tensor("la_tmpf", [128, 512], F32))
            ke_s = [c2.enter_context(nc.sbuf_tensor(f"la_ke{i}", [128, 1024], BF16)) for i in range(2)]
            ve_s = [c2.enter_context(nc.sbuf_tensor(f"la_ve{i}", [128, 1024], BF16)) for i in range(2)]
            ps_ab = c2.enter_context(nc.psum_tensor("la_psab", [128, 1024], F32))
            psK = c2.enter_context(nc.psum_tensor("la_psK", [128, 1024], F32))
            psV = c2.enter_context(nc.psum_tensor("la_psV", [128, 1024], F32))
            ps_one = c2.enter_context(nc.psum_tensor("la_pso", [1, 512], F32))
            ps_ctx = c2.enter_context(nc.psum_tensor("la_psctx", [128, 512], F32))

            B = S.base("ld0", "tns", "vec", "scl", "sto")
            ld_done = lambda i: B["ld0"] + 128 * (i + 1)
            ab_mm = lambda i: B["tns"] + 21 * i + 1
            k_mm = lambda i, u: B["tns"] + 21 * i + 1 + 5 * u + 1
            v_mm = lambda i, u: B["tns"] + 21 * i + 1 + 5 * u + 2
            one_mm = lambda i, u, nh: B["tns"] + 21 * i + 1 + 5 * u + 3 + nh
            ctx_mm = lambda i, u: B["tns"] + 21 * i + 1 + 5 * u + 5
            kexp = lambda i, u: B["scl"] + 4 * i + (u + 1)
            ab_evac = lambda i: B["vec"] + 14 * i + 1
            apl = lambda i: B["vec"] + 14 * i + 2
            vevac = lambda i, u: B["vec"] + 14 * i + 2 + 3 * u + 1
            sadd = lambda i, u, nh: B["vec"] + 14 * i + 2 + 3 * u + 2 + nh

            with nc.Block() as block:
                @block.sync
                def _(sync):
                    for c in range(NCH):
                        S.inc("ld0", sync.dma_start(
                            out=xs[:, c * 512:(c + 1) * 512],
                            in_=scr_in[c * 128:(c + 1) * 128, 0:512]), 16)
                    for i in range(NT - 1):
                        sync.wait_ge(S.h["vec"], apl(i))
                        for c in range(NCH):
                            S.inc("ld0", sync.dma_start(
                                out=xs[:, c * 512:(c + 1) * 512],
                                in_=scr_in[c * 128:(c + 1) * 128, (i + 1) * 512:(i + 2) * 512]), 16)

                @block.tensor
                def _(t):
                    for i in range(NT):
                        if i >= 1:
                            t.wait_ge(S.h["vec"], ab_evac(i - 1))
                        t.matmul(ps_ab[:, 0:512], onescol[0:1, :], a_sb[:, i * 512:(i + 1) * 512],
                                 start=True, stop=True)
                        inst = t.matmul(ps_ab[:, 512:1024], onescol[0:1, :],
                                        b_sb[:, i * 512:(i + 1) * 512], start=True, stop=True)
                        S.inc("tns", inst, 1)
                        t.wait_ge(S.h["vec"], apl(i))
                        for u in range(4):
                            if i > 0 or u > 0:
                                pi, pu = (i, u - 1) if u > 0 else (i - 1, 3)
                                t.wait_ge(S.h["scl"], kexp(pi, pu))
                            for c in range(NCH):
                                for nn in range(2):
                                    inst = t.matmul(
                                        psK[:, nn * 512:(nn + 1) * 512],
                                        xn[:, c * 512 + u * 128: c * 512 + (u + 1) * 128],
                                        wqkvT[:, c * 3072 + 1024 + nn * 512: c * 3072 + 1024 + (nn + 1) * 512],
                                        start=(c == 0), stop=(c == NCH - 1))
                            S.inc("tns", inst, 1)
                            if i > 0 or u > 0:
                                pi, pu = (i, u - 1) if u > 0 else (i - 1, 3)
                                t.wait_ge(S.h["vec"], vevac(pi, pu))
                            for c in range(NCH):
                                for nn in range(2):
                                    inst = t.matmul(
                                        psV[:, nn * 512:(nn + 1) * 512],
                                        xn[:, c * 512 + u * 128: c * 512 + (u + 1) * 128],
                                        wqkvT[:, c * 3072 + 2048 + nn * 512: c * 3072 + 2048 + (nn + 1) * 512],
                                        start=(c == 0), stop=(c == NCH - 1))
                            S.inc("tns", inst, 1)
                            t.wait_ge(S.h["scl"], kexp(i, u))
                            for nh in range(2):
                                if i > 0 or u > 0 or nh > 0:
                                    if nh > 0:
                                        t.wait_ge(S.h["vec"], sadd(i, u, 0))
                                    else:
                                        pi, pu = (i, u - 1) if u > 0 else (i - 1, 3)
                                        t.wait_ge(S.h["vec"], sadd(pi, pu, 1))
                                S.inc("tns", t.matmul(ps_one[:, :], ones_bf[:, :],
                                                      ke_s[u % 2][:, nh * 512:(nh + 1) * 512],
                                                      start=True, stop=True), 1)
                            t.wait_ge(S.h["vec"], vevac(i, u))
                            first = (i == 0 and u == 0)
                            last = (i == NT - 1 and u == 3)
                            for hh in range(H):
                                inst = t.matmul(
                                    ps_ctx[(hh % 2) * 64:(hh % 2) * 64 + 64,
                                           (hh // 2) * 64:(hh // 2) * 64 + 64],
                                    ke_s[u % 2][:, hh * 64:(hh + 1) * 64],
                                    ve_s[u % 2][:, hh * 64:(hh + 1) * 64],
                                    start=(first and hh == 0), stop=(last and hh == H - 1),
                                    skip_group_check=True)
                            S.inc("tns", inst, 1)

                @block.vector
                def _(v):
                    for i in range(NT):
                        v.wait_ge(S.h["tns"], ab_mm(i))
                        v.tensor_copy(abx[:, 0:512], ps_ab[:, 0:512])
                        S.inc("vec", v.tensor_copy(abx[:, 512:1024], ps_ab[:, 512:1024]), 1)
                        v.wait_ge(S.h["ld0"], ld_done(i))
                        for c in range(NCH):
                            v.tensor_tensor(tmpf[:, :], xs[:, c * 512:(c + 1) * 512],
                                            abx[:, 0:512], ALU.mult)
                            inst = v.tensor_tensor(xn[:, c * 512:(c + 1) * 512], tmpf[:, :],
                                                   abx[:, 512:1024], ALU.add)
                        S.inc("vec", inst, 1)
                        for u in range(4):
                            v.wait_ge(S.h["tns"], v_mm(i, u))
                            if i > 0 or u >= 2:
                                pi, pu = (i, u - 2) if u >= 2 else (i - 1, u + 2)
                                v.wait_ge(S.h["tns"], ctx_mm(pi, pu))
                            S.inc("vec", v.tensor_copy(ve_s[u % 2][:, :], psV[:, :]), 1)
                            for nh in range(2):
                                v.wait_ge(S.h["tns"], one_mm(i, u, nh))
                                S.inc("vec", v.tensor_tensor(
                                    S_sb[:, nh * 512:(nh + 1) * 512],
                                    S_sb[:, nh * 512:(nh + 1) * 512],
                                    ps_one[:, :], ALU.add), 1)

                @block.scalar
                def _(s):
                    for i in range(NT):
                        for u in range(4):
                            s.wait_ge(S.h["tns"], k_mm(i, u))
                            if i > 0 or u >= 2:
                                pi, pu = (i, u - 2) if u >= 2 else (i - 1, u + 2)
                                s.wait_ge(S.h["tns"], ctx_mm(pi, pu))
                                s.wait_ge(S.h["tns"], one_mm(pi, pu, 1))
                            S.inc("scl", s.activation(ke_s[u % 2][:, :], psK[:, :], AF.Exp), 1)

            assert S.v["tns"] == B["tns"] + 21 * NT
            assert S.v["vec"] == B["vec"] + 14 * NT
            assert S.v["scl"] == B["scl"] + 4 * NT

            with nc.Block() as block:
                @block.tensor
                def _(t):
                    for j in range(8):
                        t.transpose(ps_ab[:, j:j + 1], S_sb[0:1, j * 128:(j + 1) * 128],
                                    ident[0:1, 0:1])
            with nc.Block() as block:
                @block.vector
                def _(v):
                    v.reciprocal(srecT[:, :], ps_ab[:, 0:8])
            with nc.Block() as block:
                @block.vector
                def _(v):
                    for j in range(8):
                        v.tensor_scalar(ctx_s[:, j * 64:(j + 1) * 64],
                                        ps_ctx[:, j * 64:(j + 1) * 64],
                                        srecT[:, j:j + 1], None, ALU.mult)

        with ExitStack() as c2:
            xs2b = [c2.enter_context(nc.sbuf_tensor(f"lb_xs{j}", [128, NCH * 512], F32))
                    for j in range(2)]
            xn = c2.enter_context(nc.sbuf_tensor("lb_xn", [128, NCH * 512], BF16))
            abx = c2.enter_context(nc.sbuf_tensor("lb_abx", [128, 1024], F32))
            tmpf = c2.enter_context(nc.sbuf_tensor("lb_tmpf", [128, 512], F32))
            qe_s = c2.enter_context(nc.sbuf_tensor("lb_qe", [128, NCH * 512], BF16))
            an_s = c2.enter_context(nc.sbuf_tensor("lb_an", [128, NCH * 512], BF16))
            qS_sb = c2.enter_context(nc.sbuf_tensor("lb_qS", [2, T], F32))
            rc_bf = c2.enter_context(nc.sbuf_tensor("lb_rc", [2, T], BF16))
            ps_ab = c2.enter_context(nc.psum_tensor("lb_psab", [128, 1024], F32))
            P0 = c2.enter_context(nc.psum_tensor("lb_P0", [128, 512], F32))
            P1 = c2.enter_context(nc.psum_tensor("lb_P1", [128, 512], F32))
            PP = [P0, P1]
            ps_qS = c2.enter_context(nc.psum_tensor("lb_qSp", [2, 512], F32))
            ps_ex = c2.enter_context(nc.psum_tensor("lb_exp", [128, 512], F32))
            PA = [c2.enter_context(nc.psum_tensor(f"lb_PA{i}", [128, 512], F32)) for i in range(2)]

            B = S.base("ld0", "tns", "vec", "scl", "sto")
            ld_done = lambda i: B["ld0"] + 128 * (i + 1)
            ab_mm = lambda i: B["tns"] + 41 * i + 1
            q_mm = lambda i, o: B["tns"] + 41 * i + 1 + (o + 1)
            qs_mm = lambda i, o: B["tns"] + 41 * i + 9 + (o + 1)
            ex_mm = lambda i, o: B["tns"] + 41 * i + 17 + (o + 1)
            at_mm = lambda i, o: B["tns"] + 41 * i + 25 + (o + 1)
            pj_mm = lambda i, o: B["tns"] + 41 * i + 33 + (o + 1)
            exp2 = lambda i, o: B["scl"] + 8 * i + (o + 1)
            ab_evac = lambda i: B["vec"] + 35 * i + 1
            apl = lambda i: B["vec"] + 35 * i + 2
            qsev = lambda i, o: B["vec"] + 35 * i + 2 + (o + 1)
            rcp = lambda i: B["vec"] + 35 * i + 11
            nrm = lambda i, o: B["vec"] + 35 * i + 11 + (o + 1)
            aev = lambda i, o: B["vec"] + 35 * i + 19 + (o + 1)
            upd = lambda i, o: B["vec"] + 35 * i + 27 + (o + 1)
            sto_done = lambda i: B["sto"] + 128 * (i + 1)

            with nc.Block() as block:
                @block.sync
                def _(sync):
                    for i in range(NT):
                        if i == 0:
                            todo = [0, 1]
                        elif i + 1 < NT:
                            todo = [i + 1]
                        else:
                            todo = []
                        for j in todo:
                            if j >= 2:
                                sync.wait_ge(S.h["sto"], sto_done(j - 2))
                            for c in range(NCH):
                                S.inc("ld0", sync.dma_start(
                                    out=xs2b[j % 2][:, c * 512:(c + 1) * 512],
                                    in_=scr_in[c * 128:(c + 1) * 128, j * 512:(j + 1) * 512]), 16)
                        for o in range(NCH):
                            sync.wait_ge(S.h["vec"], upd(i, o))
                            S.inc("sto", sync.dma_start(
                                out=scr_out[o * 128:(o + 1) * 128, i * 512:(i + 1) * 512],
                                in_=xs2b[i % 2][:, o * 512:(o + 1) * 512]), 16)
                    sync.wait_ge(S.h["sto"], sto_done(NT - 1))

                @block.tensor
                def _(t):
                    for i in range(NT):
                        if i >= 1:
                            t.wait_ge(S.h["vec"], ab_evac(i - 1))
                        t.matmul(ps_ab[:, 0:512], onescol[0:1, :], a_sb[:, i * 512:(i + 1) * 512],
                                 start=True, stop=True)
                        inst = t.matmul(ps_ab[:, 512:1024], onescol[0:1, :],
                                        b_sb[:, i * 512:(i + 1) * 512], start=True, stop=True)
                        S.inc("tns", inst, 1)
                        t.wait_ge(S.h["vec"], apl(i))
                        for o in range(NCH):
                            if o >= 2:
                                t.wait_ge(S.h["scl"], exp2(i, o - 2))
                            elif i > 0:
                                t.wait_ge(S.h["vec"], upd(i - 1, o + 6))
                            for c in range(NCH):
                                inst = t.matmul(PP[o % 2][:, :],
                                                wqkvT[:, c * 3072 + o * 128: c * 3072 + (o + 1) * 128],
                                                xn[:, c * 512:(c + 1) * 512],
                                                start=(c == 0), stop=(c == NCH - 1))
                            S.inc("tns", inst, 1)
                        for o in range(NCH):
                            t.wait_ge(S.h["scl"], exp2(i, o))
                            if o >= 1:
                                t.wait_ge(S.h["vec"], qsev(i, o - 1))
                            elif i > 0:
                                t.wait_ge(S.h["vec"], qsev(i - 1, 7))
                            S.inc("tns", t.matmul(ps_qS[:, :], ones2[:, :],
                                                  qe_s[:, o * 512:(o + 1) * 512],
                                                  start=True, stop=True), 1)
                        for o in range(NCH):
                            t.wait_ge(S.h["vec"], rcp(i))
                            if o >= 1:
                                t.wait_ge(S.h["vec"], nrm(i, o - 1))
                            S.inc("tns", t.matmul(ps_ex[:, :], ind2[:, :],
                                                  rc_bf[:, o * 512:(o + 1) * 512],
                                                  start=True, stop=True), 1)
                        for o in range(NCH):
                            t.wait_ge(S.h["vec"], nrm(i, o))
                            if o >= 2:
                                t.wait_ge(S.h["vec"], aev(i, o - 2))
                            elif i > 0:
                                t.wait_ge(S.h["vec"], aev(i - 1, o + 6))
                            for ss in range(2):
                                inst = t.matmul(
                                    PA[o % 2][ss * 64:(ss + 1) * 64, :],
                                    ctx_s[ss * 64:(ss + 1) * 64, o * 64:(o + 1) * 64],
                                    qe_s[ss * 64:(ss + 1) * 64, o * 512:(o + 1) * 512],
                                    start=True, stop=True, skip_group_check=True)
                            S.inc("tns", inst, 1)
                        for o in range(NCH):
                            t.wait_ge(S.h["vec"], aev(i, 7))
                            if o < 2:
                                t.wait_ge(S.h["scl"], exp2(i, o + 6))
                            else:
                                t.wait_ge(S.h["vec"], upd(i, o - 2))
                            for c in range(NCH):
                                inst = t.matmul(PP[o % 2][:, :],
                                                wprojT[:, c * C + o * 128: c * C + (o + 1) * 128],
                                                an_s[:, c * 512:(c + 1) * 512],
                                                start=(c == 0), stop=(c == NCH - 1))
                            S.inc("tns", inst, 1)

                @block.scalar
                def _(s):
                    for i in range(NT):
                        for o in range(NCH):
                            s.wait_ge(S.h["tns"], q_mm(i, o))
                            if i > 0:
                                s.wait_ge(S.h["tns"], at_mm(i - 1, o))
                            S.inc("scl", s.activation(qe_s[:, o * 512:(o + 1) * 512],
                                                      PP[o % 2][:, :], AF.Exp), 1)

                @block.vector
                def _(v):
                    for i in range(NT):
                        v.wait_ge(S.h["tns"], ab_mm(i))
                        v.tensor_copy(abx[:, 0:512], ps_ab[:, 0:512])
                        S.inc("vec", v.tensor_copy(abx[:, 512:1024], ps_ab[:, 512:1024]), 1)
                        v.wait_ge(S.h["ld0"], ld_done(i))
                        for c in range(NCH):
                            v.tensor_tensor(tmpf[:, :], xs2b[i % 2][:, c * 512:(c + 1) * 512],
                                            abx[:, 0:512], ALU.mult)
                            inst = v.tensor_tensor(xn[:, c * 512:(c + 1) * 512], tmpf[:, :],
                                                   abx[:, 512:1024], ALU.add)
                        S.inc("vec", inst, 1)
                        for o in range(NCH):
                            v.wait_ge(S.h["tns"], qs_mm(i, o))
                            S.inc("vec", v.tensor_copy(qS_sb[:, o * 512:(o + 1) * 512],
                                                       ps_qS[:, :]), 1)
                        if i > 0:
                            v.wait_ge(S.h["tns"], ex_mm(i - 1, 7))
                        v.reciprocal(qS_sb[:, :], qS_sb[:, :])
                        S.inc("vec", v.tensor_copy(rc_bf[:, :], qS_sb[:, :]), 1)
                        for o in range(NCH):
                            v.wait_ge(S.h["tns"], ex_mm(i, o))
                            S.inc("vec", v.tensor_tensor(qe_s[:, o * 512:(o + 1) * 512],
                                                         qe_s[:, o * 512:(o + 1) * 512],
                                                         ps_ex[:, :], ALU.mult), 1)
                        for o in range(NCH):
                            v.wait_ge(S.h["tns"], at_mm(i, o))
                            S.inc("vec", v.tensor_copy(an_s[:, o * 512:(o + 1) * 512],
                                                       PA[o % 2][:, :]), 1)
                        for o in range(NCH):
                            v.wait_ge(S.h["tns"], pj_mm(i, o))
                            S.inc("vec", v.scalar_tensor_tensor(
                                xs2b[i % 2][:, o * 512:(o + 1) * 512], PP[o % 2][:, :],
                                bpT[:, o:o + 1], xs2b[i % 2][:, o * 512:(o + 1) * 512],
                                ALU.add, ALU.add), 1)

            assert S.v["tns"] == B["tns"] + 41 * NT
            assert S.v["vec"] == B["vec"] + 35 * NT
            assert S.v["scl"] == B["scl"] + 8 * NT





def emit_prep_l1ffn(nc, D, S):
    with nc.Block() as block:
        @block.gpsimd
        def _(g):
            S.marks["w1_bf"] = S.inc("cast", g.dma_start(out=D["w1_bf"], in_=D["W1"]), 16)
            S.marks["w2_bf"] = S.inc("cast", g.dma_start(out=D["w2_bf"], in_=D["W2"]), 16)
            S.marks["w3_bf"] = S.inc("cast", g.dma_start(out=D["w3_bf"], in_=D["W3"]), 16)


def emit_moe(nc, D, S, tag):
    scr_in, scr_out = D["scr_in"], D["scr_out"]
    ident, onescol = D["cm_ident"], D["cm_onescol"]
    onescol_bf = D["cm_onescol_bf"]
    NTT = 4
    TT = 1024

    with ExitStack() as ctx:
        ab_bf = ctx.enter_context(nc.sbuf_tensor(f"{tag}_abbf", [128, T], BF16))
        a_bf = ab_bf[0:1, :]
        b_bf = ab_bf[64:65, :]
        abT = ctx.enter_context(nc.sbuf_tensor(f"{tag}_abT", [128, 64], F32))
        WgT_s = ctx.enter_context(nc.sbuf_tensor(f"{tag}_wgT", [128, 64], F32))
        W0 = ctx.enter_context(nc.sbuf_tensor(f"{tag}_W0", [128, 8], F32))
        ind8 = ctx.enter_context(nc.sbuf_tensor(f"{tag}_ind8", [8, 1024], BF16))

        with ExitStack() as c2:
            a_sb = c2.enter_context(nc.sbuf_tensor(f"{tag}_a32", [1, T], F32))
            b_sb = c2.enter_context(nc.sbuf_tensor(f"{tag}_b32", [1, T], F32))
            _emit_ln_stats(nc, D, S, c2, scr_in, a_sb, b_sb, tag)
            ps_abT = c2.enter_context(nc.psum_tensor(f"{tag}_psabT", [128, 64], F32))
            with nc.Block() as block:
                @block.scalar
                def _(s):
                    s.copy(a_bf, a_sb[:, :])
                    s.copy(b_bf, b_sb[:, :])

                @block.tensor
                def _(t):
                    for tc in range(32):
                        t.transpose(ps_abT[:, tc:tc + 1],
                                    a_sb[0:1, tc * 128:(tc + 1) * 128], ident[0:1, 0:1])
                        t.transpose(ps_abT[:, 32 + tc:33 + tc],
                                    b_sb[0:1, tc * 128:(tc + 1) * 128], ident[0:1, 0:1])
            with nc.Block() as block:
                @block.vector
                def _(v):
                    v.tensor_copy(abT[:, :], ps_abT[:, :])

        with ExitStack() as c2:
            wg_sb = c2.enter_context(nc.sbuf_tensor(f"{tag}_wg", [8, 1024], F32))
            wgtmp = c2.enter_context(nc.sbuf_tensor(f"{tag}_wgt", [8, 1024], F32))
            wgs = c2.enter_context(nc.sbuf_tensor(f"{tag}_wgs", [8, 1], F32))
            wgsr = c2.enter_context(nc.sbuf_tensor(f"{tag}_wgsr", [1, 8], F32))
            ps_g = c2.enter_context(nc.psum_tensor(f"{tag}_psg", [128, 80], F32))
            B0 = S.base("ld1")
            with nc.Block() as block:
                @block.sync
                def _(sync):
                    S.inc("ld1", sync.dma_start(out=wg_sb[:, :], in_=D["Wg"]), 16)

                @block.gpsimd
                def _(g):
                    g.memset(ind8[:, :], 1.0)
                    g.affine_select(out=ind8[:, :], in_=ind8[:, :], compare_op=ALU.is_ge,
                                    fill=0.0, base=0, pattern=[[1, 1024]], channel_multiplier=-128)
                    g.affine_select(out=ind8[:, :], in_=ind8[:, :], compare_op=ALU.is_ge,
                                    fill=0.0, base=127, pattern=[[-1, 1024]], channel_multiplier=128)

                @block.scalar
                def _(s):
                    s.wait_ge(S.h["ld1"], B0["ld1"] + 16)
                    s.activation(wgtmp[:, :], wg_sb[:, :], AF.Copy, accum_out=wgs[:, 0:1])

            with nc.Block() as block:
                @block.tensor
                def _(t):
                    for j in range(8):
                        t.transpose(ps_g[:, j * 8:(j + 1) * 8],
                                    wg_sb[0:8, j * 128:(j + 1) * 128], ident[0:8, 0:8])
                    t.transpose(ps_g[0:1, 64:72], wgs[0:8, 0:1], ident[0:8, 0:8])
            with nc.Block() as block:
                @block.vector
                def _(v):
                    v.tensor_copy(WgT_s[:, :], ps_g[:, 0:64])
                    v.tensor_copy(wgsr[:, :], ps_g[0:1, 64:72])
            with nc.Block() as block:
                @block.tensor
                def _(t):
                    t.matmul(ps_g[:, 72:80], onescol[0:1, :], wgsr[:, :], start=True, stop=True)
            with nc.Block() as block:
                @block.vector
                def _(v):
                    v.tensor_copy(W0[:, :], ps_g[:, 72:80])

        with ExitStack() as c2:
            xs = c2.enter_context(nc.sbuf_tensor(f"{tag}_xs", [128, NCH * TT], F32))
            xn = c2.enter_context(nc.sbuf_tensor(f"{tag}_xn", [128, NCH * TT], BF16))
            abx = c2.enter_context(nc.sbuf_tensor(f"{tag}_abx", [128, 1024], F32))
            tmpf = c2.enter_context(nc.sbuf_tensor(f"{tag}_tmpf", [128, 512], F32))
            w1T = c2.enter_context(nc.sbuf_tensor(f"{tag}_w1T", [128, NCH * HFF], BF16))
            w2T = c2.enter_context(nc.sbuf_tensor(f"{tag}_w2T", [128, NCH * HFF], BF16))
            w3T = c2.enter_context(nc.sbuf_tensor(f"{tag}_w3T", [128, 16 * C], BF16))
            gbuf = c2.enter_context(nc.sbuf_tensor(f"{tag}_g", [128, 16 * TT], BF16))
            h1s = c2.enter_context(nc.sbuf_tensor(f"{tag}_h1s", [128, TT], BF16))
            wexp = c2.enter_context(nc.sbuf_tensor(f"{tag}_wex", [128, TT], BF16))
            w_tok = c2.enter_context(nc.sbuf_tensor(f"{tag}_wtok", [128, 64], F32))
            w_fm = c2.enter_context(nc.sbuf_tensor(f"{tag}_wfm", [8, 1024], BF16))
            gate_sb = c2.enter_context(nc.sbuf_tensor(f"{tag}_gt", [128, 8], F32))
            msk = c2.enter_context(nc.sbuf_tensor(f"{tag}_msk", [128, 8], F32))
            g2b = c2.enter_context(nc.sbuf_tensor(f"{tag}_g2", [128, 8], F32))
            egb = c2.enter_context(nc.sbuf_tensor(f"{tag}_eg", [128, 8], F32))
            wrb = c2.enter_context(nc.sbuf_tensor(f"{tag}_wr", [128, 8], F32))
            m1b = c2.enter_context(nc.sbuf_tensor(f"{tag}_m1", [128, 1], F32))
            nm1b = c2.enter_context(nc.sbuf_tensor(f"{tag}_nm1", [128, 1], F32))
            m2b = c2.enter_context(nc.sbuf_tensor(f"{tag}_m2", [128, 1], F32))
            ddb = c2.enter_context(nc.sbuf_tensor(f"{tag}_dd", [128, 1], F32))
            edb = c2.enter_context(nc.sbuf_tensor(f"{tag}_ed", [128, 1], F32))
            wvb = c2.enter_context(nc.sbuf_tensor(f"{tag}_wv", [128, 1], F32))
            PB1 = c2.enter_context(nc.psum_tensor(f"{tag}_PB1", [128, 1024], F32))
            PB2 = c2.enter_context(nc.psum_tensor(f"{tag}_PB2", [128, 1024], F32))
            PB3 = c2.enter_context(nc.psum_tensor(f"{tag}_PB3", [128, 1024], F32))
            psG = c2.enter_context(nc.psum_tensor(f"{tag}_psG", [128, 8], F32))
            psT = c2.enter_context(nc.psum_tensor(f"{tag}_psT", [128, 128], F32))

            B = S.base("ld0", "tns", "vec", "scl", "sto", "aux")
            TNS_T, SCL_T, VEC_T, AUX_T = 346, 144, 211, 112
            ab_mm = lambda i, r: B["tns"] + TNS_T * i + (r + 1)
            gate_mm = lambda i, u: B["tns"] + TNS_T * i + 2 + (u + 1)
            wT_mm = lambda i, u: B["tns"] + TNS_T * i + 10 + (u + 1)
            eb_t = lambda i, e: B["tns"] + TNS_T * i + 18 + 41 * e
            wexp_mm = lambda i, e: eb_t(i, e) + 1
            h1_mm = lambda i, e, hh: eb_t(i, e) + 1 + 2 * hh + 1
            h2_mm = lambda i, e, hh: eb_t(i, e) + 1 + 2 * hh + 2
            y_mm = lambda i, e, d: eb_t(i, e) + 33 + (d + 1)
            eg_s = lambda i, u: B["scl"] + SCL_T * i + 2 * u + 1
            ed_s = lambda i, u: B["scl"] + SCL_T * i + 2 * u + 2
            silu_s = lambda i, e, hh: B["scl"] + SCL_T * i + 16 + 16 * e + hh + 1
            ab_ev = lambda i, r: B["vec"] + VEC_T * i + (r + 1)
            apl = lambda i: B["vec"] + VEC_T * i + 3
            gva = lambda i, u: B["aux"] + AUX_T * i + 14 * u + 8
            gvb = lambda i, u: B["aux"] + AUX_T * i + 14 * u + 14
            wfm_ev = lambda i, u: B["vec"] + VEC_T * i + 3 + (u + 1)
            eb_v = lambda i, e: B["vec"] + VEC_T * i + 11 + 25 * e
            wexp_ev = lambda i, e: eb_v(i, e) + 1
            gtt_v = lambda i, e, hh: eb_v(i, e) + 1 + (hh + 1)
            yev_v = lambda i, e, d: eb_v(i, e) + 17 + (d + 1)
            sto_done = lambda i: B["sto"] + 128 * (i + 1)
            ld_done = lambda i: B["ld0"] + 128 * (i + 1)
            wld = {}

            with nc.Block() as block:
                @block.sync
                def _(sync):
                    for i in range(NTT):
                        if i >= 1:
                            sync.wait_ge(S.h["sto"], sto_done(i - 1))
                        for c in range(NCH):
                            S.inc("ld0", sync.dma_start(
                                out=xs[:, c * TT:(c + 1) * TT],
                                in_=scr_in[c * 128:(c + 1) * 128, i * TT:(i + 1) * TT]), 16)
                        for e in range(NE):
                            if (i, e) != (0, 0):
                                pi, pe = (i, e - 1) if e > 0 else (i - 1, NE - 1)
                                sync.wait_ge(S.h["tns"], h2_mm(pi, pe, 15))
                            for c in range(NCH):
                                inst = sync.dma_start(
                                    out=w1T[:, c * HFF:(c + 1) * HFF],
                                    in_=D["w1_bf"][e * HFF:(e + 1) * HFF, c * 128:(c + 1) * 128],
                                    transpose=True)
                                S.inc("ld1", inst, 16)
                                inst = sync.dma_start(
                                    out=w2T[:, c * HFF:(c + 1) * HFF],
                                    in_=D["w2_bf"][e * HFF:(e + 1) * HFF, c * 128:(c + 1) * 128],
                                    transpose=True)
                                S.inc("ld1", inst, 16)
                            wld[(i, e, "w12")] = S.v["ld1"]
                            if (i, e) != (0, 0):
                                pi, pe = (i, e - 1) if e > 0 else (i - 1, NE - 1)
                                sync.wait_ge(S.h["tns"], y_mm(pi, pe, 7))
                            for hh in range(16):
                                inst = sync.dma_start(
                                    out=w3T[:, hh * C:(hh + 1) * C],
                                    in_=D["w3_bf"][e * C:(e + 1) * C, hh * 128:(hh + 1) * 128],
                                    transpose=True)
                                S.inc("ld1", inst, 16)
                            wld[(i, e, "w3")] = S.v["ld1"]
                        for o in range(NCH):
                            sync.wait_ge(S.h["vec"], yev_v(i, NE - 1, o))
                            S.inc("sto", sync.dma_start(
                                out=scr_out[o * 128:(o + 1) * 128, i * TT:(i + 1) * TT],
                                in_=xs[:, o * TT:(o + 1) * TT]), 16)
                    sync.wait_ge(S.h["sto"], sto_done(NTT - 1))

                @block.tensor
                def _(t):
                    for i in range(NTT):
                        if i >= 1:
                            t.wait_ge(S.h["vec"], wexp_ev(i - 1, NE - 1))
                        for r in range(2):
                            if r == 1:
                                t.wait_ge(S.h["vec"], ab_ev(i, 0))
                            t.matmul(PB1[:, 0:512], onescol_bf[0:1, :],
                                     a_bf[:, i * TT + r * 512: i * TT + (r + 1) * 512],
                                     start=True, stop=True)
                            inst = t.matmul(PB1[:, 512:1024], onescol_bf[64:65, :],
                                            b_bf[:, i * TT + r * 512: i * TT + (r + 1) * 512],
                                            start=True, stop=True)
                            S.inc("tns", inst, 1)
                        for u in range(8):
                            t.wait_ge(S.h["ld0"], ld_done(i))
                            if (i, u) != (0, 0):
                                pi, pu = (i, u - 1) if u > 0 else (i - 1, 7)
                                t.wait_ge(S.h["aux"], gva(pi, pu))
                            for c in range(NCH):
                                inst = t.matmul(psG[:, :],
                                                xs[:, c * TT + u * 128: c * TT + (u + 1) * 128],
                                                WgT_s[:, c * 8:(c + 1) * 8],
                                                start=(c == 0), stop=(c == NCH - 1))
                            S.inc("tns", inst, 1)
                        for u in range(8):
                            t.wait_ge(S.h["aux"], gvb(i, u))
                            if (i, u) != (0, 0):
                                pi, pu = (i, u - 1) if u > 0 else (i - 1, 7)
                                t.wait_ge(S.h["vec"], wfm_ev(pi, pu))
                            S.inc("tns", t.transpose(psT[0:8, :], w_tok[:, u * 8:(u + 1) * 8],
                                                     ident[:, :]), 1)
                        for e in range(NE):
                            if e > 0:
                                t.wait_ge(S.h["vec"], wexp_ev(i, e - 1))
                            else:
                                t.wait_ge(S.h["vec"], ab_ev(i, 1))
                            t.wait_ge(S.h["vec"], wfm_ev(i, 7))
                            t.matmul(PB1[:, 0:512], ind8[:, e * 128:(e + 1) * 128],
                                     w_fm[:, 0:512], start=True, stop=True)
                            inst = t.matmul(PB1[:, 512:1024], ind8[:, e * 128:(e + 1) * 128],
                                            w_fm[:, 512:1024], start=True, stop=True)
                            S.inc("tns", inst, 1)
                            t.wait_ge(S.h["ld1"], wld[(i, e, "w12")])
                            if e == 0:
                                t.wait_ge(S.h["vec"], apl(i))
                            for hh in range(16):
                                if (e, hh) != (0, 0):
                                    ph = (e, hh - 1) if hh > 0 else (e - 1, 15)
                                    t.wait_ge(S.h["scl"], silu_s(i, ph[0], ph[1]))
                                elif i > 0:
                                    t.wait_ge(S.h["scl"], silu_s(i - 1, NE - 1, 15))
                                if hh == 0:
                                    if e > 0:
                                        t.wait_ge(S.h["vec"], yev_v(i, e - 1, 7))
                                    elif i > 0:
                                        t.wait_ge(S.h["vec"], yev_v(i - 1, NE - 1, 7))
                                for c in range(NCH):
                                    for nn in range(2):
                                        inst = t.matmul(
                                            PB2[:, nn * 512:(nn + 1) * 512],
                                            w1T[:, c * HFF + hh * 128: c * HFF + (hh + 1) * 128],
                                            xn[:, c * TT + nn * 512: c * TT + (nn + 1) * 512],
                                            start=(c == 0), stop=(c == NCH - 1))
                                S.inc("tns", inst, 1)
                                if (e, hh) != (0, 0):
                                    ph = (e, hh - 1) if hh > 0 else (e - 1, 15)
                                    t.wait_ge(S.h["vec"], gtt_v(i, ph[0], ph[1]))
                                elif i > 0:
                                    t.wait_ge(S.h["vec"], gtt_v(i - 1, NE - 1, 15))
                                for c in range(NCH):
                                    for nn in range(2):
                                        inst = t.matmul(
                                            PB3[:, nn * 512:(nn + 1) * 512],
                                            w2T[:, c * HFF + hh * 128: c * HFF + (hh + 1) * 128],
                                            xn[:, c * TT + nn * 512: c * TT + (nn + 1) * 512],
                                            start=(c == 0), stop=(c == NCH - 1))
                                S.inc("tns", inst, 1)
                            t.wait_ge(S.h["ld1"], wld[(i, e, "w3")])
                            t.wait_ge(S.h["vec"], gtt_v(i, e, 15))
                            for d in range(NCH):
                                PY = PB2 if d % 2 == 0 else PB3
                                if d >= 2:
                                    t.wait_ge(S.h["vec"], yev_v(i, e, d - 2))
                                for hh in range(16):
                                    for nn in range(2):
                                        inst = t.matmul(
                                            PY[:, nn * 512:(nn + 1) * 512],
                                            w3T[:, hh * C + d * 128: hh * C + (d + 1) * 128],
                                            gbuf[:, hh * TT + nn * 512: hh * TT + (nn + 1) * 512],
                                            start=(hh == 0), stop=(hh == 15))
                                S.inc("tns", inst, 1)

                @block.scalar
                def _(s):
                    for i in range(NTT):
                        for u in range(8):
                            s.wait_ge(S.h["aux"], gva(i, u))
                            S.inc("scl", s.activation(egb[:, :], gate_sb[:, :], AF.Exp,
                                                      bias=nm1b[:, 0:1]), 1)
                            S.inc("scl", s.activation(edb[:, :], ddb[:, :], AF.Exp), 1)
                        for e in range(NE):
                            for hh in range(16):
                                s.wait_ge(S.h["tns"], h1_mm(i, e, hh))
                                if (e, hh) != (0, 0):
                                    ph = (e, hh - 1) if hh > 0 else (e - 1, 15)
                                    s.wait_ge(S.h["vec"], gtt_v(i, ph[0], ph[1]))
                                elif i > 0:
                                    s.wait_ge(S.h["vec"], gtt_v(i - 1, NE - 1, 15))
                                S.inc("scl", s.activation(h1s[:, :], PB2[:, :], AF.Silu), 1)

                @block.vector
                def _(v):
                    for i in range(NTT):
                        v.wait_ge(S.h["ld0"], ld_done(i))
                        if i > 0:
                            v.wait_ge(S.h["tns"], h2_mm(i - 1, NE - 1, 15))
                        for r in range(2):
                            v.wait_ge(S.h["tns"], ab_mm(i, r))
                            v.tensor_copy(abx[:, 0:512], PB1[:, 0:512])
                            S.inc("vec", v.tensor_copy(abx[:, 512:1024], PB1[:, 512:1024]), 1)
                            for c in range(NCH):
                                v.tensor_tensor(tmpf[:, 0:512],
                                                xs[:, c * TT + r * 512: c * TT + (r + 1) * 512],
                                                abx[:, 0:512], ALU.mult)
                                inst = v.tensor_tensor(
                                    xn[:, c * TT + r * 512: c * TT + (r + 1) * 512],
                                    tmpf[:, 0:512], abx[:, 512:1024], ALU.add)
                        S.inc("vec", inst, 1)
                        for u in range(8):
                            tc = i * 8 + u
                            v.wait_ge(S.h["tns"], gate_mm(i, u))

                            def step(inst):
                                S.inc("aux", inst, 1)
                                v.wait_ge(S.h["aux"], S.v["aux"])
                                return inst

                            step(v.tensor_scalar(gate_sb[:, :], psG[:, :], abT[:, tc:tc + 1],
                                                 None, ALU.mult))
                            step(v.scalar_tensor_tensor(gate_sb[:, :], W0[:, :],
                                                        abT[:, 32 + tc:33 + tc],
                                                        gate_sb[:, :], ALU.mult, ALU.add))
                            step(v.reduce_max(m1b[:, :], gate_sb[:, :], axis=AX.X))
                            step(v.tensor_scalar(nm1b[:, :], m1b[:, :], -1.0, None, ALU.mult))
                            step(v.tensor_scalar(msk[:, :], gate_sb[:, :], m1b[:, 0:1], None,
                                                 ALU.is_equal))
                            step(v.scalar_tensor_tensor(g2b[:, :], msk[:, :], -1e9, gate_sb[:, :],
                                                        ALU.mult, ALU.add))
                            step(v.reduce_max(m2b[:, :], g2b[:, :], axis=AX.X))
                            step(v.tensor_tensor(ddb[:, :], m2b[:, :], nm1b[:, :], ALU.add))
                            v.wait_ge(S.h["scl"], ed_s(i, u))
                            step(v.tensor_scalar(msk[:, :], gate_sb[:, :], m2b[:, 0:1], None,
                                                 ALU.is_ge))
                            step(v.tensor_tensor(wrb[:, :], egb[:, :], msk[:, :], ALU.mult))
                            step(v.tensor_scalar(edb[:, :], edb[:, :], 1.0, None, ALU.add))
                            step(v.reciprocal(wvb[:, :], edb[:, :]))
                            step(v.tensor_scalar(wvb[:, :], wvb[:, :], 0.5, None, ALU.mult))
                            step(v.tensor_scalar(w_tok[:, u * 8:(u + 1) * 8], wrb[:, :],
                                                  wvb[:, 0:1], None, ALU.mult))
                        for u in range(8):
                            v.wait_ge(S.h["tns"], wT_mm(i, u))
                            S.inc("vec", v.tensor_copy(w_fm[:, u * 128:(u + 1) * 128],
                                                       psT[0:8, :]), 1)
                        for e in range(NE):
                            v.wait_ge(S.h["tns"], wexp_mm(i, e))
                            S.inc("vec", v.tensor_copy(wexp[:, :], PB1[:, :]), 1)
                            for hh in range(16):
                                v.wait_ge(S.h["scl"], silu_s(i, e, hh))
                                v.wait_ge(S.h["tns"], h2_mm(i, e, hh))
                                v.tensor_tensor(gbuf[:, hh * TT:(hh + 1) * TT], h1s[:, :],
                                                PB3[:, :], ALU.mult)
                                S.inc("vec", v.tensor_tensor(gbuf[:, hh * TT:(hh + 1) * TT],
                                                             gbuf[:, hh * TT:(hh + 1) * TT],
                                                             wexp[:, :], ALU.mult), 1)
                            for d in range(NCH):
                                PY = PB2 if d % 2 == 0 else PB3
                                v.wait_ge(S.h["tns"], y_mm(i, e, d))
                                S.inc("vec", v.tensor_tensor(xs[:, d * TT:(d + 1) * TT],
                                                             xs[:, d * TT:(d + 1) * TT],
                                                             PY[:, :], ALU.add), 1)

            assert S.v["tns"] == B["tns"] + TNS_T * NTT, (S.v["tns"], B["tns"] + TNS_T * NTT)
            assert S.v["scl"] == B["scl"] + SCL_T * NTT
            assert S.v["vec"] == B["vec"] + VEC_T * NTT, (S.v["vec"], B["vec"] + VEC_T * NTT)
            assert S.v["aux"] == B["aux"] + AUX_T * NTT


STAGES = {
    "common": dict(emit=emit_common, io={}),
    "entry": dict(emit=emit_entry, io={"x_in": ((T, C), F32, "in")}),
    "exit": dict(emit=emit_exit, io={"x_out": ((T, C), F32, "out")}),
    "prep_l0ffn": dict(emit=emit_prep_l0ffn, io={
        "W_r": ((C, C), F32, "in"), "W_k": ((C, C), F32, "in"), "W_v": ((C, C), F32, "in"),
        "wr_bf": ((C, C), BF16, "internal"), "wk_bf": ((C, C), BF16, "internal"),
        "wv_bf": ((C, C), BF16, "internal")}),
    "s1": dict(emit=lambda nc, D, S: emit_l0ffn(nc, D, S, "f1"), io={
        "mu_r": ((1, C), F32, "in"), "mu_k": ((1, C), F32, "in"),
        "wr_bf": ((C, C), BF16, "internal"), "wk_bf": ((C, C), BF16, "internal"),
        "wv_bf": ((C, C), BF16, "internal")}),
    "s3": dict(emit=lambda nc, D, S: emit_l0ffn(nc, D, S, "f2"), io={
        "mu_r": ((1, C), F32, "in"), "mu_k": ((1, C), F32, "in"),
        "wr_bf": ((C, C), BF16, "internal"), "wk_bf": ((C, C), BF16, "internal"),
        "wv_bf": ((C, C), BF16, "internal")}),
    "prep_l0attn": dict(emit=emit_prep_l0attn, io={
        "WQ": ((C, C), F32, "in"), "WK": ((C, C), F32, "in"),
        "WV": ((C, C), F32, "in"), "WO": ((C, C), F32, "in"),
        "wq_bf": ((C, C), BF16, "internal"), "wk_bf2": ((C, C), BF16, "internal"),
        "wv_bf2": ((C, C), BF16, "internal"), "wo_bf": ((C, C), BF16, "internal")}),
    "prep_l1ffn": dict(emit=emit_prep_l1ffn, io={
        "W1": ((NE * HFF, C), F32, "in"), "W2": ((NE * HFF, C), F32, "in"),
        "W3": ((NE * C, HFF), F32, "in"),
        "w1_bf": ((NE * HFF, C), BF16, "internal"), "w2_bf": ((NE * HFF, C), BF16, "internal"),
        "w3_bf": ((NE * C, HFF), BF16, "internal")}),
    "s4": dict(emit=lambda nc, D, S: emit_moe(nc, D, S, "m1"), io={
        "Wg": ((NE, C), F32, "in"),
        "w1_bf": ((NE * HFF, C), BF16, "internal"), "w2_bf": ((NE * HFF, C), BF16, "internal"),
        "w3_bf": ((NE * C, HFF), BF16, "internal")}),
    "s6": dict(emit=lambda nc, D, S: emit_moe(nc, D, S, "m2"), io={
        "Wg": ((NE, C), F32, "in"),
        "w1_bf": ((NE * HFF, C), BF16, "internal"), "w2_bf": ((NE * HFF, C), BF16, "internal"),
        "w3_bf": ((NE * C, HFF), BF16, "internal")}),
    "prep_l1attn": dict(emit=emit_prep_l1attn, io={
        "WQKV": ((3072, C), F32, "in"), "WPROJ": ((C, C), F32, "in"),
        "wqkv_bf": ((3072, C), BF16, "internal"), "wproj_bf": ((C, C), BF16, "internal")}),
    "s5": dict(emit=emit_l1attn, io={
        "b_proj": ((1, C), F32, "in"),
        "wqkv_bf": ((3072, C), BF16, "internal"), "wproj_bf": ((C, C), BF16, "internal")}),
    "s2": dict(emit=emit_l0attn, io={
        "bo": ((1, C), F32, "in"),
        "wq_bf": ((C, C), BF16, "internal"), "wk_bf2": ((C, C), BF16, "internal"),
        "wv_bf2": ((C, C), BF16, "internal"), "wo_bf": ((C, C), BF16, "internal"),
        "q_scr": ((C, T), BF16, "internal"), "k_scr": ((C, T), BF16, "internal"),
        "v_scr": ((T, C), BF16, "internal"), "a_scr": ((16 * T, 64), BF16, "internal")}),
}





def build_program(stage_names, scr_mode="internal", uses_scr=True, scr_need="rw"):
    nc = bass.Bass("TRN2", target_bir_lowering=False)
    io = {}
    for sn in stage_names:
        for name, spec in STAGES[sn].get("io", {}).items():
            io[name] = spec
    D = {}
    for name, (shape, dtype, kind) in io.items():
        k = {"in": "ExternalInput", "out": "ExternalOutput", "internal": "Internal"}[kind]
        D[name] = nc.dram_tensor(name, list(shape), dtype, kind=k).ap()
    if uses_scr:
        if scr_mode == "internal":
            scr = nc.dram_tensor("scr", [C, T], F32, kind="Internal").ap()
            D["scr_in"] = scr
            D["scr_out"] = scr
        else:
            if "r" in scr_need:
                D["scr_in"] = nc.dram_tensor("scr_in", [C, T], F32, kind="ExternalInput").ap()
            if "w" in scr_need:
                D["scr_out"] = nc.dram_tensor("scr_out", [C, T], F32, kind="ExternalOutput").ap()
    with ExitStack() as stack:
        for name, (shape, dtype) in COMMON_SBUF.items():
            D[name] = stack.enter_context(nc.sbuf_tensor(name, shape, dtype))
        S = Sems(nc, stack, SEM_NAMES)
        for sn in stage_names:
            STAGES[sn]["emit"](nc, D, S)
    return nc


FUSED_STAGES = ["common", "prep_l0ffn", "entry", "prep_l0attn", "prep_l1attn", "prep_l1ffn",
                "s1", "s2", "s3", "s4", "s5", "s6", "exit"]


def kernel(**inputs):
    f32 = np.float32
    x = np.ascontiguousarray(np.asarray(inputs["L0_attn_x"], dtype=f32))
    n = 8

    def arr(name, shape=None):
        a = np.ascontiguousarray(np.asarray(inputs[name], dtype=f32))
        return a.reshape(shape) if shape is not None else a

    wmap = dict(W_r=arr("L0_ffn_W_r"), W_k=arr("L0_ffn_W_k"), W_v=arr("L0_ffn_W_v"),
                mu_r=arr("L0_ffn_mu_r", (1, C)), mu_k=arr("L0_ffn_mu_k", (1, C)),
                WQ=arr("L0_attn_Wq"), WK=arr("L0_attn_Wk"), WV=arr("L0_attn_Wv"),
                WO=arr("L0_attn_Wo"), bo=arr("L0_attn_bo", (1, C)),
                Wg=arr("L1_ffn_Wg"), W1=arr("L1_ffn_W1", (NE * HFF, C)),
                W2=arr("L1_ffn_W2", (NE * HFF, C)), W3=arr("L1_ffn_W3", (NE * C, HFF)),
                WQKV=arr("L1_attn_w_qkv"), WPROJ=arr("L1_attn_w_proj"),
                b_proj=arr("L1_attn_b_proj", (1, C)))

    nc = build_program(FUSED_STAGES, scr_mode="internal")
    in_maps = [dict(x_in=np.ascontiguousarray(x[i]), **wmap) for i in range(n)]
    res = run_bass_kernel_spmd(nc, in_maps, core_ids=list(range(n)))
    return np.stack([res.results[i]["x_out"] for i in range(n)], axis=0)


# version 12
# speedup vs baseline: 1.0550x; 1.0550x over previous
import math
from contextlib import ExitStack

import numpy as np
import concourse.bass as bass
import concourse.mybir as mybir
from concourse.bass_utils import run_bass_kernel_spmd

F32 = mybir.dt.float32
BF16 = mybir.dt.bfloat16
AF = mybir.ActivationFunctionType
ALU = mybir.AluOpType
AX = mybir.AxisListType

T = 4096
C = 1024
NCH = 8
NTCH = 32
NT = 8
H = 16
HD = 64
NE = 8
HFF = 2048


class Sems:

    def __init__(self, nc, stack, names):
        self.nc = nc
        self.h = {}
        self.v = {}
        self.marks = {}
        for n in names:
            self.h[n] = stack.enter_context(nc.semaphore(n))
            self.v[n] = 0

    def inc(self, name, inst, n=1):
        inst.then_inc(self.h[name], n)
        self.v[name] += n
        return self.v[name]

    def base(self, *names):
        return {n: self.v[n] for n in names}


SEM_NAMES = ["ld0", "ld1", "tns", "vec", "scl", "sto", "cast", "aux"]



def emit_common(nc, D, S):
    ident = D["cm_ident"]
    ones_bf = D["cm_ones_bf"]
    onescol = D["cm_onescol"]
    with nc.Block() as block:
        @block.gpsimd
        def _(g):
            g.memset(ident[:, :], 0.0)
            g.affine_select(out=ident[:, :], in_=ident[:, :], compare_op=ALU.not_equal,
                            fill=1.0, base=0, pattern=[[-1, 128]], channel_multiplier=1)
            g.memset(ones_bf[:, :], 1.0)
            g.memset(onescol[:, :], 1.0)
            g.memset(D["cm_eps"][:, :], 1e-5)
            g.memset(D["cm_onescol_bf"][:, :], 1.0)
            g.memset(D["cm_ident_bf"][:, :], 0.0)
            g.affine_select(out=D["cm_ident_bf"][:, :], in_=D["cm_ident_bf"][:, :],
                            compare_op=ALU.not_equal, fill=1.0, base=0,
                            pattern=[[-1, 128]], channel_multiplier=1)


COMMON_SBUF = {
    "cm_ident": ([128, 128], F32),
    "cm_ident_bf": ([128, 128], BF16),
    "cm_ones_bf": ([128, 1], BF16),
    "cm_onescol": ([128, 128], F32),
    "cm_onescol_bf": ([128, 128], BF16),
    "cm_eps": ([1, 1], F32),
}



def emit_entry(nc, D, S):
    x_in, scr = D["x_in"], D["scr_out"]
    ident = D["cm_ident"]
    with ExitStack() as ctx:
        xt = [ctx.enter_context(nc.sbuf_tensor(f"en_xt{i}", [128, C], F32)) for i in range(2)]
        xf = [ctx.enter_context(nc.sbuf_tensor(f"en_xf{c}", [128, T], F32)) for c in range(NCH)]
        ps = [ctx.enter_context(nc.psum_tensor(f"en_ps{i}", [128, 512], F32)) for i in range(4)]
        B = S.base("ld0", "tns", "vec", "sto")

        with nc.Block() as block:
            @block.sync
            def _(sync):
                for tc in range(NTCH):
                    if tc >= 2:
                        sync.wait_ge(S.h["tns"], B["tns"] + 2 * (tc - 1))
                    S.inc("ld0", sync.dma_start(out=xt[tc % 2][:, :],
                                                in_=x_in[tc * 128:(tc + 1) * 128, :]), 16)

            @block.tensor
            def _(t):
                for tc in range(NTCH):
                    t.wait_ge(S.h["ld0"], B["ld0"] + 16 * (tc + 1))
                    if tc >= 2:
                        t.wait_ge(S.h["vec"], B["vec"] + 2 * (tc - 1))
                    for c in range(NCH):
                        bank = ps[(tc % 2) * 2 + c // 4]
                        inst = t.transpose(bank[:, (c % 4) * 128:(c % 4 + 1) * 128],
                                           xt[tc % 2][:, c * 128:(c + 1) * 128], ident[:, :])
                        if c in (3, 7):
                            S.inc("tns", inst, 1)

            @block.vector
            def _(v):
                for tc in range(NTCH):
                    for half in range(2):
                        v.wait_ge(S.h["tns"], B["tns"] + 2 * tc + half + 1)
                        bank = ps[(tc % 2) * 2 + half]
                        for cc in range(4):
                            c = half * 4 + cc
                            inst = v.tensor_copy(xf[c][:, tc * 128:(tc + 1) * 128],
                                                 bank[:, cc * 128:(cc + 1) * 128])
                            if cc == 3:
                                S.inc("vec", inst, 1)

        with nc.Block() as block:
            @block.sync
            def _(sync):
                for c in range(NCH):
                    S.inc("sto", sync.dma_start(out=scr[c * 128:(c + 1) * 128, :], in_=xf[c][:, :]), 16)
                sync.wait_ge(S.h["sto"], S.v["sto"])


def emit_exit(nc, D, S):
    scr, x_out = D["scr_in"], D["x_out"]
    ident = D["cm_ident"]
    with ExitStack() as ctx:
        xf = [ctx.enter_context(nc.sbuf_tensor(f"ex_xf{c}", [128, T], F32)) for c in range(NCH)]
        xt = [ctx.enter_context(nc.sbuf_tensor(f"ex_xt{i}", [128, C], F32)) for i in range(2)]
        ps = [ctx.enter_context(nc.psum_tensor(f"ex_ps{i}", [128, 512], F32)) for i in range(4)]
        B = S.base("ld0", "tns", "vec", "sto")

        with nc.Block() as block:
            @block.sync
            def _(sync):
                for c in range(NCH):
                    S.inc("ld0", sync.dma_start(out=xf[c][:, :], in_=scr[c * 128:(c + 1) * 128, :]), 16)

        with nc.Block() as block:
            @block.tensor
            def _(t):
                t.wait_ge(S.h["ld0"], B["ld0"] + 128)
                for tc in range(NTCH):
                    if tc >= 2:
                        t.wait_ge(S.h["sto"], B["sto"] + 16 * (tc - 1))
                    for c in range(NCH):
                        bank = ps[(tc % 2) * 2 + c // 4]
                        inst = t.transpose(bank[:, (c % 4) * 128:(c % 4 + 1) * 128],
                                           xf[c][:, tc * 128:(tc + 1) * 128], ident[:, :])
                        if c in (3, 7):
                            S.inc("tns", inst, 1)

            @block.vector
            def _(v):
                for tc in range(NTCH):
                    for half in range(2):
                        v.wait_ge(S.h["tns"], B["tns"] + 2 * tc + half + 1)
                        bank = ps[(tc % 2) * 2 + half]
                        for cc in range(4):
                            c = half * 4 + cc
                            inst = v.tensor_copy(xt[tc % 2][:, c * 128:(c + 1) * 128],
                                                 bank[:, cc * 128:(cc + 1) * 128])
                            if c == 7:
                                S.inc("vec", inst, 1)

            @block.sync
            def _(sync):
                for tc in range(NTCH):
                    sync.wait_ge(S.h["vec"], B["vec"] + tc + 1)
                    S.inc("sto", sync.dma_start(out=x_out[tc * 128:(tc + 1) * 128, :],
                                                in_=xt[tc % 2][:, :]), 16)
                sync.wait_ge(S.h["sto"], S.v["sto"])




def emit_prep_l0ffn(nc, D, S):
    with nc.Block() as block:
        @block.gpsimd
        def _(g):
            S.marks["wr_bf"] = S.inc("cast", g.dma_start(out=D["wr_bf"], in_=D["W_r"]), 16)
            S.marks["wk_bf"] = S.inc("cast", g.dma_start(out=D["wk_bf"], in_=D["W_k"]), 16)
            S.marks["wv_bf"] = S.inc("cast", g.dma_start(out=D["wv_bf"], in_=D["W_v"]), 16)


def _emit_ln_stats(nc, D, S, ctx, scr, a_sb, b_sb, tag=""):
    ones_bf = D["cm_ones_bf"]
    TH = T // 2
    with ExitStack() as c2:
        xs32 = [c2.enter_context(nc.sbuf_tensor(f"{tag}ls_x{i}", [128, TH], F32)) for i in range(2)]
        xb = c2.enter_context(nc.sbuf_tensor(f"{tag}ls_xb", [128, NCH * TH], BF16))
        s1_sb = c2.enter_context(nc.sbuf_tensor(f"{tag}ls_s1", [1, T], F32))
        s2_sb = c2.enter_context(nc.sbuf_tensor(f"{tag}ls_s2", [1, T], F32))
        tmp = c2.enter_context(nc.sbuf_tensor(f"{tag}ls_tmp", [1, T], F32))
        tmp2 = c2.enter_context(nc.sbuf_tensor(f"{tag}ls_tmp2", [1, 512], F32))
        ps_s = [c2.enter_context(nc.psum_tensor(f"{tag}ls_ps{i}", [1, 512], F32)) for i in range(2)]
        B = S.base("ld0", "tns", "vec", "scl")
        cp = lambda h, c: B["scl"] + 16 * h + (c + 1)
        sq = lambda h, c: B["scl"] + 16 * h + 8 + (c + 1)
        g1 = lambda h, ts: B["tns"] + 8 * h + (ts + 1)
        g2 = lambda h, ts: B["tns"] + 8 * h + 4 + (ts + 1)
        e1 = lambda h, ts: B["vec"] + 8 * h + (ts + 1)
        e2 = lambda h, ts: B["vec"] + 8 * h + 4 + (ts + 1)

        with nc.Block() as block:
            @block.sync
            def _(sync):
                for h2 in range(2):
                    for c in range(NCH):
                        if h2 * NCH + c >= 2:
                            pc = h2 * NCH + c - 2
                            sync.wait_ge(S.h["scl"], cp(pc // NCH, pc % NCH))
                        S.inc("ld0", sync.dma_start(
                            out=xs32[c % 2][:, :],
                            in_=scr[c * 128:(c + 1) * 128, h2 * TH:(h2 + 1) * TH]), 16)

            @block.scalar
            def _(s):
                for h2 in range(2):
                    if h2 > 0:
                        s.wait_ge(S.h["tns"], g2(h2 - 1, 3))
                    for c in range(NCH):
                        s.wait_ge(S.h["ld0"], B["ld0"] + 16 * (h2 * NCH + c + 1))
                        S.inc("scl", s.copy(xb[:, c * TH:(c + 1) * TH], xs32[c % 2][:, :]), 1)
                    s.wait_ge(S.h["tns"], g1(h2, 3))
                    for c in range(NCH):
                        S.inc("scl", s.activation(xb[:, c * TH:(c + 1) * TH],
                                                  xb[:, c * TH:(c + 1) * TH], AF.Square), 1)

            @block.tensor
            def _(t):
                for h2 in range(2):
                    t.wait_ge(S.h["scl"], cp(h2, 7))
                    for ts_ in range(4):
                        if h2 > 0 or ts_ >= 2:
                            pt = h2 * 8 + ts_ - 2
                            t.wait_ge(S.h["vec"], B["vec"] + pt + 1)
                        for c in range(NCH):
                            inst = t.matmul(ps_s[ts_ % 2][:, :], ones_bf[:, :],
                                            xb[:, c * TH + ts_ * 512: c * TH + (ts_ + 1) * 512],
                                            start=(c == 0), stop=(c == NCH - 1))
                        S.inc("tns", inst, 1)
                    t.wait_ge(S.h["scl"], sq(h2, 7))
                    for ts_ in range(4):
                        pt = h2 * 8 + 4 + ts_ - 2
                        t.wait_ge(S.h["vec"], B["vec"] + pt + 1)
                        for c in range(NCH):
                            inst = t.matmul(ps_s[ts_ % 2][:, :], ones_bf[:, :],
                                            xb[:, c * TH + ts_ * 512: c * TH + (ts_ + 1) * 512],
                                            start=(c == 0), stop=(c == NCH - 1))
                        S.inc("tns", inst, 1)

            @block.vector
            def _(v):
                for h2 in range(2):
                    for ts_ in range(4):
                        v.wait_ge(S.h["tns"], g1(h2, ts_))
                        S.inc("vec", v.tensor_copy(
                            s1_sb[:, h2 * TH + ts_ * 512: h2 * TH + (ts_ + 1) * 512],
                            ps_s[ts_ % 2][:, :]), 1)
                    for ts_ in range(4):
                        v.wait_ge(S.h["tns"], g2(h2, ts_))
                        S.inc("vec", v.tensor_copy(
                            s2_sb[:, h2 * TH + ts_ * 512: h2 * TH + (ts_ + 1) * 512],
                            ps_s[ts_ % 2][:, :]), 1)
                for ts_ in range(8):
                    sl = slice(ts_ * 512, (ts_ + 1) * 512)
                    v.tensor_scalar(tmp[:, sl], s1_sb[:, sl], 1.0 / C, None, ALU.mult)
                    v.tensor_tensor(tmp2[:, :], tmp[:, sl], tmp[:, sl], ALU.mult)
                    inst = v.scalar_tensor_tensor(s2_sb[:, sl], s2_sb[:, sl], 1.0 / C,
                                                  tmp2[:, :], ALU.mult, ALU.subtract)
                S.inc("vec", inst, 1)

            @block.scalar
            def _(s):
                s.wait_ge(S.h["vec"], B["vec"] + 17)
                s.activation(s2_sb[:, :], s2_sb[:, :], AF.Ln, bias=D["cm_eps"][0:1, 0:1])
                S.inc("scl", s.activation(a_sb[:, :], s2_sb[:, :], AF.Exp, scale=-0.5), 1)

            @block.vector
            def _(v):
                v.wait_ge(S.h["scl"], B["scl"] + 33)
                S.inc("vec", v.scalar_tensor_tensor(b_sb[:, :], tmp[:, :], -1.0,
                                                    a_sb[:, :], ALU.mult, ALU.mult), 1)


def emit_l0ffn(nc, D, S, tag="f1"):
    scr_in, scr_out = D["scr_in"], D["scr_out"]
    ident, onescol = D["cm_ident"], D["cm_onescol"]
    with ExitStack() as ctx:
        wrT = ctx.enter_context(nc.sbuf_tensor(f"{tag}_wrT", [128, NCH * C], BF16))
        wkT = ctx.enter_context(nc.sbuf_tensor(f"{tag}_wkT", [128, NCH * C], BF16))
        wvT = ctx.enter_context(nc.sbuf_tensor(f"{tag}_wvT", [128, NCH * C], BF16))
        mus = ctx.enter_context(nc.sbuf_tensor(f"{tag}_mus", [1, 2 * C], F32))
        muT = ctx.enter_context(nc.sbuf_tensor(f"{tag}_muT", [128, 16], F32))
        a_sb = ctx.enter_context(nc.sbuf_tensor(f"{tag}_a", [1, T], F32))
        b_sb = ctx.enter_context(nc.sbuf_tensor(f"{tag}_b", [1, T], F32))

        with ExitStack() as c2:
            ps_mu = c2.enter_context(nc.psum_tensor(f"{tag}_psmu", [128, 16], F32))
            B = S.base("ld1", "tns", "cast")
            with nc.Block() as block:
                @block.sync
                def _(sync):
                    sync.wait_ge(S.h["cast"], S.marks["wr_bf"])
                    for c in range(NCH):
                        S.inc("ld1", sync.dma_start(out=wrT[:, c * C:(c + 1) * C],
                                                    in_=D["wr_bf"][:, c * 128:(c + 1) * 128],
                                                    transpose=True), 16)
                    sync.wait_ge(S.h["cast"], S.marks["wk_bf"])
                    for c in range(NCH):
                        S.inc("ld1", sync.dma_start(out=wkT[:, c * C:(c + 1) * C],
                                                    in_=D["wk_bf"][:, c * 128:(c + 1) * 128],
                                                    transpose=True), 16)
                    sync.wait_ge(S.h["cast"], S.marks["wv_bf"])
                    for c in range(NCH):
                        S.inc("ld1", sync.dma_start(out=wvT[:, c * C:(c + 1) * C],
                                                    in_=D["wv_bf"][:, c * 128:(c + 1) * 128],
                                                    transpose=True), 16)
                    S.inc("ld1", sync.dma_start(out=mus[:, 0:C], in_=D["mu_r"]), 16)
                    S.inc("ld1", sync.dma_start(out=mus[:, C:2 * C], in_=D["mu_k"]), 16)

                @block.tensor
                def _(t):
                    t.wait_ge(S.h["ld1"], S.v["ld1"])
                    for j in range(16):
                        t.transpose(ps_mu[:, j:j + 1], mus[0:1, j * 128:(j + 1) * 128],
                                    ident[0:1, 0:1])

            with nc.Block() as block:
                @block.vector
                def _(v):
                    v.tensor_copy(muT[:, :], ps_mu[:, :])

        _emit_ln_stats(nc, D, S, ctx, scr_in, a_sb, b_sb, tag)

        with ExitStack() as c2:
            xs = c2.enter_context(nc.sbuf_tensor(f"{tag}_xs", [128, 2 * NCH * 512], F32))
            xnb = c2.enter_context(nc.sbuf_tensor(f"{tag}_xn", [128, NCH * 513], BF16))
            xr = c2.enter_context(nc.sbuf_tensor(f"{tag}_xr", [128, NCH * 512], BF16))
            xk = c2.enter_context(nc.sbuf_tensor(f"{tag}_xk", [128, NCH * 512], BF16))
            rsb = c2.enter_context(nc.sbuf_tensor(f"{tag}_r", [128, NCH * 512], BF16))
            krelu = c2.enter_context(nc.sbuf_tensor(f"{tag}_kr", [128, 2 * 512], BF16))
            kact = c2.enter_context(nc.sbuf_tensor(f"{tag}_ka", [128, NCH * 512], BF16))
            dbuf = c2.enter_context(nc.sbuf_tensor(f"{tag}_d", [128, 512], F32))
            tmpf = c2.enter_context(nc.sbuf_tensor(f"{tag}_tf", [128, 512], F32))
            abx = c2.enter_context(nc.sbuf_tensor(f"{tag}_abx", [128, 1024], F32))
            ps_ab = c2.enter_context(nc.psum_tensor(f"{tag}_psab", [128, 1024], F32))
            psA = [c2.enter_context(nc.psum_tensor(f"{tag}_psA{i}", [128, 512], F32)) for i in range(2)]

            B = S.base("ld0", "tns", "vec", "scl", "sto")
            ld_done = lambda i: B["ld0"] + 128 * (i + 1)
            ab_mm = lambda i: B["tns"] + 25 * i + 1
            r_mm = lambda i, o: B["tns"] + 25 * i + 1 + (o + 1)
            k_mm = lambda i, o: B["tns"] + 25 * i + 9 + (o + 1)
            v_mm = lambda i, o: B["tns"] + 25 * i + 17 + (o + 1)
            sig = lambda i, o: B["scl"] + 16 * i + (o + 1)
            relu = lambda i, o: B["scl"] + 16 * i + 8 + (o + 1)
            ab_evac = lambda i: B["vec"] + 18 * i + 1
            mix_rdy = lambda i: B["vec"] + 18 * i + 2
            kact_v = lambda i, o: B["vec"] + 18 * i + 2 + (o + 1)
            upd_v = lambda i, o: B["vec"] + 18 * i + 10 + (o + 1)
            sto_done = lambda i: B["sto"] + 128 * (i + 1)

            def xsv(i, c):
                return xs[:, ((i % 2) * NCH + c) * 512:((i % 2) * NCH + c) * 512 + 512]

            with nc.Block() as block:
                @block.sync
                def _(sync):
                    for i in range(2):
                        for c in range(NCH):
                            S.inc("ld0", sync.dma_start(
                                out=xsv(i, c),
                                in_=scr_in[c * 128:(c + 1) * 128, i * 512:(i + 1) * 512]), 16)
                    for i in range(NT):
                        for o in range(NCH):
                            sync.wait_ge(S.h["vec"], upd_v(i, o))
                            S.inc("sto", sync.dma_start(
                                out=scr_out[o * 128:(o + 1) * 128, i * 512:(i + 1) * 512],
                                in_=xsv(i, o)), 16)
                        if i + 2 < NT:
                            sync.wait_ge(S.h["sto"], sto_done(i))
                            for c in range(NCH):
                                S.inc("ld0", sync.dma_start(
                                    out=xsv(i + 2, c),
                                    in_=scr_in[c * 128:(c + 1) * 128,
                                               (i + 2) * 512:(i + 3) * 512]), 16)
                    sync.wait_ge(S.h["sto"], sto_done(NT - 1))

                @block.tensor
                def _(t):
                    for i in range(NT):
                        if i >= 1:
                            t.wait_ge(S.h["vec"], ab_evac(i - 1))
                        t.matmul(ps_ab[:, 0:512], onescol[0:1, :], a_sb[:, i * 512:(i + 1) * 512],
                                 start=True, stop=True)
                        inst = t.matmul(ps_ab[:, 512:1024], onescol[0:1, :], b_sb[:, i * 512:(i + 1) * 512],
                                        start=True, stop=True)
                        S.inc("tns", inst, 1)
                        t.wait_ge(S.h["vec"], mix_rdy(i))
                        for o in range(NCH):
                            if o >= 2:
                                t.wait_ge(S.h["scl"], sig(i, o - 2))
                            elif i > 0:
                                t.wait_ge(S.h["vec"], upd_v(i - 1, o + 6))
                            for c in range(NCH):
                                inst = t.matmul(psA[o % 2][:, :],
                                                wrT[:, c * C + o * 128: c * C + (o + 1) * 128],
                                                xr[:, c * 512:(c + 1) * 512],
                                                start=(c == 0), stop=(c == NCH - 1))
                            S.inc("tns", inst, 1)
                        for o in range(NCH):
                            if o < 2:
                                t.wait_ge(S.h["scl"], sig(i, o + 6))
                            else:
                                t.wait_ge(S.h["vec"], kact_v(i, o - 2))
                            for c in range(NCH):
                                inst = t.matmul(psA[o % 2][:, :],
                                                wkT[:, c * C + o * 128: c * C + (o + 1) * 128],
                                                xk[:, c * 512:(c + 1) * 512],
                                                start=(c == 0), stop=(c == NCH - 1))
                            S.inc("tns", inst, 1)
                        t.wait_ge(S.h["vec"], kact_v(i, 7))
                        for o in range(NCH):
                            if o >= 2:
                                t.wait_ge(S.h["vec"], upd_v(i, o - 2))
                            for c in range(NCH):
                                inst = t.matmul(psA[o % 2][:, :],
                                                wvT[:, c * C + o * 128: c * C + (o + 1) * 128],
                                                kact[:, c * 512:(c + 1) * 512],
                                                start=(c == 0), stop=(c == NCH - 1))
                            S.inc("tns", inst, 1)

                @block.scalar
                def _(s):
                    for i in range(NT):
                        for o in range(NCH):
                            s.wait_ge(S.h["tns"], r_mm(i, o))
                            S.inc("scl", s.activation(rsb[:, o * 512:(o + 1) * 512],
                                                      psA[o % 2][:, :], AF.Sigmoid), 1)
                        for o in range(NCH):
                            s.wait_ge(S.h["tns"], k_mm(i, o))
                            if o >= 2:
                                s.wait_ge(S.h["vec"], kact_v(i, o - 2))
                            S.inc("scl", s.activation(krelu[:, (o % 2) * 512:(o % 2 + 1) * 512],
                                                      psA[o % 2][:, :], AF.Relu), 1)

                @block.vector
                def _(v):
                    for i in range(NT):
                        v.wait_ge(S.h["tns"], ab_mm(i))
                        v.tensor_copy(abx[:, 0:512], ps_ab[:, 0:512])
                        S.inc("vec", v.tensor_copy(abx[:, 512:1024], ps_ab[:, 512:1024]), 1)
                        v.wait_ge(S.h["ld0"], ld_done(i))
                        for c in range(NCH):
                            if i == 0:
                                v.memset(xnb[:, c * 513:c * 513 + 1], 0.0)
                            else:
                                v.tensor_copy(xnb[:, c * 513:c * 513 + 1],
                                              xnb[:, c * 513 + 512:c * 513 + 513])
                        for c in range(NCH):
                            v.tensor_tensor(tmpf[:, :], xsv(i, c), abx[:, 0:512], ALU.mult)
                            v.tensor_tensor(xnb[:, c * 513 + 1:c * 513 + 513], tmpf[:, :],
                                            abx[:, 512:1024], ALU.add)
                            xn_new = xnb[:, c * 513 + 1:c * 513 + 513]
                            xn_sh = xnb[:, c * 513:c * 513 + 512]
                            v.tensor_tensor(dbuf[:, :], xn_new, xn_sh, ALU.subtract)
                            v.scalar_tensor_tensor(xr[:, c * 512:(c + 1) * 512], dbuf[:, :],
                                                   muT[:, c:c + 1], xn_sh, ALU.mult, ALU.add)
                            inst = v.scalar_tensor_tensor(xk[:, c * 512:(c + 1) * 512], dbuf[:, :],
                                                          muT[:, 8 + c:8 + c + 1], xn_sh,
                                                          ALU.mult, ALU.add)
                        S.inc("vec", inst, 1)
                        for o in range(NCH):
                            v.wait_ge(S.h["scl"], relu(i, o))
                            S.inc("vec", v.tensor_tensor(kact[:, o * 512:(o + 1) * 512],
                                                         psA[o % 2][:, :],
                                                         krelu[:, (o % 2) * 512:(o % 2 + 1) * 512],
                                                         ALU.mult), 1)
                        for o in range(NCH):
                            v.wait_ge(S.h["tns"], v_mm(i, o))
                            v.tensor_tensor(tmpf[:, :], psA[o % 2][:, :],
                                            rsb[:, o * 512:(o + 1) * 512], ALU.mult)
                            S.inc("vec", v.scalar_tensor_tensor(xsv(i, o), tmpf[:, :], 0.5,
                                                                xsv(i, o), ALU.mult, ALU.add), 1)

            assert S.v["tns"] == B["tns"] + 25 * NT, (S.v["tns"], B["tns"])
            assert S.v["vec"] == B["vec"] + 18 * NT
            assert S.v["scl"] == B["scl"] + 16 * NT
            assert S.v["ld0"] == B["ld0"] + 128 * NT
            assert S.v["sto"] == B["sto"] + 128 * NT



ATTN_W = (("WQ", "wq_bf"), ("WK", "wk_bf2"), ("WV", "wv_bf2"), ("WO", "wo_bf"))


def emit_prep_l0attn(nc, D, S):
    with nc.Block() as block:
        @block.gpsimd
        def _(g):
            for src_name, dst in ATTN_W:
                S.marks[dst] = S.inc("cast", g.dma_start(out=D[dst], in_=D[src_name]), 16)


def emit_l0attn(nc, D, S):
    scr_in, scr_out = D["scr_in"], D["scr_out"]
    ident, onescol = D["cm_ident"], D["cm_onescol"]
    ident_bf = D["cm_ident_bf"]
    q_scr, k_scr, v_scr, a_scr = D["q_scr"], D["k_scr"], D["v_scr"], D["a_scr"]

    with ExitStack() as ctx:
        a_sb = ctx.enter_context(nc.sbuf_tensor("at_a", [1, T], F32))
        b_sb = ctx.enter_context(nc.sbuf_tensor("at_b", [1, T], F32))
        _emit_ln_stats(nc, D, S, ctx, scr_in, a_sb, b_sb, "at")

        wq_s = ctx.enter_context(nc.sbuf_tensor("at_wq", [128, NCH * C], BF16))
        wk_s = ctx.enter_context(nc.sbuf_tensor("at_wk", [128, NCH * C], BF16))
        wv_s = ctx.enter_context(nc.sbuf_tensor("at_wv", [128, NCH * C], BF16))
        wo_s = ctx.enter_context(nc.sbuf_tensor("at_wo", [128, NCH * C], BF16))
        bos = ctx.enter_context(nc.sbuf_tensor("at_bos", [1, C], F32))
        boT = ctx.enter_context(nc.sbuf_tensor("at_boT", [128, 8], F32))
        with ExitStack() as c2:
            ps_bo = c2.enter_context(nc.psum_tensor("at_psbo", [128, 8], F32))
            with nc.Block() as block:
                @block.sync
                def _(sync):
                    sync.wait_ge(S.h["cast"], S.marks["wo_bf"])
                    for (_, wname), buf in zip(ATTN_W, (wq_s, wk_s, wv_s, wo_s)):
                        for c in range(NCH):
                            S.inc("ld1", sync.dma_start(
                                out=buf[:, c * C:(c + 1) * C],
                                in_=D[wname][c * 128:(c + 1) * 128, :]), 16)
                    S.inc("ld1", sync.dma_start(out=bos[:, :], in_=D["bo"]), 16)

                @block.tensor
                def _(t):
                    t.wait_ge(S.h["ld1"], S.v["ld1"])
                    for j in range(8):
                        t.transpose(ps_bo[:, j:j + 1], bos[0:1, j * 128:(j + 1) * 128],
                                    ident[0:1, 0:1])
            with nc.Block() as block:
                @block.vector
                def _(v):
                    v.tensor_copy(boT[:, :], ps_bo[:, :])

        with ExitStack() as c2:
            xs = c2.enter_context(nc.sbuf_tensor("at_xs", [128, NCH * 512], F32))
            xn = c2.enter_context(nc.sbuf_tensor("at_xn", [128, NCH * 512], BF16))
            abx = c2.enter_context(nc.sbuf_tensor("at_abx", [128, 1024], F32))
            qt = c2.enter_context(nc.sbuf_tensor("at_qt", [128, NCH * 512], BF16))
            kt = c2.enter_context(nc.sbuf_tensor("at_kt", [128, NCH * 512], BF16))
            vt = c2.enter_context(nc.sbuf_tensor("at_vt", [128, 2 * 1024], BF16))
            tmpf = c2.enter_context(nc.sbuf_tensor("at_tmpf", [128, 512], F32))
            ps_ab = c2.enter_context(nc.psum_tensor("at_psab", [128, 1024], F32))
            psA = [c2.enter_context(nc.psum_tensor(f"at_psA{i}", [128, 512], F32)) for i in range(2)]
            psV = [c2.enter_context(nc.psum_tensor(f"at_psV{i}", [128, 1024], F32)) for i in range(2)]

            B = S.base("ld0", "tns", "vec", "scl", "sto")
            ld_done = lambda i: B["ld0"] + 128 * (i + 1)
            ab_mm = lambda i: B["tns"] + 21 * i + 1
            q_mm = lambda i, o: B["tns"] + 21 * i + 1 + (o + 1)
            k_mm = lambda i, o: B["tns"] + 21 * i + 9 + (o + 1)
            v_mm = lambda i, u: B["tns"] + 21 * i + 17 + (u + 1)
            qe = lambda i, o: B["scl"] + 16 * i + (o + 1)
            ke = lambda i, o: B["scl"] + 16 * i + 8 + (o + 1)
            ab_evac = lambda i: B["vec"] + 6 * i + 1
            apl = lambda i: B["vec"] + 6 * i + 2
            ve = lambda i, u: B["vec"] + 6 * i + 2 + (u + 1)
            sto_done = lambda i: B["sto"] + 320 * (i + 1)

            def xsv(c):
                return xs[:, c * 512:(c + 1) * 512]

            with nc.Block() as block:
                @block.sync
                def _(sync):
                    for c in range(NCH):
                        S.inc("ld0", sync.dma_start(
                            out=xsv(c),
                            in_=scr_in[c * 128:(c + 1) * 128, 0:512]), 16)
                    for i in range(NT):
                        if i + 1 < NT:
                            sync.wait_ge(S.h["vec"], apl(i))
                            for c in range(NCH):
                                S.inc("ld0", sync.dma_start(
                                    out=xsv(c),
                                    in_=scr_in[c * 128:(c + 1) * 128,
                                               (i + 1) * 512:(i + 2) * 512]), 16)
                        for o in range(NCH):
                            sync.wait_ge(S.h["scl"], qe(i, o))
                            S.inc("sto", sync.dma_start(
                                out=q_scr[o * 128:(o + 1) * 128, i * 512:(i + 1) * 512],
                                in_=qt[:, o * 512:(o + 1) * 512]), 16)
                        for o in range(NCH):
                            sync.wait_ge(S.h["scl"], ke(i, o))
                            S.inc("sto", sync.dma_start(
                                out=k_scr[o * 128:(o + 1) * 128, i * 512:(i + 1) * 512],
                                in_=kt[:, o * 512:(o + 1) * 512]), 16)
                        for u in range(4):
                            sync.wait_ge(S.h["vec"], ve(i, u))
                            S.inc("sto", sync.dma_start(
                                out=v_scr[i * 512 + u * 128: i * 512 + (u + 1) * 128, :],
                                in_=vt[:, (u % 2) * 1024:(u % 2 + 1) * 1024]), 16)
                    sync.wait_ge(S.h["sto"], sto_done(NT - 1))

                @block.tensor
                def _(t):
                    for i in range(NT):
                        if i >= 1:
                            t.wait_ge(S.h["vec"], ab_evac(i - 1))
                        t.matmul(ps_ab[:, 0:512], onescol[0:1, :], a_sb[:, i * 512:(i + 1) * 512],
                                 start=True, stop=True)
                        inst = t.matmul(ps_ab[:, 512:1024], onescol[0:1, :],
                                        b_sb[:, i * 512:(i + 1) * 512], start=True, stop=True)
                        S.inc("tns", inst, 1)
                        t.wait_ge(S.h["vec"], apl(i))
                        for o in range(NCH):
                            if o >= 2:
                                t.wait_ge(S.h["scl"], qe(i, o - 2))
                            elif i > 0:
                                t.wait_ge(S.h["scl"], ke(i - 1, o + 6))
                            for c in range(NCH):
                                inst = t.matmul(psA[o % 2][:, :],
                                                wq_s[:, c * C + o * 128: c * C + (o + 1) * 128],
                                                xn[:, c * 512:(c + 1) * 512],
                                                start=(c == 0), stop=(c == NCH - 1))
                            S.inc("tns", inst, 1)
                        for o in range(NCH):
                            t.wait_ge(S.h["scl"], qe(i, o + 6) if o < 2 else ke(i, o - 2))
                            for c in range(NCH):
                                inst = t.matmul(psA[o % 2][:, :],
                                                wk_s[:, c * C + o * 128: c * C + (o + 1) * 128],
                                                xn[:, c * 512:(c + 1) * 512],
                                                start=(c == 0), stop=(c == NCH - 1))
                            S.inc("tns", inst, 1)
                        for u in range(4):
                            if u >= 2:
                                t.wait_ge(S.h["vec"], ve(i, u - 2))
                            elif i > 0:
                                t.wait_ge(S.h["vec"], ve(i - 1, u + 2))
                            for c in range(NCH):
                                for nn in range(2):
                                    inst = t.matmul(
                                        psV[u % 2][:, nn * 512:(nn + 1) * 512],
                                        xn[:, c * 512 + u * 128: c * 512 + (u + 1) * 128],
                                        wv_s[:, c * C + nn * 512: c * C + (nn + 1) * 512],
                                        start=(c == 0), stop=(c == NCH - 1))
                            S.inc("tns", inst, 1)

                @block.scalar
                def _(s):
                    for i in range(NT):
                        for o in range(NCH):
                            s.wait_ge(S.h["tns"], q_mm(i, o))
                            if i > 0:
                                s.wait_ge(S.h["sto"], B["sto"] + 320 * i - 320 + (o + 1) * 16)
                            S.inc("scl", s.copy(qt[:, o * 512:(o + 1) * 512], psA[o % 2][:, :]), 1)
                        for o in range(NCH):
                            s.wait_ge(S.h["tns"], k_mm(i, o))
                            if i > 0:
                                s.wait_ge(S.h["sto"], B["sto"] + 320 * i - 192 + (o + 1) * 16)
                            S.inc("scl", s.copy(kt[:, o * 512:(o + 1) * 512], psA[o % 2][:, :]), 1)

                @block.vector
                def _(v):
                    for i in range(NT):
                        v.wait_ge(S.h["tns"], ab_mm(i))
                        v.tensor_copy(abx[:, 0:512], ps_ab[:, 0:512])
                        S.inc("vec", v.tensor_copy(abx[:, 512:1024], ps_ab[:, 512:1024]), 1)
                        v.wait_ge(S.h["ld0"], ld_done(i))
                        for c in range(NCH):
                            v.tensor_tensor(tmpf[:, :], xsv(c), abx[:, 0:512], ALU.mult)
                            inst = v.tensor_tensor(xn[:, c * 512:(c + 1) * 512], tmpf[:, :],
                                                   abx[:, 512:1024], ALU.add)
                        S.inc("vec", inst, 1)
                        for u in range(4):
                            v.wait_ge(S.h["tns"], v_mm(i, u))
                            if i > 0 or u >= 2:
                                pi, pu = (i, u - 2) if u >= 2 else (i - 1, u + 2)
                                v.wait_ge(S.h["sto"], B["sto"] + 320 * pi + 256 + (pu + 1) * 16)
                            S.inc("vec", v.tensor_copy(vt[:, (u % 2) * 1024:(u % 2 + 1) * 1024],
                                                       psV[u % 2][:, :]), 1)

            assert S.v["tns"] == B["tns"] + 21 * NT
            assert S.v["vec"] == B["vec"] + 6 * NT
            assert S.v["scl"] == B["scl"] + 16 * NT
            assert S.v["sto"] == B["sto"] + 320 * NT
        A_store_done = S.v["sto"]

        with ExitStack() as c2:
            qh = [c2.enter_context(nc.sbuf_tensor(f"at_qh{i}", [64, T], BF16)) for i in range(2)]
            kh = [c2.enter_context(nc.sbuf_tensor(f"at_kh{i}", [64, T], BF16)) for i in range(2)]
            vh = [c2.enter_context(nc.sbuf_tensor(f"at_vh{i}", [128, 32 * 65], BF16)) for i in range(2)]
            pb = [c2.enter_context(nc.sbuf_tensor(f"at_p{i}", [128, 1024], BF16)) for i in range(4)]
            attn = c2.enter_context(nc.sbuf_tensor("at_attn", [64, 1024], BF16))
            recs = c2.enter_context(nc.sbuf_tensor("at_recs", [1, 512], F32))
            recx = c2.enter_context(nc.sbuf_tensor("at_recx", [64, 512], F32))
            atok = c2.enter_context(nc.sbuf_tensor("at_atok", [128, 512], BF16))
            mask = c2.enter_context(nc.sbuf_tensor("at_mask", [128, 1920], F32))
            ps_sc = [c2.enter_context(nc.psum_tensor(f"at_sc{i}", [128, 1024], F32)) for i in range(2)]
            ps_at = [c2.enter_context(nc.psum_tensor(f"at_at{i}", [65, 512], F32)) for i in range(2)]
            ps_rec = c2.enter_context(nc.psum_tensor("at_rec", [64, 512], F32))
            ps_trb = c2.enter_context(nc.psum_tensor("at_tr", [128, 128], BF16))
            ps_tr = [ps_trb[:, 0:64], ps_trb[:, 64:128]]

            with nc.Block() as block:
                @block.gpsimd
                def _(g):
                    g.memset(mask[:, :], 1.0)
                    g.affine_select(out=mask[:, :], in_=mask[:, :], compare_op=ALU.is_ge,
                                    fill=0.0, base=-896, pattern=[[1, 1920]], channel_multiplier=-1)
                    g.memset(vh[0][:, :], 1.0)
                    g.memset(vh[1][:, :], 1.0)

            B = S.base("ld0", "tns", "vec", "scl", "sto")
            KV = [8 * (qb + 1) for qb in range(4)]

            sc_t = {}; pv_t = {}; exp_t = {}; msk_t = {}; rec_t = {}; rex_t = {}
            rcx_t = {}; attn_t = {}; trm_t = {}; tre_t = {}
            tv = B["tns"]; sv = B["scl"]; vv = B["vec"]
            for h in range(H):
                for qb in range(4):
                    nk = KV[qb]
                    for j in range(nk // 2):
                        tv += 1; sc_t[(h, qb, 2 * j)] = tv
                        tv += 1; sc_t[(h, qb, 2 * j + 1)] = tv
                        if j >= 1:
                            tv += 2; pv_t[(h, qb, 2 * j - 2)] = tv
                            tv += 2; pv_t[(h, qb, 2 * j - 1)] = tv
                    tv += 2; pv_t[(h, qb, nk - 2)] = tv
                    tv += 2; pv_t[(h, qb, nk - 1)] = tv
                    for nn in range(2):
                        tv += 1; rex_t[(h, qb, nn)] = tv
                    for j in range(8):
                        tv += 1; trm_t[(h, qb, j)] = tv
                    for kc in range(nk):
                        sv += 1; exp_t[(h, qb, kc)] = sv
                    for kc in range(qb * 8, nk):
                        vv += 1; msk_t[(h, qb, kc)] = vv
                    for nn in range(2):
                        vv += 1; rec_t[(h, qb, nn)] = vv
                        vv += 1; rcx_t[(h, qb, nn)] = vv
                        vv += 1; attn_t[(h, qb, nn)] = vv
                    for j in range(8):
                        vv += 1; tre_t[(h, qb, j)] = vv
            T_END, S_END, V_END = tv, sv, vv

            def prev_exp(h, qb, kc):
                idx = kc - 2
                while idx < 0:
                    if qb > 0:
                        qb -= 1; idx += KV[qb]
                    elif h > 0:
                        h -= 1; qb = 3; idx += KV[3]
                    else:
                        return None
                return exp_t[(h, qb, idx)]

            def prev_pv(h, qb, kc):
                idx = kc - 4
                while idx < 0:
                    if qb > 0:
                        qb -= 1; idx += KV[qb]
                    elif h > 0:
                        h -= 1; qb = 3; idx += KV[3]
                    else:
                        return None
                return pv_t[(h, qb, idx)]

            with nc.Block() as block:
                @block.sync
                def _(sync):
                    sync.wait_ge(S.h["sto"], A_store_done)

                    def head_loads(h):
                        S.inc("ld0", sync.dma_start(out=qh[h % 2][:, :],
                                                    in_=q_scr[h * 64:(h + 1) * 64, :]), 16)
                        S.inc("ld0", sync.dma_start(out=kh[h % 2][:, :],
                                                    in_=k_scr[h * 64:(h + 1) * 64, :]), 16)
                        S.inc("ld0", sync.dma_start(
                            out=vh[h % 2].rearrange("p (k w) -> p k w", w=65)[:, :, 0:64],
                            in_=v_scr[:, h * 64:(h + 1) * 64].rearrange("(k p) w -> p k w", p=128)), 16)

                    head_loads(0)
                    head_loads(1)
                    for h in range(H):
                        for qb in range(4):
                            sync.wait_ge(S.h["vec"], tre_t[(h, qb, 7)])
                            S.inc("sto", sync.dma_start(
                                out=a_scr[(h * T + qb * 1024):(h * T + (qb + 1) * 1024), :]
                                    .rearrange("(j p) w -> p j w", p=128),
                                in_=atok.rearrange("p (j w) -> p j w", w=64)), 16)
                        if h + 2 < H:
                            sync.wait_ge(S.h["tns"], pv_t[(h, 3, KV[3] - 1)])
                            head_loads(h + 2)
                    sync.wait_ge(S.h["sto"], S.v["sto"])

                @block.tensor
                def _(t):
                    for h in range(H):
                        t.wait_ge(S.h["ld0"], B["ld0"] + 48 * (h + 1))
                        for qb in range(4):
                            nk = KV[qb]

                            def emit_sc(kc):
                                pe = prev_exp(h, qb, kc)
                                if pe is not None:
                                    t.wait_ge(S.h["scl"], pe)
                                for nn in range(2):
                                    inst = t.matmul(
                                        ps_sc[kc % 2][:, nn * 512:(nn + 1) * 512],
                                        kh[h % 2][:, kc * 128:(kc + 1) * 128],
                                        qh[h % 2][:, qb * 1024 + nn * 512: qb * 1024 + (nn + 1) * 512],
                                        start=True, stop=True)
                                S.inc("tns", inst, 1)

                            def emit_pv(pkc):
                                t.wait_ge(S.h["scl"], exp_t[(h, qb, pkc)])
                                if pkc >= qb * 8:
                                    t.wait_ge(S.h["vec"], msk_t[(h, qb, pkc)])
                                for nn in range(2):
                                    inst = t.matmul(ps_at[nn][:, :],
                                                    vh[h % 2][:, pkc * 65:(pkc + 1) * 65],
                                                    pb[pkc % 4][:, nn * 512:(nn + 1) * 512],
                                                    start=(pkc == 0), stop=(pkc == nk - 1))
                                    S.inc("tns", inst, 1)

                            for j in range(nk // 2):
                                emit_sc(2 * j)
                                emit_sc(2 * j + 1)
                                if j >= 1:
                                    emit_pv(2 * j - 2)
                                    emit_pv(2 * j - 1)
                            emit_pv(nk - 2)
                            emit_pv(nk - 1)
                            for nn in range(2):
                                t.wait_ge(S.h["vec"], rec_t[(h, qb, nn)])
                                if nn == 1:
                                    t.wait_ge(S.h["vec"], rcx_t[(h, qb, 0)])
                                S.inc("tns", t.matmul(ps_rec[:, :], onescol[0:1, 0:64],
                                                      recs[:, :], start=True, stop=True), 1)
                            t.wait_ge(S.h["vec"], attn_t[(h, qb, 1)])
                            for j in range(8):
                                if j >= 2:
                                    t.wait_ge(S.h["vec"], tre_t[(h, qb, j - 2)])
                                elif (h, qb) != (0, 0):
                                    ph, pqb = (h, qb - 1) if qb > 0 else (h - 1, 3)
                                    t.wait_ge(S.h["vec"], tre_t[(ph, pqb, j + 6)])
                                S.inc("tns", t.transpose(ps_tr[j % 2],
                                                         attn[:, j * 128:(j + 1) * 128],
                                                         ident_bf[0:64, 0:64]), 1)

                @block.scalar
                def _(s):
                    for h in range(H):
                        for qb in range(4):
                            nk = KV[qb]
                            for kc in range(nk):
                                s.wait_ge(S.h["tns"], sc_t[(h, qb, kc)])
                                ppv = prev_pv(h, qb, kc)
                                if ppv is not None:
                                    s.wait_ge(S.h["tns"], ppv)
                                S.inc("scl", s.activation(pb[kc % 4][:, :], ps_sc[kc % 2][:, :],
                                                          AF.Exp, scale=0.125), 1)

                @block.vector
                def _(v):
                    for h in range(H):
                        for qb in range(4):
                            nk = KV[qb]
                            for kc in range(qb * 8, nk):
                                v.wait_ge(S.h["scl"], exp_t[(h, qb, kc)])
                                dd = kc * 128 - qb * 1024
                                S.inc("vec", v.tensor_tensor(
                                    pb[kc % 4][:, :], pb[kc % 4][:, :],
                                    mask[:, 896 - dd:896 - dd + 1024], ALU.mult), 1)
                            for nn in range(2):
                                v.wait_ge(S.h["tns"], pv_t[(h, qb, nk - 1)])
                                S.inc("vec", v.reciprocal(recs[:, :], ps_at[nn][64:65, :]), 1)
                                v.wait_ge(S.h["tns"], rex_t[(h, qb, nn)])
                                S.inc("vec", v.tensor_copy(recx[:, :], ps_rec[:, :]), 1)
                                S.inc("vec", v.tensor_tensor(attn[:, nn * 512:(nn + 1) * 512],
                                                             ps_at[nn][0:64, :], recx[:, :],
                                                             ALU.mult), 1)
                            for j in range(8):
                                v.wait_ge(S.h["tns"], trm_t[(h, qb, j)])
                                if (h, qb) != (0, 0) and j == 0:
                                    n_prev = h * 4 + qb
                                    v.wait_ge(S.h["sto"], B["sto"] + n_prev * 16)
                                S.inc("vec", v.tensor_copy(atok[:, j * 64:(j + 1) * 64],
                                                           ps_tr[j % 2]), 1)

            assert S.v["tns"] == T_END, (S.v["tns"], T_END)
            assert S.v["scl"] == S_END
            assert S.v["vec"] == V_END
        B_store_done = S.v["sto"]

        with ExitStack() as c2:
            xs2 = [c2.enter_context(nc.sbuf_tensor(f"atc_xs{j}", [128, NCH * 512], F32))
                   for j in range(2)]
            at2 = [c2.enter_context(nc.sbuf_tensor(f"atc_at{j}", [128, NCH * 512], BF16))
                   for j in range(2)]
            psA = [c2.enter_context(nc.psum_tensor(f"atc_ps{i}", [128, 512], F32)) for i in range(2)]
            a_view = a_scr.rearrange("(s g) w -> s (g w)", g=16)

            B = S.base("ld0", "tns", "vec", "sto")
            ld_done = lambda i: B["ld0"] + 256 * (i + 1)
            wo_mm = lambda i, o: B["tns"] + 8 * i + (o + 1)
            upd = lambda i, o: B["vec"] + 8 * i + (o + 1)
            sto_done = lambda i: B["sto"] + 128 * (i + 1)

            with nc.Block() as block:
                @block.sync
                def _(sync):
                    sync.wait_ge(S.h["sto"], B_store_done)
                    for i in range(NT):
                        if i == 0:
                            todo = [0, 1]
                        elif i + 1 < NT:
                            todo = [i + 1]
                        else:
                            todo = []
                        for j in todo:
                            if j >= 2:
                                sync.wait_ge(S.h["sto"], sto_done(j - 2))
                                sync.wait_ge(S.h["tns"], wo_mm(j - 2, NCH - 1))
                            for c in range(NCH):
                                S.inc("ld0", sync.dma_start(
                                    out=at2[j % 2][:, c * 512:(c + 1) * 512],
                                    in_=a_view[j * 512:(j + 1) * 512, c * 128:(c + 1) * 128],
                                    transpose=True), 16)
                            for c in range(NCH):
                                S.inc("ld0", sync.dma_start(
                                    out=xs2[j % 2][:, c * 512:(c + 1) * 512],
                                    in_=scr_in[c * 128:(c + 1) * 128, j * 512:(j + 1) * 512]), 16)
                        for o in range(NCH):
                            sync.wait_ge(S.h["vec"], upd(i, o))
                            S.inc("sto", sync.dma_start(
                                out=scr_out[o * 128:(o + 1) * 128, i * 512:(i + 1) * 512],
                                in_=xs2[i % 2][:, o * 512:(o + 1) * 512]), 16)
                    sync.wait_ge(S.h["sto"], sto_done(NT - 1))

                @block.tensor
                def _(t):
                    for i in range(NT):
                        t.wait_ge(S.h["ld0"], B["ld0"] + 256 * (i + 1))
                        for o in range(NCH):
                            if o >= 2:
                                t.wait_ge(S.h["vec"], upd(i, o - 2))
                            elif i > 0:
                                t.wait_ge(S.h["vec"], upd(i - 1, o + 6))
                            for c in range(NCH):
                                inst = t.matmul(psA[o % 2][:, :],
                                                wo_s[:, c * C + o * 128: c * C + (o + 1) * 128],
                                                at2[i % 2][:, c * 512:(c + 1) * 512],
                                                start=(c == 0), stop=(c == NCH - 1))
                            S.inc("tns", inst, 1)

                @block.vector
                def _(v):
                    for i in range(NT):
                        for o in range(NCH):
                            v.wait_ge(S.h["tns"], wo_mm(i, o))
                            S.inc("vec", v.scalar_tensor_tensor(
                                xs2[i % 2][:, o * 512:(o + 1) * 512], psA[o % 2][:, :],
                                boT[:, o:o + 1], xs2[i % 2][:, o * 512:(o + 1) * 512],
                                ALU.add, ALU.add), 1)

            assert S.v["tns"] == B["tns"] + 8 * NT
            assert S.v["vec"] == B["vec"] + 8 * NT




def emit_prep_l1attn(nc, D, S):
    with nc.Block() as block:
        @block.gpsimd
        def _(g):
            S.marks["wqkv_bf"] = S.inc("cast", g.dma_start(out=D["wqkv_bf"], in_=D["WQKV"]), 16)
            S.marks["wproj_bf"] = S.inc("cast", g.dma_start(out=D["wproj_bf"], in_=D["WPROJ"]), 16)


def emit_l1attn(nc, D, S):
    scr_in, scr_out = D["scr_in"], D["scr_out"]
    ident, onescol, ones_bf = D["cm_ident"], D["cm_onescol"], D["cm_ones_bf"]

    with ExitStack() as ctx:
        a_sb = ctx.enter_context(nc.sbuf_tensor("la_a", [1, T], F32))
        b_sb = ctx.enter_context(nc.sbuf_tensor("la_b", [1, T], F32))
        _emit_ln_stats(nc, D, S, ctx, scr_in, a_sb, b_sb, "la")

        wqkvT = ctx.enter_context(nc.sbuf_tensor("la_wqkv", [128, NCH * 3072], BF16))
        wprojT = ctx.enter_context(nc.sbuf_tensor("la_wproj", [128, NCH * C], BF16))
        bps = ctx.enter_context(nc.sbuf_tensor("la_bps", [1, C], F32))
        bpT = ctx.enter_context(nc.sbuf_tensor("la_bpT", [128, 8], F32))
        ones2 = ctx.enter_context(nc.sbuf_tensor("la_ones2", [128, 2], BF16))
        ind2 = ctx.enter_context(nc.sbuf_tensor("la_ind2", [2, 128], BF16))
        ctx_s = ctx.enter_context(nc.sbuf_tensor("la_ctx", [128, 512], BF16))
        S_sb = ctx.enter_context(nc.sbuf_tensor("la_S", [1, 1024], F32))
        srecT = ctx.enter_context(nc.sbuf_tensor("la_srT", [128, 8], F32))

        with ExitStack() as c2:
            ps_bp = c2.enter_context(nc.psum_tensor("la_psbp", [128, 8], F32))
            with nc.Block() as block:
                @block.sync
                def _(sync):
                    sync.wait_ge(S.h["cast"], S.marks["wqkv_bf"])
                    for c in range(NCH):
                        S.inc("ld1", sync.dma_start(out=wqkvT[:, c * 3072:(c + 1) * 3072],
                                                    in_=D["wqkv_bf"][:, c * 128:(c + 1) * 128],
                                                    transpose=True), 16)
                    sync.wait_ge(S.h["cast"], S.marks["wproj_bf"])
                    for c in range(NCH):
                        S.inc("ld1", sync.dma_start(out=wprojT[:, c * C:(c + 1) * C],
                                                    in_=D["wproj_bf"][:, c * 128:(c + 1) * 128],
                                                    transpose=True), 16)
                    S.inc("ld1", sync.dma_start(out=bps[:, :], in_=D["b_proj"]), 16)

                @block.gpsimd
                def _(g):
                    g.memset(ones2[:, :], 0.0)
                    g.memset(ones2[0:64, 0:1], 1.0)
                    g.memset(ones2[64:128, 1:2], 1.0)
                    g.memset(ind2[:, :], 1.0)
                    g.affine_select(out=ind2[:, :], in_=ind2[:, :], compare_op=ALU.is_ge,
                                    fill=0.0, base=0, pattern=[[1, 128]], channel_multiplier=-64)
                    g.affine_select(out=ind2[:, :], in_=ind2[:, :], compare_op=ALU.is_ge,
                                    fill=0.0, base=63, pattern=[[-1, 128]], channel_multiplier=64)
                    g.memset(S_sb[:, :], 0.0)

                @block.tensor
                def _(t):
                    t.wait_ge(S.h["ld1"], S.v["ld1"])
                    for j in range(8):
                        t.transpose(ps_bp[:, j:j + 1], bps[0:1, j * 128:(j + 1) * 128],
                                    ident[0:1, 0:1])
            with nc.Block() as block:
                @block.vector
                def _(v):
                    v.tensor_copy(bpT[:, :], ps_bp[:, :])

        with ExitStack() as c2:
            xs = c2.enter_context(nc.sbuf_tensor("la_xs", [128, NCH * 512], F32))
            xn = c2.enter_context(nc.sbuf_tensor("la_xn", [128, NCH * 512], BF16))
            abx = c2.enter_context(nc.sbuf_tensor("la_abx", [128, 1024], F32))
            tmpf = c2.enter_context(nc.sbuf_tensor("la_tmpf", [128, 512], F32))
            ke_s = [c2.enter_context(nc.sbuf_tensor(f"la_ke{i}", [128, 1024], BF16)) for i in range(2)]
            ve_s = [c2.enter_context(nc.sbuf_tensor(f"la_ve{i}", [128, 1024], BF16)) for i in range(2)]
            ps_ab = c2.enter_context(nc.psum_tensor("la_psab", [128, 1024], F32))
            psK = c2.enter_context(nc.psum_tensor("la_psK", [128, 1024], F32))
            psV = c2.enter_context(nc.psum_tensor("la_psV", [128, 1024], F32))
            ps_one = c2.enter_context(nc.psum_tensor("la_pso", [1, 512], F32))
            ps_ctx = c2.enter_context(nc.psum_tensor("la_psctx", [128, 512], F32))

            B = S.base("ld0", "tns", "vec", "scl", "sto")
            ld_done = lambda i: B["ld0"] + 128 * (i + 1)
            ab_mm = lambda i: B["tns"] + 21 * i + 1
            k_mm = lambda i, u: B["tns"] + 21 * i + 1 + 5 * u + 1
            v_mm = lambda i, u: B["tns"] + 21 * i + 1 + 5 * u + 2
            one_mm = lambda i, u, nh: B["tns"] + 21 * i + 1 + 5 * u + 3 + nh
            ctx_mm = lambda i, u: B["tns"] + 21 * i + 1 + 5 * u + 5
            kexp = lambda i, u: B["scl"] + 8 * i + 2 * u + 1
            vevac = lambda i, u: B["scl"] + 8 * i + 2 * u + 2
            ab_evac = lambda i: B["vec"] + 10 * i + 1
            apl = lambda i: B["vec"] + 10 * i + 2
            sadd = lambda i, u, nh: B["vec"] + 10 * i + 2 + 2 * u + 1 + nh

            with nc.Block() as block:
                @block.sync
                def _(sync):
                    for c in range(NCH):
                        S.inc("ld0", sync.dma_start(
                            out=xs[:, c * 512:(c + 1) * 512],
                            in_=scr_in[c * 128:(c + 1) * 128, 0:512]), 16)
                    for i in range(NT - 1):
                        sync.wait_ge(S.h["vec"], apl(i))
                        for c in range(NCH):
                            S.inc("ld0", sync.dma_start(
                                out=xs[:, c * 512:(c + 1) * 512],
                                in_=scr_in[c * 128:(c + 1) * 128, (i + 1) * 512:(i + 2) * 512]), 16)

                @block.tensor
                def _(t):
                    for i in range(NT):
                        if i >= 1:
                            t.wait_ge(S.h["vec"], ab_evac(i - 1))
                        t.matmul(ps_ab[:, 0:512], onescol[0:1, :], a_sb[:, i * 512:(i + 1) * 512],
                                 start=True, stop=True)
                        inst = t.matmul(ps_ab[:, 512:1024], onescol[0:1, :],
                                        b_sb[:, i * 512:(i + 1) * 512], start=True, stop=True)
                        S.inc("tns", inst, 1)
                        t.wait_ge(S.h["vec"], apl(i))
                        for u in range(4):
                            if i > 0 or u > 0:
                                pi, pu = (i, u - 1) if u > 0 else (i - 1, 3)
                                t.wait_ge(S.h["scl"], kexp(pi, pu))
                            for c in range(NCH):
                                for nn in range(2):
                                    inst = t.matmul(
                                        psK[:, nn * 512:(nn + 1) * 512],
                                        xn[:, c * 512 + u * 128: c * 512 + (u + 1) * 128],
                                        wqkvT[:, c * 3072 + 1024 + nn * 512: c * 3072 + 1024 + (nn + 1) * 512],
                                        start=(c == 0), stop=(c == NCH - 1))
                            S.inc("tns", inst, 1)
                            if i > 0 or u > 0:
                                pi, pu = (i, u - 1) if u > 0 else (i - 1, 3)
                                t.wait_ge(S.h["scl"], vevac(pi, pu))
                            for c in range(NCH):
                                for nn in range(2):
                                    inst = t.matmul(
                                        psV[:, nn * 512:(nn + 1) * 512],
                                        xn[:, c * 512 + u * 128: c * 512 + (u + 1) * 128],
                                        wqkvT[:, c * 3072 + 2048 + nn * 512: c * 3072 + 2048 + (nn + 1) * 512],
                                        start=(c == 0), stop=(c == NCH - 1))
                            S.inc("tns", inst, 1)
                            t.wait_ge(S.h["scl"], kexp(i, u))
                            for nh in range(2):
                                if i > 0 or u > 0 or nh > 0:
                                    if nh > 0:
                                        t.wait_ge(S.h["vec"], sadd(i, u, 0))
                                    else:
                                        pi, pu = (i, u - 1) if u > 0 else (i - 1, 3)
                                        t.wait_ge(S.h["vec"], sadd(pi, pu, 1))
                                S.inc("tns", t.matmul(ps_one[:, :], ones_bf[:, :],
                                                      ke_s[u % 2][:, nh * 512:(nh + 1) * 512],
                                                      start=True, stop=True), 1)
                            t.wait_ge(S.h["scl"], vevac(i, u))
                            first = (i == 0 and u == 0)
                            last = (i == NT - 1 and u == 3)
                            for hh in range(H):
                                inst = t.matmul(
                                    ps_ctx[(hh % 2) * 64:(hh % 2) * 64 + 64,
                                           (hh // 2) * 64:(hh // 2) * 64 + 64],
                                    ke_s[u % 2][:, hh * 64:(hh + 1) * 64],
                                    ve_s[u % 2][:, hh * 64:(hh + 1) * 64],
                                    start=(first and hh == 0), stop=(last and hh == H - 1),
                                    skip_group_check=True)
                            S.inc("tns", inst, 1)

                @block.vector
                def _(v):
                    for i in range(NT):
                        v.wait_ge(S.h["tns"], ab_mm(i))
                        v.tensor_copy(abx[:, 0:512], ps_ab[:, 0:512])
                        S.inc("vec", v.tensor_copy(abx[:, 512:1024], ps_ab[:, 512:1024]), 1)
                        v.wait_ge(S.h["ld0"], ld_done(i))
                        for c in range(NCH):
                            v.tensor_tensor(tmpf[:, :], xs[:, c * 512:(c + 1) * 512],
                                            abx[:, 0:512], ALU.mult)
                            inst = v.tensor_tensor(xn[:, c * 512:(c + 1) * 512], tmpf[:, :],
                                                   abx[:, 512:1024], ALU.add)
                        S.inc("vec", inst, 1)
                        for u in range(4):
                            for nh in range(2):
                                v.wait_ge(S.h["tns"], one_mm(i, u, nh))
                                S.inc("vec", v.tensor_tensor(
                                    S_sb[:, nh * 512:(nh + 1) * 512],
                                    S_sb[:, nh * 512:(nh + 1) * 512],
                                    ps_one[:, :], ALU.add), 1)

                @block.scalar
                def _(s):
                    for i in range(NT):
                        for u in range(4):
                            s.wait_ge(S.h["tns"], k_mm(i, u))
                            if i > 0 or u >= 2:
                                pi, pu = (i, u - 2) if u >= 2 else (i - 1, u + 2)
                                s.wait_ge(S.h["tns"], ctx_mm(pi, pu))
                                s.wait_ge(S.h["tns"], one_mm(pi, pu, 1))
                            S.inc("scl", s.activation(ke_s[u % 2][:, :], psK[:, :], AF.Exp), 1)
                            s.wait_ge(S.h["tns"], v_mm(i, u))
                            S.inc("scl", s.copy(ve_s[u % 2][:, :], psV[:, :]), 1)

            assert S.v["tns"] == B["tns"] + 21 * NT
            assert S.v["vec"] == B["vec"] + 10 * NT
            assert S.v["scl"] == B["scl"] + 8 * NT

            with nc.Block() as block:
                @block.tensor
                def _(t):
                    for j in range(8):
                        t.transpose(ps_ab[:, j:j + 1], S_sb[0:1, j * 128:(j + 1) * 128],
                                    ident[0:1, 0:1])
            with nc.Block() as block:
                @block.vector
                def _(v):
                    v.reciprocal(srecT[:, :], ps_ab[:, 0:8])
            with nc.Block() as block:
                @block.vector
                def _(v):
                    for j in range(8):
                        v.tensor_scalar(ctx_s[:, j * 64:(j + 1) * 64],
                                        ps_ctx[:, j * 64:(j + 1) * 64],
                                        srecT[:, j:j + 1], None, ALU.mult)

        with ExitStack() as c2:
            xs2b = [c2.enter_context(nc.sbuf_tensor(f"lb_xs{j}", [128, NCH * 512], F32))
                    for j in range(2)]
            xn = c2.enter_context(nc.sbuf_tensor("lb_xn", [128, NCH * 512], BF16))
            abx = c2.enter_context(nc.sbuf_tensor("lb_abx", [128, 1024], F32))
            tmpf = c2.enter_context(nc.sbuf_tensor("lb_tmpf", [128, 512], F32))
            qe_s = c2.enter_context(nc.sbuf_tensor("lb_qe", [128, NCH * 512], BF16))
            an_s = c2.enter_context(nc.sbuf_tensor("lb_an", [128, NCH * 512], BF16))
            qS_sb = c2.enter_context(nc.sbuf_tensor("lb_qS", [2, T], F32))
            rc_bf = c2.enter_context(nc.sbuf_tensor("lb_rc", [2, T], BF16))
            ps_ab = c2.enter_context(nc.psum_tensor("lb_psab", [128, 1024], F32))
            P0 = c2.enter_context(nc.psum_tensor("lb_P0", [128, 512], F32))
            P1 = c2.enter_context(nc.psum_tensor("lb_P1", [128, 512], F32))
            PP = [P0, P1]
            ps_qS = c2.enter_context(nc.psum_tensor("lb_qSp", [2, 512], F32))
            ps_ex = c2.enter_context(nc.psum_tensor("lb_exp", [128, 512], F32))
            PA = [c2.enter_context(nc.psum_tensor(f"lb_PA{i}", [128, 512], F32)) for i in range(2)]

            B = S.base("ld0", "tns", "vec", "scl", "sto")
            ld_done = lambda i: B["ld0"] + 128 * (i + 1)
            ab_mm = lambda i: B["tns"] + 41 * i + 1
            q_mm = lambda i, o: B["tns"] + 41 * i + 1 + (o + 1)
            qs_mm = lambda i, o: B["tns"] + 41 * i + 9 + (o + 1)
            ex_mm = lambda i, o: B["tns"] + 41 * i + 17 + (o + 1)
            at_mm = lambda i, o: B["tns"] + 41 * i + 25 + (o + 1)
            pj_mm = lambda i, o: B["tns"] + 41 * i + 33 + (o + 1)
            exp2 = lambda i, o: B["scl"] + 8 * i + (o + 1)
            ab_evac = lambda i: B["vec"] + 35 * i + 1
            apl = lambda i: B["vec"] + 35 * i + 2
            qsev = lambda i, o: B["vec"] + 35 * i + 2 + (o + 1)
            rcp = lambda i: B["vec"] + 35 * i + 11
            nrm = lambda i, o: B["vec"] + 35 * i + 11 + (o + 1)
            aev = lambda i, o: B["vec"] + 35 * i + 19 + (o + 1)
            upd = lambda i, o: B["vec"] + 35 * i + 27 + (o + 1)
            sto_done = lambda i: B["sto"] + 128 * (i + 1)

            with nc.Block() as block:
                @block.sync
                def _(sync):
                    for i in range(NT):
                        if i == 0:
                            todo = [0, 1]
                        elif i + 1 < NT:
                            todo = [i + 1]
                        else:
                            todo = []
                        for j in todo:
                            if j >= 2:
                                sync.wait_ge(S.h["sto"], sto_done(j - 2))
                            for c in range(NCH):
                                S.inc("ld0", sync.dma_start(
                                    out=xs2b[j % 2][:, c * 512:(c + 1) * 512],
                                    in_=scr_in[c * 128:(c + 1) * 128, j * 512:(j + 1) * 512]), 16)
                        for o in range(NCH):
                            sync.wait_ge(S.h["vec"], upd(i, o))
                            S.inc("sto", sync.dma_start(
                                out=scr_out[o * 128:(o + 1) * 128, i * 512:(i + 1) * 512],
                                in_=xs2b[i % 2][:, o * 512:(o + 1) * 512]), 16)
                    sync.wait_ge(S.h["sto"], sto_done(NT - 1))

                @block.tensor
                def _(t):
                    for i in range(NT):
                        if i >= 1:
                            t.wait_ge(S.h["vec"], ab_evac(i - 1))
                        t.matmul(ps_ab[:, 0:512], onescol[0:1, :], a_sb[:, i * 512:(i + 1) * 512],
                                 start=True, stop=True)
                        inst = t.matmul(ps_ab[:, 512:1024], onescol[0:1, :],
                                        b_sb[:, i * 512:(i + 1) * 512], start=True, stop=True)
                        S.inc("tns", inst, 1)
                        t.wait_ge(S.h["vec"], apl(i))
                        for o in range(NCH):
                            if o >= 2:
                                t.wait_ge(S.h["scl"], exp2(i, o - 2))
                            elif i > 0:
                                t.wait_ge(S.h["vec"], upd(i - 1, o + 6))
                            for c in range(NCH):
                                inst = t.matmul(PP[o % 2][:, :],
                                                wqkvT[:, c * 3072 + o * 128: c * 3072 + (o + 1) * 128],
                                                xn[:, c * 512:(c + 1) * 512],
                                                start=(c == 0), stop=(c == NCH - 1))
                            S.inc("tns", inst, 1)
                        for o in range(NCH):
                            t.wait_ge(S.h["scl"], exp2(i, o))
                            if o >= 1:
                                t.wait_ge(S.h["vec"], qsev(i, o - 1))
                            elif i > 0:
                                t.wait_ge(S.h["vec"], qsev(i - 1, 7))
                            S.inc("tns", t.matmul(ps_qS[:, :], ones2[:, :],
                                                  qe_s[:, o * 512:(o + 1) * 512],
                                                  start=True, stop=True), 1)
                        for o in range(NCH):
                            t.wait_ge(S.h["vec"], rcp(i))
                            if o >= 1:
                                t.wait_ge(S.h["vec"], nrm(i, o - 1))
                            S.inc("tns", t.matmul(ps_ex[:, :], ind2[:, :],
                                                  rc_bf[:, o * 512:(o + 1) * 512],
                                                  start=True, stop=True), 1)
                        for o in range(NCH):
                            t.wait_ge(S.h["vec"], nrm(i, o))
                            if o >= 2:
                                t.wait_ge(S.h["vec"], aev(i, o - 2))
                            elif i > 0:
                                t.wait_ge(S.h["vec"], aev(i - 1, o + 6))
                            for ss in range(2):
                                inst = t.matmul(
                                    PA[o % 2][ss * 64:(ss + 1) * 64, :],
                                    ctx_s[ss * 64:(ss + 1) * 64, o * 64:(o + 1) * 64],
                                    qe_s[ss * 64:(ss + 1) * 64, o * 512:(o + 1) * 512],
                                    start=True, stop=True, skip_group_check=True)
                            S.inc("tns", inst, 1)
                        for o in range(NCH):
                            t.wait_ge(S.h["vec"], aev(i, 7))
                            if o < 2:
                                t.wait_ge(S.h["scl"], exp2(i, o + 6))
                            else:
                                t.wait_ge(S.h["vec"], upd(i, o - 2))
                            for c in range(NCH):
                                inst = t.matmul(PP[o % 2][:, :],
                                                wprojT[:, c * C + o * 128: c * C + (o + 1) * 128],
                                                an_s[:, c * 512:(c + 1) * 512],
                                                start=(c == 0), stop=(c == NCH - 1))
                            S.inc("tns", inst, 1)

                @block.scalar
                def _(s):
                    for i in range(NT):
                        for o in range(NCH):
                            s.wait_ge(S.h["tns"], q_mm(i, o))
                            if i > 0:
                                s.wait_ge(S.h["tns"], at_mm(i - 1, o))
                            S.inc("scl", s.activation(qe_s[:, o * 512:(o + 1) * 512],
                                                      PP[o % 2][:, :], AF.Exp), 1)

                @block.vector
                def _(v):
                    for i in range(NT):
                        v.wait_ge(S.h["tns"], ab_mm(i))
                        v.tensor_copy(abx[:, 0:512], ps_ab[:, 0:512])
                        S.inc("vec", v.tensor_copy(abx[:, 512:1024], ps_ab[:, 512:1024]), 1)
                        v.wait_ge(S.h["ld0"], ld_done(i))
                        for c in range(NCH):
                            v.tensor_tensor(tmpf[:, :], xs2b[i % 2][:, c * 512:(c + 1) * 512],
                                            abx[:, 0:512], ALU.mult)
                            inst = v.tensor_tensor(xn[:, c * 512:(c + 1) * 512], tmpf[:, :],
                                                   abx[:, 512:1024], ALU.add)
                        S.inc("vec", inst, 1)
                        for o in range(NCH):
                            v.wait_ge(S.h["tns"], qs_mm(i, o))
                            S.inc("vec", v.tensor_copy(qS_sb[:, o * 512:(o + 1) * 512],
                                                       ps_qS[:, :]), 1)
                        if i > 0:
                            v.wait_ge(S.h["tns"], ex_mm(i - 1, 7))
                        v.reciprocal(qS_sb[:, :], qS_sb[:, :])
                        S.inc("vec", v.tensor_copy(rc_bf[:, :], qS_sb[:, :]), 1)
                        for o in range(NCH):
                            v.wait_ge(S.h["tns"], ex_mm(i, o))
                            S.inc("vec", v.tensor_tensor(qe_s[:, o * 512:(o + 1) * 512],
                                                         qe_s[:, o * 512:(o + 1) * 512],
                                                         ps_ex[:, :], ALU.mult), 1)
                        for o in range(NCH):
                            v.wait_ge(S.h["tns"], at_mm(i, o))
                            S.inc("vec", v.tensor_copy(an_s[:, o * 512:(o + 1) * 512],
                                                       PA[o % 2][:, :]), 1)
                        for o in range(NCH):
                            v.wait_ge(S.h["tns"], pj_mm(i, o))
                            S.inc("vec", v.scalar_tensor_tensor(
                                xs2b[i % 2][:, o * 512:(o + 1) * 512], PP[o % 2][:, :],
                                bpT[:, o:o + 1], xs2b[i % 2][:, o * 512:(o + 1) * 512],
                                ALU.add, ALU.add), 1)

            assert S.v["tns"] == B["tns"] + 41 * NT
            assert S.v["vec"] == B["vec"] + 35 * NT
            assert S.v["scl"] == B["scl"] + 8 * NT





def emit_prep_l1ffn(nc, D, S):
    with nc.Block() as block:
        @block.gpsimd
        def _(g):
            S.marks["w1_bf"] = S.inc("cast", g.dma_start(out=D["w1_bf"], in_=D["W1"]), 16)
            S.marks["w2_bf"] = S.inc("cast", g.dma_start(out=D["w2_bf"], in_=D["W2"]), 16)
            S.marks["w3_bf"] = S.inc("cast", g.dma_start(out=D["w3_bf"], in_=D["W3"]), 16)


def emit_moe(nc, D, S, tag):
    scr_in, scr_out = D["scr_in"], D["scr_out"]
    ident, onescol = D["cm_ident"], D["cm_onescol"]
    onescol_bf = D["cm_onescol_bf"]
    NTT = 4
    TT = 1024

    with ExitStack() as ctx:
        ab_bf = ctx.enter_context(nc.sbuf_tensor(f"{tag}_abbf", [128, T], BF16))
        a_bf = ab_bf[0:1, :]
        b_bf = ab_bf[64:65, :]
        abT = ctx.enter_context(nc.sbuf_tensor(f"{tag}_abT", [128, 64], F32))
        WgT_s = ctx.enter_context(nc.sbuf_tensor(f"{tag}_wgT", [128, 64], F32))
        W0 = ctx.enter_context(nc.sbuf_tensor(f"{tag}_W0", [128, 8], F32))
        ind8 = ctx.enter_context(nc.sbuf_tensor(f"{tag}_ind8", [8, 1024], BF16))

        with ExitStack() as c2:
            a_sb = c2.enter_context(nc.sbuf_tensor(f"{tag}_a32", [1, T], F32))
            b_sb = c2.enter_context(nc.sbuf_tensor(f"{tag}_b32", [1, T], F32))
            _emit_ln_stats(nc, D, S, c2, scr_in, a_sb, b_sb, tag)
            ps_abT = c2.enter_context(nc.psum_tensor(f"{tag}_psabT", [128, 64], F32))
            with nc.Block() as block:
                @block.scalar
                def _(s):
                    s.copy(a_bf, a_sb[:, :])
                    s.copy(b_bf, b_sb[:, :])

                @block.tensor
                def _(t):
                    for tc in range(32):
                        t.transpose(ps_abT[:, tc:tc + 1],
                                    a_sb[0:1, tc * 128:(tc + 1) * 128], ident[0:1, 0:1])
                        t.transpose(ps_abT[:, 32 + tc:33 + tc],
                                    b_sb[0:1, tc * 128:(tc + 1) * 128], ident[0:1, 0:1])
            with nc.Block() as block:
                @block.vector
                def _(v):
                    v.tensor_copy(abT[:, :], ps_abT[:, :])

        with ExitStack() as c2:
            wg_sb = c2.enter_context(nc.sbuf_tensor(f"{tag}_wg", [8, 1024], F32))
            wgtmp = c2.enter_context(nc.sbuf_tensor(f"{tag}_wgt", [8, 1024], F32))
            wgs = c2.enter_context(nc.sbuf_tensor(f"{tag}_wgs", [8, 1], F32))
            wgsr = c2.enter_context(nc.sbuf_tensor(f"{tag}_wgsr", [1, 8], F32))
            ps_g = c2.enter_context(nc.psum_tensor(f"{tag}_psg", [128, 80], F32))
            B0 = S.base("ld1")
            with nc.Block() as block:
                @block.sync
                def _(sync):
                    S.inc("ld1", sync.dma_start(out=wg_sb[:, :], in_=D["Wg"]), 16)

                @block.gpsimd
                def _(g):
                    g.memset(ind8[:, :], 1.0)
                    g.affine_select(out=ind8[:, :], in_=ind8[:, :], compare_op=ALU.is_ge,
                                    fill=0.0, base=0, pattern=[[1, 1024]], channel_multiplier=-128)
                    g.affine_select(out=ind8[:, :], in_=ind8[:, :], compare_op=ALU.is_ge,
                                    fill=0.0, base=127, pattern=[[-1, 1024]], channel_multiplier=128)

                @block.scalar
                def _(s):
                    s.wait_ge(S.h["ld1"], B0["ld1"] + 16)
                    s.activation(wgtmp[:, :], wg_sb[:, :], AF.Copy, accum_out=wgs[:, 0:1])

            with nc.Block() as block:
                @block.tensor
                def _(t):
                    for j in range(8):
                        t.transpose(ps_g[:, j * 8:(j + 1) * 8],
                                    wg_sb[0:8, j * 128:(j + 1) * 128], ident[0:8, 0:8])
                    t.transpose(ps_g[0:1, 64:72], wgs[0:8, 0:1], ident[0:8, 0:8])
            with nc.Block() as block:
                @block.vector
                def _(v):
                    v.tensor_copy(WgT_s[:, :], ps_g[:, 0:64])
                    v.tensor_copy(wgsr[:, :], ps_g[0:1, 64:72])
            with nc.Block() as block:
                @block.tensor
                def _(t):
                    t.matmul(ps_g[:, 72:80], onescol[0:1, :], wgsr[:, :], start=True, stop=True)
            with nc.Block() as block:
                @block.vector
                def _(v):
                    v.tensor_copy(W0[:, :], ps_g[:, 72:80])

        with ExitStack() as c2:
            xs = c2.enter_context(nc.sbuf_tensor(f"{tag}_xs", [128, NCH * TT], F32))
            xn = c2.enter_context(nc.sbuf_tensor(f"{tag}_xn", [128, NCH * TT], BF16))
            abx = c2.enter_context(nc.sbuf_tensor(f"{tag}_abx", [128, 1024], F32))
            tmpf = c2.enter_context(nc.sbuf_tensor(f"{tag}_tmpf", [128, 512], F32))
            w1T = c2.enter_context(nc.sbuf_tensor(f"{tag}_w1T", [128, NCH * HFF], BF16))
            w2T = c2.enter_context(nc.sbuf_tensor(f"{tag}_w2T", [128, NCH * HFF], BF16))
            w3T = c2.enter_context(nc.sbuf_tensor(f"{tag}_w3T", [128, 16 * C], BF16))
            gbuf = c2.enter_context(nc.sbuf_tensor(f"{tag}_g", [128, 16 * TT], BF16))
            h1s = c2.enter_context(nc.sbuf_tensor(f"{tag}_h1s", [128, TT], BF16))
            wexp = c2.enter_context(nc.sbuf_tensor(f"{tag}_wex", [128, TT], BF16))
            w_tok = c2.enter_context(nc.sbuf_tensor(f"{tag}_wtok", [128, 64], F32))
            w_fm = c2.enter_context(nc.sbuf_tensor(f"{tag}_wfm", [8, 1024], BF16))
            gate_sb = c2.enter_context(nc.sbuf_tensor(f"{tag}_gt", [128, 8], F32))
            msk = c2.enter_context(nc.sbuf_tensor(f"{tag}_msk", [128, 8], F32))
            g2b = c2.enter_context(nc.sbuf_tensor(f"{tag}_g2", [128, 8], F32))
            egb = c2.enter_context(nc.sbuf_tensor(f"{tag}_eg", [128, 8], F32))
            wrb = c2.enter_context(nc.sbuf_tensor(f"{tag}_wr", [128, 8], F32))
            m1b = c2.enter_context(nc.sbuf_tensor(f"{tag}_m1", [128, 1], F32))
            nm1b = c2.enter_context(nc.sbuf_tensor(f"{tag}_nm1", [128, 1], F32))
            m2b = c2.enter_context(nc.sbuf_tensor(f"{tag}_m2", [128, 1], F32))
            ddb = c2.enter_context(nc.sbuf_tensor(f"{tag}_dd", [128, 1], F32))
            edb = c2.enter_context(nc.sbuf_tensor(f"{tag}_ed", [128, 1], F32))
            wvb = c2.enter_context(nc.sbuf_tensor(f"{tag}_wv", [128, 1], F32))
            PB1 = c2.enter_context(nc.psum_tensor(f"{tag}_PB1", [128, 1024], F32))
            PB2 = c2.enter_context(nc.psum_tensor(f"{tag}_PB2", [128, 1024], F32))
            PB3 = c2.enter_context(nc.psum_tensor(f"{tag}_PB3", [128, 1024], F32))
            psG = c2.enter_context(nc.psum_tensor(f"{tag}_psG", [128, 8], F32))
            psT = c2.enter_context(nc.psum_tensor(f"{tag}_psT", [128, 128], F32))

            B = S.base("ld0", "tns", "vec", "scl", "sto", "aux")
            TNS_T, SCL_T, VEC_T, AUX_T = 346, 144, 211, 112
            ab_mm = lambda i, r: B["tns"] + TNS_T * i + (r + 1)
            gate_mm = lambda i, u: B["tns"] + TNS_T * i + 2 + (u + 1)
            wT_mm = lambda i, u: B["tns"] + TNS_T * i + 10 + (u + 1)
            eb_t = lambda i, e: B["tns"] + TNS_T * i + 18 + 41 * e
            wexp_mm = lambda i, e: eb_t(i, e) + 1
            h1_mm = lambda i, e, hh: eb_t(i, e) + 1 + 2 * hh + 1
            h2_mm = lambda i, e, hh: eb_t(i, e) + 1 + 2 * hh + 2
            y_mm = lambda i, e, d: eb_t(i, e) + 33 + (d + 1)
            eg_s = lambda i, u: B["scl"] + SCL_T * i + 2 * u + 1
            ed_s = lambda i, u: B["scl"] + SCL_T * i + 2 * u + 2
            silu_s = lambda i, e, hh: B["scl"] + SCL_T * i + 16 + 16 * e + hh + 1
            ab_ev = lambda i, r: B["vec"] + VEC_T * i + (r + 1)
            apl = lambda i: B["vec"] + VEC_T * i + 3
            gva = lambda i, u: B["aux"] + AUX_T * i + 14 * u + 8
            gvb = lambda i, u: B["aux"] + AUX_T * i + 14 * u + 14
            wfm_ev = lambda i, u: B["vec"] + VEC_T * i + 3 + (u + 1)
            eb_v = lambda i, e: B["vec"] + VEC_T * i + 11 + 25 * e
            wexp_ev = lambda i, e: eb_v(i, e) + 1
            gtt_v = lambda i, e, hh: eb_v(i, e) + 1 + (hh + 1)
            yev_v = lambda i, e, d: eb_v(i, e) + 17 + (d + 1)
            sto_done = lambda i: B["sto"] + 128 * (i + 1)
            ld_done = lambda i: B["ld0"] + 128 * (i + 1)
            wld = {}

            with nc.Block() as block:
                @block.sync
                def _(sync):
                    for i in range(NTT):
                        if i >= 1:
                            sync.wait_ge(S.h["sto"], sto_done(i - 1))
                        for c in range(NCH):
                            S.inc("ld0", sync.dma_start(
                                out=xs[:, c * TT:(c + 1) * TT],
                                in_=scr_in[c * 128:(c + 1) * 128, i * TT:(i + 1) * TT]), 16)
                        for e in range(NE):
                            if (i, e) != (0, 0):
                                pi, pe = (i, e - 1) if e > 0 else (i - 1, NE - 1)
                                sync.wait_ge(S.h["tns"], h2_mm(pi, pe, 15))
                            for c in range(NCH):
                                inst = sync.dma_start(
                                    out=w1T[:, c * HFF:(c + 1) * HFF],
                                    in_=D["w1_bf"][e * HFF:(e + 1) * HFF, c * 128:(c + 1) * 128],
                                    transpose=True)
                                S.inc("ld1", inst, 16)
                                inst = sync.dma_start(
                                    out=w2T[:, c * HFF:(c + 1) * HFF],
                                    in_=D["w2_bf"][e * HFF:(e + 1) * HFF, c * 128:(c + 1) * 128],
                                    transpose=True)
                                S.inc("ld1", inst, 16)
                            wld[(i, e, "w12")] = S.v["ld1"]
                            if (i, e) != (0, 0):
                                pi, pe = (i, e - 1) if e > 0 else (i - 1, NE - 1)
                                sync.wait_ge(S.h["tns"], y_mm(pi, pe, 7))
                            for hh in range(16):
                                inst = sync.dma_start(
                                    out=w3T[:, hh * C:(hh + 1) * C],
                                    in_=D["w3_bf"][e * C:(e + 1) * C, hh * 128:(hh + 1) * 128],
                                    transpose=True)
                                S.inc("ld1", inst, 16)
                            wld[(i, e, "w3")] = S.v["ld1"]
                        for o in range(NCH):
                            sync.wait_ge(S.h["vec"], yev_v(i, NE - 1, o))
                            S.inc("sto", sync.dma_start(
                                out=scr_out[o * 128:(o + 1) * 128, i * TT:(i + 1) * TT],
                                in_=xs[:, o * TT:(o + 1) * TT]), 16)
                    sync.wait_ge(S.h["sto"], sto_done(NTT - 1))

                @block.tensor
                def _(t):
                    for i in range(NTT):
                        if i >= 1:
                            t.wait_ge(S.h["vec"], wexp_ev(i - 1, NE - 1))
                        for r in range(2):
                            if r == 1:
                                t.wait_ge(S.h["vec"], ab_ev(i, 0))
                            t.matmul(PB1[:, 0:512], onescol_bf[0:1, :],
                                     a_bf[:, i * TT + r * 512: i * TT + (r + 1) * 512],
                                     start=True, stop=True)
                            inst = t.matmul(PB1[:, 512:1024], onescol_bf[64:65, :],
                                            b_bf[:, i * TT + r * 512: i * TT + (r + 1) * 512],
                                            start=True, stop=True)
                            S.inc("tns", inst, 1)
                        for u in range(8):
                            t.wait_ge(S.h["ld0"], ld_done(i))
                            if (i, u) != (0, 0):
                                pi, pu = (i, u - 1) if u > 0 else (i - 1, 7)
                                t.wait_ge(S.h["aux"], gva(pi, pu))
                            for c in range(NCH):
                                inst = t.matmul(psG[:, :],
                                                xs[:, c * TT + u * 128: c * TT + (u + 1) * 128],
                                                WgT_s[:, c * 8:(c + 1) * 8],
                                                start=(c == 0), stop=(c == NCH - 1))
                            S.inc("tns", inst, 1)
                        for u in range(8):
                            t.wait_ge(S.h["aux"], gvb(i, u))
                            if (i, u) != (0, 0):
                                pi, pu = (i, u - 1) if u > 0 else (i - 1, 7)
                                t.wait_ge(S.h["vec"], wfm_ev(pi, pu))
                            S.inc("tns", t.transpose(psT[0:8, :], w_tok[:, u * 8:(u + 1) * 8],
                                                     ident[:, :]), 1)
                        for e in range(NE):
                            if e > 0:
                                t.wait_ge(S.h["vec"], wexp_ev(i, e - 1))
                            else:
                                t.wait_ge(S.h["vec"], ab_ev(i, 1))
                            t.wait_ge(S.h["vec"], wfm_ev(i, 7))
                            t.matmul(PB1[:, 0:512], ind8[:, e * 128:(e + 1) * 128],
                                     w_fm[:, 0:512], start=True, stop=True)
                            inst = t.matmul(PB1[:, 512:1024], ind8[:, e * 128:(e + 1) * 128],
                                            w_fm[:, 512:1024], start=True, stop=True)
                            S.inc("tns", inst, 1)
                            t.wait_ge(S.h["ld1"], wld[(i, e, "w12")])
                            if e == 0:
                                t.wait_ge(S.h["vec"], apl(i))
                            for hh in range(16):
                                if (e, hh) != (0, 0):
                                    ph = (e, hh - 1) if hh > 0 else (e - 1, 15)
                                    t.wait_ge(S.h["scl"], silu_s(i, ph[0], ph[1]))
                                elif i > 0:
                                    t.wait_ge(S.h["scl"], silu_s(i - 1, NE - 1, 15))
                                if hh == 0:
                                    if e > 0:
                                        t.wait_ge(S.h["vec"], yev_v(i, e - 1, 7))
                                    elif i > 0:
                                        t.wait_ge(S.h["vec"], yev_v(i - 1, NE - 1, 7))
                                for c in range(NCH):
                                    for nn in range(2):
                                        inst = t.matmul(
                                            PB2[:, nn * 512:(nn + 1) * 512],
                                            w1T[:, c * HFF + hh * 128: c * HFF + (hh + 1) * 128],
                                            xn[:, c * TT + nn * 512: c * TT + (nn + 1) * 512],
                                            start=(c == 0), stop=(c == NCH - 1))
                                S.inc("tns", inst, 1)
                                if (e, hh) != (0, 0):
                                    ph = (e, hh - 1) if hh > 0 else (e - 1, 15)
                                    t.wait_ge(S.h["vec"], gtt_v(i, ph[0], ph[1]))
                                elif i > 0:
                                    t.wait_ge(S.h["vec"], gtt_v(i - 1, NE - 1, 15))
                                for c in range(NCH):
                                    for nn in range(2):
                                        inst = t.matmul(
                                            PB3[:, nn * 512:(nn + 1) * 512],
                                            w2T[:, c * HFF + hh * 128: c * HFF + (hh + 1) * 128],
                                            xn[:, c * TT + nn * 512: c * TT + (nn + 1) * 512],
                                            start=(c == 0), stop=(c == NCH - 1))
                                S.inc("tns", inst, 1)
                            t.wait_ge(S.h["ld1"], wld[(i, e, "w3")])
                            t.wait_ge(S.h["vec"], gtt_v(i, e, 15))
                            for d in range(NCH):
                                PY = PB2 if d % 2 == 0 else PB3
                                if d >= 2:
                                    t.wait_ge(S.h["vec"], yev_v(i, e, d - 2))
                                for hh in range(16):
                                    for nn in range(2):
                                        inst = t.matmul(
                                            PY[:, nn * 512:(nn + 1) * 512],
                                            w3T[:, hh * C + d * 128: hh * C + (d + 1) * 128],
                                            gbuf[:, hh * TT + nn * 512: hh * TT + (nn + 1) * 512],
                                            start=(hh == 0), stop=(hh == 15))
                                S.inc("tns", inst, 1)

                @block.scalar
                def _(s):
                    for i in range(NTT):
                        for u in range(8):
                            s.wait_ge(S.h["aux"], gva(i, u))
                            S.inc("scl", s.activation(egb[:, :], gate_sb[:, :], AF.Exp,
                                                      bias=nm1b[:, 0:1]), 1)
                            S.inc("scl", s.activation(edb[:, :], ddb[:, :], AF.Exp), 1)
                        for e in range(NE):
                            for hh in range(16):
                                s.wait_ge(S.h["tns"], h1_mm(i, e, hh))
                                if (e, hh) != (0, 0):
                                    ph = (e, hh - 1) if hh > 0 else (e - 1, 15)
                                    s.wait_ge(S.h["vec"], gtt_v(i, ph[0], ph[1]))
                                elif i > 0:
                                    s.wait_ge(S.h["vec"], gtt_v(i - 1, NE - 1, 15))
                                S.inc("scl", s.activation(h1s[:, :], PB2[:, :], AF.Silu), 1)

                @block.vector
                def _(v):
                    for i in range(NTT):
                        v.wait_ge(S.h["ld0"], ld_done(i))
                        if i > 0:
                            v.wait_ge(S.h["tns"], h2_mm(i - 1, NE - 1, 15))
                        for r in range(2):
                            v.wait_ge(S.h["tns"], ab_mm(i, r))
                            v.tensor_copy(abx[:, 0:512], PB1[:, 0:512])
                            S.inc("vec", v.tensor_copy(abx[:, 512:1024], PB1[:, 512:1024]), 1)
                            for c in range(NCH):
                                v.tensor_tensor(tmpf[:, 0:512],
                                                xs[:, c * TT + r * 512: c * TT + (r + 1) * 512],
                                                abx[:, 0:512], ALU.mult)
                                inst = v.tensor_tensor(
                                    xn[:, c * TT + r * 512: c * TT + (r + 1) * 512],
                                    tmpf[:, 0:512], abx[:, 512:1024], ALU.add)
                        S.inc("vec", inst, 1)
                        for u in range(8):
                            tc = i * 8 + u
                            v.wait_ge(S.h["tns"], gate_mm(i, u))

                            def step(inst):
                                S.inc("aux", inst, 1)
                                v.wait_ge(S.h["aux"], S.v["aux"])
                                return inst

                            step(v.tensor_scalar(gate_sb[:, :], psG[:, :], abT[:, tc:tc + 1],
                                                 None, ALU.mult))
                            step(v.scalar_tensor_tensor(gate_sb[:, :], W0[:, :],
                                                        abT[:, 32 + tc:33 + tc],
                                                        gate_sb[:, :], ALU.mult, ALU.add))
                            step(v.reduce_max(m1b[:, :], gate_sb[:, :], axis=AX.X))
                            step(v.tensor_scalar(nm1b[:, :], m1b[:, :], -1.0, None, ALU.mult))
                            step(v.tensor_scalar(msk[:, :], gate_sb[:, :], m1b[:, 0:1], None,
                                                 ALU.is_equal))
                            step(v.scalar_tensor_tensor(g2b[:, :], msk[:, :], -1e9, gate_sb[:, :],
                                                        ALU.mult, ALU.add))
                            step(v.reduce_max(m2b[:, :], g2b[:, :], axis=AX.X))
                            step(v.tensor_tensor(ddb[:, :], m2b[:, :], nm1b[:, :], ALU.add))
                            v.wait_ge(S.h["scl"], ed_s(i, u))
                            step(v.tensor_scalar(msk[:, :], gate_sb[:, :], m2b[:, 0:1], None,
                                                 ALU.is_ge))
                            step(v.tensor_tensor(wrb[:, :], egb[:, :], msk[:, :], ALU.mult))
                            step(v.tensor_scalar(edb[:, :], edb[:, :], 1.0, None, ALU.add))
                            step(v.reciprocal(wvb[:, :], edb[:, :]))
                            step(v.tensor_scalar(wvb[:, :], wvb[:, :], 0.5, None, ALU.mult))
                            step(v.tensor_scalar(w_tok[:, u * 8:(u + 1) * 8], wrb[:, :],
                                                  wvb[:, 0:1], None, ALU.mult))
                        for u in range(8):
                            v.wait_ge(S.h["tns"], wT_mm(i, u))
                            S.inc("vec", v.tensor_copy(w_fm[:, u * 128:(u + 1) * 128],
                                                       psT[0:8, :]), 1)
                        for e in range(NE):
                            v.wait_ge(S.h["tns"], wexp_mm(i, e))
                            S.inc("vec", v.tensor_copy(wexp[:, :], PB1[:, :]), 1)
                            for hh in range(16):
                                v.wait_ge(S.h["scl"], silu_s(i, e, hh))
                                v.wait_ge(S.h["tns"], h2_mm(i, e, hh))
                                v.tensor_tensor(gbuf[:, hh * TT:(hh + 1) * TT], h1s[:, :],
                                                PB3[:, :], ALU.mult)
                                S.inc("vec", v.tensor_tensor(gbuf[:, hh * TT:(hh + 1) * TT],
                                                             gbuf[:, hh * TT:(hh + 1) * TT],
                                                             wexp[:, :], ALU.mult), 1)
                            for d in range(NCH):
                                PY = PB2 if d % 2 == 0 else PB3
                                v.wait_ge(S.h["tns"], y_mm(i, e, d))
                                S.inc("vec", v.tensor_tensor(xs[:, d * TT:(d + 1) * TT],
                                                             xs[:, d * TT:(d + 1) * TT],
                                                             PY[:, :], ALU.add), 1)

            assert S.v["tns"] == B["tns"] + TNS_T * NTT, (S.v["tns"], B["tns"] + TNS_T * NTT)
            assert S.v["scl"] == B["scl"] + SCL_T * NTT
            assert S.v["vec"] == B["vec"] + VEC_T * NTT, (S.v["vec"], B["vec"] + VEC_T * NTT)
            assert S.v["aux"] == B["aux"] + AUX_T * NTT


STAGES = {
    "common": dict(emit=emit_common, io={}),
    "entry": dict(emit=emit_entry, io={"x_in": ((T, C), F32, "in")}),
    "exit": dict(emit=emit_exit, io={"x_out": ((T, C), F32, "out")}),
    "prep_l0ffn": dict(emit=emit_prep_l0ffn, io={
        "W_r": ((C, C), F32, "in"), "W_k": ((C, C), F32, "in"), "W_v": ((C, C), F32, "in"),
        "wr_bf": ((C, C), BF16, "internal"), "wk_bf": ((C, C), BF16, "internal"),
        "wv_bf": ((C, C), BF16, "internal")}),
    "s1": dict(emit=lambda nc, D, S: emit_l0ffn(nc, D, S, "f1"), io={
        "mu_r": ((1, C), F32, "in"), "mu_k": ((1, C), F32, "in"),
        "wr_bf": ((C, C), BF16, "internal"), "wk_bf": ((C, C), BF16, "internal"),
        "wv_bf": ((C, C), BF16, "internal")}),
    "s3": dict(emit=lambda nc, D, S: emit_l0ffn(nc, D, S, "f2"), io={
        "mu_r": ((1, C), F32, "in"), "mu_k": ((1, C), F32, "in"),
        "wr_bf": ((C, C), BF16, "internal"), "wk_bf": ((C, C), BF16, "internal"),
        "wv_bf": ((C, C), BF16, "internal")}),
    "prep_l0attn": dict(emit=emit_prep_l0attn, io={
        "WQ": ((C, C), F32, "in"), "WK": ((C, C), F32, "in"),
        "WV": ((C, C), F32, "in"), "WO": ((C, C), F32, "in"),
        "wq_bf": ((C, C), BF16, "internal"), "wk_bf2": ((C, C), BF16, "internal"),
        "wv_bf2": ((C, C), BF16, "internal"), "wo_bf": ((C, C), BF16, "internal")}),
    "prep_l1ffn": dict(emit=emit_prep_l1ffn, io={
        "W1": ((NE * HFF, C), F32, "in"), "W2": ((NE * HFF, C), F32, "in"),
        "W3": ((NE * C, HFF), F32, "in"),
        "w1_bf": ((NE * HFF, C), BF16, "internal"), "w2_bf": ((NE * HFF, C), BF16, "internal"),
        "w3_bf": ((NE * C, HFF), BF16, "internal")}),
    "s4": dict(emit=lambda nc, D, S: emit_moe(nc, D, S, "m1"), io={
        "Wg": ((NE, C), F32, "in"),
        "w1_bf": ((NE * HFF, C), BF16, "internal"), "w2_bf": ((NE * HFF, C), BF16, "internal"),
        "w3_bf": ((NE * C, HFF), BF16, "internal")}),
    "s6": dict(emit=lambda nc, D, S: emit_moe(nc, D, S, "m2"), io={
        "Wg": ((NE, C), F32, "in"),
        "w1_bf": ((NE * HFF, C), BF16, "internal"), "w2_bf": ((NE * HFF, C), BF16, "internal"),
        "w3_bf": ((NE * C, HFF), BF16, "internal")}),
    "prep_l1attn": dict(emit=emit_prep_l1attn, io={
        "WQKV": ((3072, C), F32, "in"), "WPROJ": ((C, C), F32, "in"),
        "wqkv_bf": ((3072, C), BF16, "internal"), "wproj_bf": ((C, C), BF16, "internal")}),
    "s5": dict(emit=emit_l1attn, io={
        "b_proj": ((1, C), F32, "in"),
        "wqkv_bf": ((3072, C), BF16, "internal"), "wproj_bf": ((C, C), BF16, "internal")}),
    "s2": dict(emit=emit_l0attn, io={
        "bo": ((1, C), F32, "in"),
        "wq_bf": ((C, C), BF16, "internal"), "wk_bf2": ((C, C), BF16, "internal"),
        "wv_bf2": ((C, C), BF16, "internal"), "wo_bf": ((C, C), BF16, "internal"),
        "q_scr": ((C, T), BF16, "internal"), "k_scr": ((C, T), BF16, "internal"),
        "v_scr": ((T, C), BF16, "internal"), "a_scr": ((16 * T, 64), BF16, "internal")}),
}





def build_program(stage_names, scr_mode="internal", uses_scr=True, scr_need="rw"):
    nc = bass.Bass("TRN2", target_bir_lowering=False)
    io = {}
    for sn in stage_names:
        for name, spec in STAGES[sn].get("io", {}).items():
            io[name] = spec
    D = {}
    for name, (shape, dtype, kind) in io.items():
        k = {"in": "ExternalInput", "out": "ExternalOutput", "internal": "Internal"}[kind]
        D[name] = nc.dram_tensor(name, list(shape), dtype, kind=k).ap()
    if uses_scr:
        if scr_mode == "internal":
            scr = nc.dram_tensor("scr", [C, T], F32, kind="Internal").ap()
            D["scr_in"] = scr
            D["scr_out"] = scr
        else:
            if "r" in scr_need:
                D["scr_in"] = nc.dram_tensor("scr_in", [C, T], F32, kind="ExternalInput").ap()
            if "w" in scr_need:
                D["scr_out"] = nc.dram_tensor("scr_out", [C, T], F32, kind="ExternalOutput").ap()
    with ExitStack() as stack:
        for name, (shape, dtype) in COMMON_SBUF.items():
            D[name] = stack.enter_context(nc.sbuf_tensor(name, shape, dtype))
        S = Sems(nc, stack, SEM_NAMES)
        for sn in stage_names:
            STAGES[sn]["emit"](nc, D, S)
    return nc


FUSED_STAGES = ["common", "prep_l0ffn", "entry", "prep_l0attn", "prep_l1attn", "prep_l1ffn",
                "s1", "s2", "s3", "s4", "s5", "s6", "exit"]


def kernel(**inputs):
    f32 = np.float32
    x = np.ascontiguousarray(np.asarray(inputs["L0_attn_x"], dtype=f32))
    n = 8

    def arr(name, shape=None):
        a = np.ascontiguousarray(np.asarray(inputs[name], dtype=f32))
        return a.reshape(shape) if shape is not None else a

    wmap = dict(W_r=arr("L0_ffn_W_r"), W_k=arr("L0_ffn_W_k"), W_v=arr("L0_ffn_W_v"),
                mu_r=arr("L0_ffn_mu_r", (1, C)), mu_k=arr("L0_ffn_mu_k", (1, C)),
                WQ=arr("L0_attn_Wq"), WK=arr("L0_attn_Wk"), WV=arr("L0_attn_Wv"),
                WO=arr("L0_attn_Wo"), bo=arr("L0_attn_bo", (1, C)),
                Wg=arr("L1_ffn_Wg"), W1=arr("L1_ffn_W1", (NE * HFF, C)),
                W2=arr("L1_ffn_W2", (NE * HFF, C)), W3=arr("L1_ffn_W3", (NE * C, HFF)),
                WQKV=arr("L1_attn_w_qkv"), WPROJ=arr("L1_attn_w_proj"),
                b_proj=arr("L1_attn_b_proj", (1, C)))

    nc = build_program(FUSED_STAGES, scr_mode="internal")
    in_maps = [dict(x_in=np.ascontiguousarray(x[i]), **wmap) for i in range(n)]
    res = run_bass_kernel_spmd(nc, in_maps, core_ids=list(range(n)))
    return np.stack([res.results[i]["x_out"] for i in range(n)], axis=0)


# version 13 (final)
# speedup vs baseline: 1.0768x; 1.0207x over previous
import math
from contextlib import ExitStack

import numpy as np
import concourse.bass as bass
import concourse.mybir as mybir
from concourse.bass_utils import run_bass_kernel_spmd

F32 = mybir.dt.float32
BF16 = mybir.dt.bfloat16
AF = mybir.ActivationFunctionType
ALU = mybir.AluOpType
AX = mybir.AxisListType

T = 4096
C = 1024
NCH = 8
NTCH = 32
NT = 8
H = 16
HD = 64
NE = 8
HFF = 2048


class Sems:

    def __init__(self, nc, stack, names):
        self.nc = nc
        self.h = {}
        self.v = {}
        self.marks = {}
        for n in names:
            self.h[n] = stack.enter_context(nc.semaphore(n))
            self.v[n] = 0

    def inc(self, name, inst, n=1):
        inst.then_inc(self.h[name], n)
        self.v[name] += n
        return self.v[name]

    def base(self, *names):
        return {n: self.v[n] for n in names}


SEM_NAMES = ["ld0", "ld1", "tns", "vec", "scl", "sto", "cast", "aux"]



def emit_common(nc, D, S):
    ident = D["cm_ident"]
    ones_bf = D["cm_ones_bf"]
    onescol = D["cm_onescol"]
    with nc.Block() as block:
        @block.gpsimd
        def _(g):
            g.memset(ident[:, :], 0.0)
            g.affine_select(out=ident[:, :], in_=ident[:, :], compare_op=ALU.not_equal,
                            fill=1.0, base=0, pattern=[[-1, 128]], channel_multiplier=1)
            g.memset(ones_bf[:, :], 1.0)
            g.memset(onescol[:, :], 1.0)
            g.memset(D["cm_eps"][:, :], 1e-5)
            g.memset(D["cm_onescol_bf"][:, :], 1.0)
            g.memset(D["cm_ident_bf"][:, :], 0.0)
            g.affine_select(out=D["cm_ident_bf"][:, :], in_=D["cm_ident_bf"][:, :],
                            compare_op=ALU.not_equal, fill=1.0, base=0,
                            pattern=[[-1, 128]], channel_multiplier=1)


COMMON_SBUF = {
    "cm_ident": ([128, 128], F32),
    "cm_ident_bf": ([128, 128], BF16),
    "cm_ones_bf": ([128, 1], BF16),
    "cm_onescol": ([128, 128], F32),
    "cm_onescol_bf": ([128, 128], BF16),
    "cm_eps": ([1, 1], F32),
}



def emit_entry(nc, D, S):
    x_in, scr = D["x_in"], D["scr_out"]
    ident = D["cm_ident"]
    with ExitStack() as ctx:
        xt = [ctx.enter_context(nc.sbuf_tensor(f"en_xt{i}", [128, C], F32)) for i in range(2)]
        xf = [ctx.enter_context(nc.sbuf_tensor(f"en_xf{c}", [128, T], F32)) for c in range(NCH)]
        ps = [ctx.enter_context(nc.psum_tensor(f"en_ps{i}", [128, 512], F32)) for i in range(4)]
        B = S.base("ld0", "tns", "vec", "sto")

        with nc.Block() as block:
            @block.sync
            def _(sync):
                for tc in range(NTCH):
                    if tc >= 2:
                        sync.wait_ge(S.h["tns"], B["tns"] + 2 * (tc - 1))
                    S.inc("ld0", sync.dma_start(out=xt[tc % 2][:, :],
                                                in_=x_in[tc * 128:(tc + 1) * 128, :]), 16)

            @block.tensor
            def _(t):
                for tc in range(NTCH):
                    t.wait_ge(S.h["ld0"], B["ld0"] + 16 * (tc + 1))
                    if tc >= 2:
                        t.wait_ge(S.h["vec"], B["vec"] + 2 * (tc - 1))
                    for c in range(NCH):
                        bank = ps[(tc % 2) * 2 + c // 4]
                        inst = t.transpose(bank[:, (c % 4) * 128:(c % 4 + 1) * 128],
                                           xt[tc % 2][:, c * 128:(c + 1) * 128], ident[:, :])
                        if c in (3, 7):
                            S.inc("tns", inst, 1)

            @block.vector
            def _(v):
                for tc in range(NTCH):
                    for half in range(2):
                        v.wait_ge(S.h["tns"], B["tns"] + 2 * tc + half + 1)
                        bank = ps[(tc % 2) * 2 + half]
                        for cc in range(4):
                            c = half * 4 + cc
                            inst = v.tensor_copy(xf[c][:, tc * 128:(tc + 1) * 128],
                                                 bank[:, cc * 128:(cc + 1) * 128])
                            if cc == 3:
                                S.inc("vec", inst, 1)

        with nc.Block() as block:
            @block.sync
            def _(sync):
                for c in range(NCH):
                    S.inc("sto", sync.dma_start(out=scr[c * 128:(c + 1) * 128, :], in_=xf[c][:, :]), 16)
                sync.wait_ge(S.h["sto"], S.v["sto"])


def emit_exit(nc, D, S):
    scr, x_out = D["scr_in"], D["x_out"]
    ident = D["cm_ident"]
    with ExitStack() as ctx:
        xf = [ctx.enter_context(nc.sbuf_tensor(f"ex_xf{c}", [128, T], F32)) for c in range(NCH)]
        xt = [ctx.enter_context(nc.sbuf_tensor(f"ex_xt{i}", [128, C], F32)) for i in range(2)]
        ps = [ctx.enter_context(nc.psum_tensor(f"ex_ps{i}", [128, 512], F32)) for i in range(4)]
        B = S.base("ld0", "tns", "vec", "sto")

        with nc.Block() as block:
            @block.sync
            def _(sync):
                for c in range(NCH):
                    S.inc("ld0", sync.dma_start(out=xf[c][:, :], in_=scr[c * 128:(c + 1) * 128, :]), 16)

        with nc.Block() as block:
            @block.tensor
            def _(t):
                t.wait_ge(S.h["ld0"], B["ld0"] + 128)
                for tc in range(NTCH):
                    if tc >= 2:
                        t.wait_ge(S.h["sto"], B["sto"] + 16 * (tc - 1))
                    for c in range(NCH):
                        bank = ps[(tc % 2) * 2 + c // 4]
                        inst = t.transpose(bank[:, (c % 4) * 128:(c % 4 + 1) * 128],
                                           xf[c][:, tc * 128:(tc + 1) * 128], ident[:, :])
                        if c in (3, 7):
                            S.inc("tns", inst, 1)

            @block.vector
            def _(v):
                for tc in range(NTCH):
                    for half in range(2):
                        v.wait_ge(S.h["tns"], B["tns"] + 2 * tc + half + 1)
                        bank = ps[(tc % 2) * 2 + half]
                        for cc in range(4):
                            c = half * 4 + cc
                            inst = v.tensor_copy(xt[tc % 2][:, c * 128:(c + 1) * 128],
                                                 bank[:, cc * 128:(cc + 1) * 128])
                            if c == 7:
                                S.inc("vec", inst, 1)

            @block.sync
            def _(sync):
                for tc in range(NTCH):
                    sync.wait_ge(S.h["vec"], B["vec"] + tc + 1)
                    S.inc("sto", sync.dma_start(out=x_out[tc * 128:(tc + 1) * 128, :],
                                                in_=xt[tc % 2][:, :]), 16)
                sync.wait_ge(S.h["sto"], S.v["sto"])




def emit_prep_l0ffn(nc, D, S):
    with nc.Block() as block:
        @block.gpsimd
        def _(g):
            S.marks["wr_bf"] = S.inc("cast", g.dma_start(out=D["wr_bf"], in_=D["W_r"]), 16)
            S.marks["wk_bf"] = S.inc("cast", g.dma_start(out=D["wk_bf"], in_=D["W_k"]), 16)
            S.marks["wv_bf"] = S.inc("cast", g.dma_start(out=D["wv_bf"], in_=D["W_v"]), 16)


def _emit_ln_stats(nc, D, S, ctx, scr, a_sb, b_sb, tag=""):
    ones_bf = D["cm_ones_bf"]
    TH = T // 2
    with ExitStack() as c2:
        xs32 = [c2.enter_context(nc.sbuf_tensor(f"{tag}ls_x{i}", [128, TH], F32)) for i in range(2)]
        xb = c2.enter_context(nc.sbuf_tensor(f"{tag}ls_xb", [128, NCH * TH], BF16))
        s1_sb = c2.enter_context(nc.sbuf_tensor(f"{tag}ls_s1", [1, T], F32))
        s2_sb = c2.enter_context(nc.sbuf_tensor(f"{tag}ls_s2", [1, T], F32))
        tmp = c2.enter_context(nc.sbuf_tensor(f"{tag}ls_tmp", [1, T], F32))
        tmp2 = c2.enter_context(nc.sbuf_tensor(f"{tag}ls_tmp2", [1, 512], F32))
        ps_s = [c2.enter_context(nc.psum_tensor(f"{tag}ls_ps{i}", [1, 512], F32)) for i in range(2)]
        B = S.base("ld0", "tns", "vec", "scl")
        cp = lambda h, c: B["scl"] + 16 * h + (c + 1)
        sq = lambda h, c: B["scl"] + 16 * h + 8 + (c + 1)
        g1 = lambda h, ts: B["tns"] + 8 * h + (ts + 1)
        g2 = lambda h, ts: B["tns"] + 8 * h + 4 + (ts + 1)
        e1 = lambda h, ts: B["vec"] + 8 * h + (ts + 1)
        e2 = lambda h, ts: B["vec"] + 8 * h + 4 + (ts + 1)

        with nc.Block() as block:
            @block.sync
            def _(sync):
                for h2 in range(2):
                    for c in range(NCH):
                        if h2 * NCH + c >= 2:
                            pc = h2 * NCH + c - 2
                            sync.wait_ge(S.h["scl"], cp(pc // NCH, pc % NCH))
                        S.inc("ld0", sync.dma_start(
                            out=xs32[c % 2][:, :],
                            in_=scr[c * 128:(c + 1) * 128, h2 * TH:(h2 + 1) * TH]), 16)

            @block.scalar
            def _(s):
                for h2 in range(2):
                    if h2 > 0:
                        s.wait_ge(S.h["tns"], g2(h2 - 1, 3))
                    for c in range(NCH):
                        s.wait_ge(S.h["ld0"], B["ld0"] + 16 * (h2 * NCH + c + 1))
                        S.inc("scl", s.copy(xb[:, c * TH:(c + 1) * TH], xs32[c % 2][:, :]), 1)
                    s.wait_ge(S.h["tns"], g1(h2, 3))
                    for c in range(NCH):
                        S.inc("scl", s.activation(xb[:, c * TH:(c + 1) * TH],
                                                  xb[:, c * TH:(c + 1) * TH], AF.Square), 1)

            @block.tensor
            def _(t):
                for h2 in range(2):
                    t.wait_ge(S.h["scl"], cp(h2, 7))
                    for ts_ in range(4):
                        if h2 > 0 or ts_ >= 2:
                            pt = h2 * 8 + ts_ - 2
                            t.wait_ge(S.h["vec"], B["vec"] + pt + 1)
                        for c in range(NCH):
                            inst = t.matmul(ps_s[ts_ % 2][:, :], ones_bf[:, :],
                                            xb[:, c * TH + ts_ * 512: c * TH + (ts_ + 1) * 512],
                                            start=(c == 0), stop=(c == NCH - 1))
                        S.inc("tns", inst, 1)
                    t.wait_ge(S.h["scl"], sq(h2, 7))
                    for ts_ in range(4):
                        pt = h2 * 8 + 4 + ts_ - 2
                        t.wait_ge(S.h["vec"], B["vec"] + pt + 1)
                        for c in range(NCH):
                            inst = t.matmul(ps_s[ts_ % 2][:, :], ones_bf[:, :],
                                            xb[:, c * TH + ts_ * 512: c * TH + (ts_ + 1) * 512],
                                            start=(c == 0), stop=(c == NCH - 1))
                        S.inc("tns", inst, 1)

            @block.vector
            def _(v):
                for h2 in range(2):
                    for ts_ in range(4):
                        v.wait_ge(S.h["tns"], g1(h2, ts_))
                        S.inc("vec", v.tensor_copy(
                            s1_sb[:, h2 * TH + ts_ * 512: h2 * TH + (ts_ + 1) * 512],
                            ps_s[ts_ % 2][:, :]), 1)
                    for ts_ in range(4):
                        v.wait_ge(S.h["tns"], g2(h2, ts_))
                        S.inc("vec", v.tensor_copy(
                            s2_sb[:, h2 * TH + ts_ * 512: h2 * TH + (ts_ + 1) * 512],
                            ps_s[ts_ % 2][:, :]), 1)
                for ts_ in range(8):
                    sl = slice(ts_ * 512, (ts_ + 1) * 512)
                    v.tensor_scalar(tmp[:, sl], s1_sb[:, sl], 1.0 / C, None, ALU.mult)
                    v.tensor_tensor(tmp2[:, :], tmp[:, sl], tmp[:, sl], ALU.mult)
                    inst = v.scalar_tensor_tensor(s2_sb[:, sl], s2_sb[:, sl], 1.0 / C,
                                                  tmp2[:, :], ALU.mult, ALU.subtract)
                S.inc("vec", inst, 1)

            @block.scalar
            def _(s):
                s.wait_ge(S.h["vec"], B["vec"] + 17)
                s.activation(s2_sb[:, :], s2_sb[:, :], AF.Ln, bias=D["cm_eps"][0:1, 0:1])
                S.inc("scl", s.activation(a_sb[:, :], s2_sb[:, :], AF.Exp, scale=-0.5), 1)

            @block.vector
            def _(v):
                v.wait_ge(S.h["scl"], B["scl"] + 33)
                S.inc("vec", v.scalar_tensor_tensor(b_sb[:, :], tmp[:, :], -1.0,
                                                    a_sb[:, :], ALU.mult, ALU.mult), 1)


def emit_l0ffn(nc, D, S, tag="f1"):
    scr_in, scr_out = D["scr_in"], D["scr_out"]
    ident, onescol = D["cm_ident"], D["cm_onescol"]
    with ExitStack() as ctx:
        wrT = ctx.enter_context(nc.sbuf_tensor(f"{tag}_wrT", [128, NCH * C], BF16))
        wkT = ctx.enter_context(nc.sbuf_tensor(f"{tag}_wkT", [128, NCH * C], BF16))
        wvT = ctx.enter_context(nc.sbuf_tensor(f"{tag}_wvT", [128, NCH * C], BF16))
        mus = ctx.enter_context(nc.sbuf_tensor(f"{tag}_mus", [1, 2 * C], F32))
        muT = ctx.enter_context(nc.sbuf_tensor(f"{tag}_muT", [128, 16], F32))
        a_sb = ctx.enter_context(nc.sbuf_tensor(f"{tag}_a", [1, T], F32))
        b_sb = ctx.enter_context(nc.sbuf_tensor(f"{tag}_b", [1, T], F32))

        with ExitStack() as c2:
            ps_mu = c2.enter_context(nc.psum_tensor(f"{tag}_psmu", [128, 16], F32))
            B = S.base("ld1", "tns", "cast")
            with nc.Block() as block:
                @block.sync
                def _(sync):
                    sync.wait_ge(S.h["cast"], S.marks["wr_bf"])
                    for c in range(NCH):
                        S.inc("ld1", sync.dma_start(out=wrT[:, c * C:(c + 1) * C],
                                                    in_=D["wr_bf"][:, c * 128:(c + 1) * 128],
                                                    transpose=True), 16)
                    sync.wait_ge(S.h["cast"], S.marks["wk_bf"])
                    for c in range(NCH):
                        S.inc("ld1", sync.dma_start(out=wkT[:, c * C:(c + 1) * C],
                                                    in_=D["wk_bf"][:, c * 128:(c + 1) * 128],
                                                    transpose=True), 16)
                    sync.wait_ge(S.h["cast"], S.marks["wv_bf"])
                    for c in range(NCH):
                        S.inc("ld1", sync.dma_start(out=wvT[:, c * C:(c + 1) * C],
                                                    in_=D["wv_bf"][:, c * 128:(c + 1) * 128],
                                                    transpose=True), 16)
                    S.inc("ld1", sync.dma_start(out=mus[:, 0:C], in_=D["mu_r"]), 16)
                    S.inc("ld1", sync.dma_start(out=mus[:, C:2 * C], in_=D["mu_k"]), 16)

                @block.tensor
                def _(t):
                    t.wait_ge(S.h["ld1"], S.v["ld1"])
                    for j in range(16):
                        t.transpose(ps_mu[:, j:j + 1], mus[0:1, j * 128:(j + 1) * 128],
                                    ident[0:1, 0:1])

            with nc.Block() as block:
                @block.vector
                def _(v):
                    v.tensor_copy(muT[:, :], ps_mu[:, :])

        _emit_ln_stats(nc, D, S, ctx, scr_in, a_sb, b_sb, tag)

        with ExitStack() as c2:
            xs = c2.enter_context(nc.sbuf_tensor(f"{tag}_xs", [128, 2 * NCH * 512], F32))
            xnb = c2.enter_context(nc.sbuf_tensor(f"{tag}_xn", [128, NCH * 513], BF16))
            xr = c2.enter_context(nc.sbuf_tensor(f"{tag}_xr", [128, NCH * 512], BF16))
            xk = c2.enter_context(nc.sbuf_tensor(f"{tag}_xk", [128, NCH * 512], BF16))
            rsb = c2.enter_context(nc.sbuf_tensor(f"{tag}_r", [128, NCH * 512], BF16))
            krelu = c2.enter_context(nc.sbuf_tensor(f"{tag}_kr", [128, 2 * 512], BF16))
            kact = c2.enter_context(nc.sbuf_tensor(f"{tag}_ka", [128, NCH * 512], BF16))
            dbuf = c2.enter_context(nc.sbuf_tensor(f"{tag}_d", [128, 512], F32))
            tmpf = c2.enter_context(nc.sbuf_tensor(f"{tag}_tf", [128, 512], F32))
            abx = c2.enter_context(nc.sbuf_tensor(f"{tag}_abx", [128, 1024], F32))
            ps_ab = c2.enter_context(nc.psum_tensor(f"{tag}_psab", [128, 1024], F32))
            psA = [c2.enter_context(nc.psum_tensor(f"{tag}_psA{i}", [128, 512], F32)) for i in range(2)]

            B = S.base("ld0", "tns", "vec", "scl", "sto")
            ld_done = lambda i: B["ld0"] + 128 * (i + 1)
            ab_mm = lambda i: B["tns"] + 25 * i + 1
            r_mm = lambda i, o: B["tns"] + 25 * i + 1 + (o + 1)
            k_mm = lambda i, o: B["tns"] + 25 * i + 9 + (o + 1)
            v_mm = lambda i, o: B["tns"] + 25 * i + 17 + (o + 1)
            sig = lambda i, o: B["scl"] + 16 * i + (o + 1)
            relu = lambda i, o: B["scl"] + 16 * i + 8 + (o + 1)
            ab_evac = lambda i: B["vec"] + 18 * i + 1
            mix_rdy = lambda i: B["vec"] + 18 * i + 2
            kact_v = lambda i, o: B["vec"] + 18 * i + 2 + (o + 1)
            upd_v = lambda i, o: B["vec"] + 18 * i + 10 + (o + 1)
            sto_done = lambda i: B["sto"] + 128 * (i + 1)

            def xsv(i, c):
                return xs[:, ((i % 2) * NCH + c) * 512:((i % 2) * NCH + c) * 512 + 512]

            with nc.Block() as block:
                @block.sync
                def _(sync):
                    for i in range(2):
                        for c in range(NCH):
                            S.inc("ld0", sync.dma_start(
                                out=xsv(i, c),
                                in_=scr_in[c * 128:(c + 1) * 128, i * 512:(i + 1) * 512]), 16)
                    for i in range(NT):
                        for o in range(NCH):
                            sync.wait_ge(S.h["vec"], upd_v(i, o))
                            S.inc("sto", sync.dma_start(
                                out=scr_out[o * 128:(o + 1) * 128, i * 512:(i + 1) * 512],
                                in_=xsv(i, o)), 16)
                        if i + 2 < NT:
                            sync.wait_ge(S.h["sto"], sto_done(i))
                            for c in range(NCH):
                                S.inc("ld0", sync.dma_start(
                                    out=xsv(i + 2, c),
                                    in_=scr_in[c * 128:(c + 1) * 128,
                                               (i + 2) * 512:(i + 3) * 512]), 16)
                    sync.wait_ge(S.h["sto"], sto_done(NT - 1))

                @block.tensor
                def _(t):
                    for i in range(NT):
                        if i >= 1:
                            t.wait_ge(S.h["vec"], ab_evac(i - 1))
                        t.matmul(ps_ab[:, 0:512], onescol[0:1, :], a_sb[:, i * 512:(i + 1) * 512],
                                 start=True, stop=True)
                        inst = t.matmul(ps_ab[:, 512:1024], onescol[0:1, :], b_sb[:, i * 512:(i + 1) * 512],
                                        start=True, stop=True)
                        S.inc("tns", inst, 1)
                        t.wait_ge(S.h["vec"], mix_rdy(i))
                        for o in range(NCH):
                            if o >= 2:
                                t.wait_ge(S.h["scl"], sig(i, o - 2))
                            elif i > 0:
                                t.wait_ge(S.h["vec"], upd_v(i - 1, o + 6))
                            for c in range(NCH):
                                inst = t.matmul(psA[o % 2][:, :],
                                                wrT[:, c * C + o * 128: c * C + (o + 1) * 128],
                                                xr[:, c * 512:(c + 1) * 512],
                                                start=(c == 0), stop=(c == NCH - 1))
                            S.inc("tns", inst, 1)
                        for o in range(NCH):
                            if o < 2:
                                t.wait_ge(S.h["scl"], sig(i, o + 6))
                            else:
                                t.wait_ge(S.h["vec"], kact_v(i, o - 2))
                            for c in range(NCH):
                                inst = t.matmul(psA[o % 2][:, :],
                                                wkT[:, c * C + o * 128: c * C + (o + 1) * 128],
                                                xk[:, c * 512:(c + 1) * 512],
                                                start=(c == 0), stop=(c == NCH - 1))
                            S.inc("tns", inst, 1)
                        t.wait_ge(S.h["vec"], kact_v(i, 7))
                        for o in range(NCH):
                            if o >= 2:
                                t.wait_ge(S.h["vec"], upd_v(i, o - 2))
                            for c in range(NCH):
                                inst = t.matmul(psA[o % 2][:, :],
                                                wvT[:, c * C + o * 128: c * C + (o + 1) * 128],
                                                kact[:, c * 512:(c + 1) * 512],
                                                start=(c == 0), stop=(c == NCH - 1))
                            S.inc("tns", inst, 1)

                @block.scalar
                def _(s):
                    for i in range(NT):
                        for o in range(NCH):
                            s.wait_ge(S.h["tns"], r_mm(i, o))
                            S.inc("scl", s.activation(rsb[:, o * 512:(o + 1) * 512],
                                                      psA[o % 2][:, :], AF.Sigmoid), 1)
                        for o in range(NCH):
                            s.wait_ge(S.h["tns"], k_mm(i, o))
                            if o >= 2:
                                s.wait_ge(S.h["vec"], kact_v(i, o - 2))
                            S.inc("scl", s.activation(krelu[:, (o % 2) * 512:(o % 2 + 1) * 512],
                                                      psA[o % 2][:, :], AF.Relu), 1)

                @block.vector
                def _(v):
                    for i in range(NT):
                        v.wait_ge(S.h["tns"], ab_mm(i))
                        v.tensor_copy(abx[:, 0:512], ps_ab[:, 0:512])
                        S.inc("vec", v.tensor_copy(abx[:, 512:1024], ps_ab[:, 512:1024]), 1)
                        v.wait_ge(S.h["ld0"], ld_done(i))
                        for c in range(NCH):
                            if i == 0:
                                v.memset(xnb[:, c * 513:c * 513 + 1], 0.0)
                            else:
                                v.tensor_copy(xnb[:, c * 513:c * 513 + 1],
                                              xnb[:, c * 513 + 512:c * 513 + 513])
                        for c in range(NCH):
                            v.tensor_tensor(tmpf[:, :], xsv(i, c), abx[:, 0:512], ALU.mult)
                            v.tensor_tensor(xnb[:, c * 513 + 1:c * 513 + 513], tmpf[:, :],
                                            abx[:, 512:1024], ALU.add)
                            xn_new = xnb[:, c * 513 + 1:c * 513 + 513]
                            xn_sh = xnb[:, c * 513:c * 513 + 512]
                            v.tensor_tensor(dbuf[:, :], xn_new, xn_sh, ALU.subtract)
                            v.scalar_tensor_tensor(xr[:, c * 512:(c + 1) * 512], dbuf[:, :],
                                                   muT[:, c:c + 1], xn_sh, ALU.mult, ALU.add)
                            inst = v.scalar_tensor_tensor(xk[:, c * 512:(c + 1) * 512], dbuf[:, :],
                                                          muT[:, 8 + c:8 + c + 1], xn_sh,
                                                          ALU.mult, ALU.add)
                        S.inc("vec", inst, 1)
                        for o in range(NCH):
                            v.wait_ge(S.h["scl"], relu(i, o))
                            S.inc("vec", v.tensor_tensor(kact[:, o * 512:(o + 1) * 512],
                                                         psA[o % 2][:, :],
                                                         krelu[:, (o % 2) * 512:(o % 2 + 1) * 512],
                                                         ALU.mult), 1)
                        for o in range(NCH):
                            v.wait_ge(S.h["tns"], v_mm(i, o))
                            v.tensor_tensor(tmpf[:, :], psA[o % 2][:, :],
                                            rsb[:, o * 512:(o + 1) * 512], ALU.mult)
                            S.inc("vec", v.scalar_tensor_tensor(xsv(i, o), tmpf[:, :], 0.5,
                                                                xsv(i, o), ALU.mult, ALU.add), 1)

            assert S.v["tns"] == B["tns"] + 25 * NT, (S.v["tns"], B["tns"])
            assert S.v["vec"] == B["vec"] + 18 * NT
            assert S.v["scl"] == B["scl"] + 16 * NT
            assert S.v["ld0"] == B["ld0"] + 128 * NT
            assert S.v["sto"] == B["sto"] + 128 * NT



ATTN_W = (("WQ", "wq_bf"), ("WK", "wk_bf2"), ("WV", "wv_bf2"), ("WO", "wo_bf"))


def emit_prep_l0attn(nc, D, S):
    with nc.Block() as block:
        @block.gpsimd
        def _(g):
            for src_name, dst in ATTN_W:
                S.marks[dst] = S.inc("cast", g.dma_start(out=D[dst], in_=D[src_name]), 16)


def emit_l0attn(nc, D, S):
    scr_in, scr_out = D["scr_in"], D["scr_out"]
    ident, onescol = D["cm_ident"], D["cm_onescol"]
    ident_bf = D["cm_ident_bf"]
    q_scr, k_scr, v_scr, a_scr = D["q_scr"], D["k_scr"], D["v_scr"], D["a_scr"]

    with ExitStack() as ctx:
        a_sb = ctx.enter_context(nc.sbuf_tensor("at_a", [1, T], F32))
        b_sb = ctx.enter_context(nc.sbuf_tensor("at_b", [1, T], F32))
        _emit_ln_stats(nc, D, S, ctx, scr_in, a_sb, b_sb, "at")

        wq_s = ctx.enter_context(nc.sbuf_tensor("at_wq", [128, NCH * C], BF16))
        wk_s = ctx.enter_context(nc.sbuf_tensor("at_wk", [128, NCH * C], BF16))
        wv_s = ctx.enter_context(nc.sbuf_tensor("at_wv", [128, NCH * C], BF16))
        wo_s = ctx.enter_context(nc.sbuf_tensor("at_wo", [128, NCH * C], BF16))
        bos = ctx.enter_context(nc.sbuf_tensor("at_bos", [1, C], F32))
        boT = ctx.enter_context(nc.sbuf_tensor("at_boT", [128, 8], F32))
        with ExitStack() as c2:
            ps_bo = c2.enter_context(nc.psum_tensor("at_psbo", [128, 8], F32))
            with nc.Block() as block:
                @block.sync
                def _(sync):
                    sync.wait_ge(S.h["cast"], S.marks["wo_bf"])
                    for (_, wname), buf in zip(ATTN_W, (wq_s, wk_s, wv_s, wo_s)):
                        for c in range(NCH):
                            S.inc("ld1", sync.dma_start(
                                out=buf[:, c * C:(c + 1) * C],
                                in_=D[wname][c * 128:(c + 1) * 128, :]), 16)
                    S.inc("ld1", sync.dma_start(out=bos[:, :], in_=D["bo"]), 16)

                @block.tensor
                def _(t):
                    t.wait_ge(S.h["ld1"], S.v["ld1"])
                    for j in range(8):
                        t.transpose(ps_bo[:, j:j + 1], bos[0:1, j * 128:(j + 1) * 128],
                                    ident[0:1, 0:1])
            with nc.Block() as block:
                @block.vector
                def _(v):
                    v.tensor_copy(boT[:, :], ps_bo[:, :])

        with ExitStack() as c2:
            xs = c2.enter_context(nc.sbuf_tensor("at_xs", [128, NCH * 512], F32))
            xn = c2.enter_context(nc.sbuf_tensor("at_xn", [128, NCH * 512], BF16))
            abx = c2.enter_context(nc.sbuf_tensor("at_abx", [128, 1024], F32))
            qt = c2.enter_context(nc.sbuf_tensor("at_qt", [128, NCH * 512], BF16))
            kt = c2.enter_context(nc.sbuf_tensor("at_kt", [128, NCH * 512], BF16))
            vt = c2.enter_context(nc.sbuf_tensor("at_vt", [128, 2 * 1024], BF16))
            tmpf = c2.enter_context(nc.sbuf_tensor("at_tmpf", [128, 512], F32))
            ps_ab = c2.enter_context(nc.psum_tensor("at_psab", [128, 1024], F32))
            psA = [c2.enter_context(nc.psum_tensor(f"at_psA{i}", [128, 512], F32)) for i in range(2)]
            psV = [c2.enter_context(nc.psum_tensor(f"at_psV{i}", [128, 1024], F32)) for i in range(2)]

            B = S.base("ld0", "tns", "vec", "scl", "sto")
            ld_done = lambda i: B["ld0"] + 128 * (i + 1)
            ab_mm = lambda i: B["tns"] + 21 * i + 1
            q_mm = lambda i, o: B["tns"] + 21 * i + 1 + (o + 1)
            k_mm = lambda i, o: B["tns"] + 21 * i + 9 + (o + 1)
            v_mm = lambda i, u: B["tns"] + 21 * i + 17 + (u + 1)
            qe = lambda i, o: B["scl"] + 16 * i + (o + 1)
            ke = lambda i, o: B["scl"] + 16 * i + 8 + (o + 1)
            ab_evac = lambda i: B["vec"] + 6 * i + 1
            apl = lambda i: B["vec"] + 6 * i + 2
            ve = lambda i, u: B["vec"] + 6 * i + 2 + (u + 1)
            sto_done = lambda i: B["sto"] + 320 * (i + 1)

            def xsv(c):
                return xs[:, c * 512:(c + 1) * 512]

            with nc.Block() as block:
                @block.sync
                def _(sync):
                    for c in range(NCH):
                        S.inc("ld0", sync.dma_start(
                            out=xsv(c),
                            in_=scr_in[c * 128:(c + 1) * 128, 0:512]), 16)
                    for i in range(NT):
                        if i + 1 < NT:
                            sync.wait_ge(S.h["vec"], apl(i))
                            for c in range(NCH):
                                S.inc("ld0", sync.dma_start(
                                    out=xsv(c),
                                    in_=scr_in[c * 128:(c + 1) * 128,
                                               (i + 1) * 512:(i + 2) * 512]), 16)
                        for o in range(NCH):
                            sync.wait_ge(S.h["scl"], qe(i, o))
                            S.inc("sto", sync.dma_start(
                                out=q_scr[o * 128:(o + 1) * 128, i * 512:(i + 1) * 512],
                                in_=qt[:, o * 512:(o + 1) * 512]), 16)
                        for o in range(NCH):
                            sync.wait_ge(S.h["scl"], ke(i, o))
                            S.inc("sto", sync.dma_start(
                                out=k_scr[o * 128:(o + 1) * 128, i * 512:(i + 1) * 512],
                                in_=kt[:, o * 512:(o + 1) * 512]), 16)
                        for u in range(4):
                            sync.wait_ge(S.h["vec"], ve(i, u))
                            S.inc("sto", sync.dma_start(
                                out=v_scr[i * 512 + u * 128: i * 512 + (u + 1) * 128, :],
                                in_=vt[:, (u % 2) * 1024:(u % 2 + 1) * 1024]), 16)
                    sync.wait_ge(S.h["sto"], sto_done(NT - 1))

                @block.tensor
                def _(t):
                    for i in range(NT):
                        if i >= 1:
                            t.wait_ge(S.h["vec"], ab_evac(i - 1))
                        t.matmul(ps_ab[:, 0:512], onescol[0:1, :], a_sb[:, i * 512:(i + 1) * 512],
                                 start=True, stop=True)
                        inst = t.matmul(ps_ab[:, 512:1024], onescol[0:1, :],
                                        b_sb[:, i * 512:(i + 1) * 512], start=True, stop=True)
                        S.inc("tns", inst, 1)
                        t.wait_ge(S.h["vec"], apl(i))
                        for o in range(NCH):
                            if o >= 2:
                                t.wait_ge(S.h["scl"], qe(i, o - 2))
                            elif i > 0:
                                t.wait_ge(S.h["scl"], ke(i - 1, o + 6))
                            for c in range(NCH):
                                inst = t.matmul(psA[o % 2][:, :],
                                                wq_s[:, c * C + o * 128: c * C + (o + 1) * 128],
                                                xn[:, c * 512:(c + 1) * 512],
                                                start=(c == 0), stop=(c == NCH - 1))
                            S.inc("tns", inst, 1)
                        for o in range(NCH):
                            t.wait_ge(S.h["scl"], qe(i, o + 6) if o < 2 else ke(i, o - 2))
                            for c in range(NCH):
                                inst = t.matmul(psA[o % 2][:, :],
                                                wk_s[:, c * C + o * 128: c * C + (o + 1) * 128],
                                                xn[:, c * 512:(c + 1) * 512],
                                                start=(c == 0), stop=(c == NCH - 1))
                            S.inc("tns", inst, 1)
                        for u in range(4):
                            if u >= 2:
                                t.wait_ge(S.h["vec"], ve(i, u - 2))
                            elif i > 0:
                                t.wait_ge(S.h["vec"], ve(i - 1, u + 2))
                            for c in range(NCH):
                                for nn in range(2):
                                    inst = t.matmul(
                                        psV[u % 2][:, nn * 512:(nn + 1) * 512],
                                        xn[:, c * 512 + u * 128: c * 512 + (u + 1) * 128],
                                        wv_s[:, c * C + nn * 512: c * C + (nn + 1) * 512],
                                        start=(c == 0), stop=(c == NCH - 1))
                            S.inc("tns", inst, 1)

                @block.scalar
                def _(s):
                    for i in range(NT):
                        for o in range(NCH):
                            s.wait_ge(S.h["tns"], q_mm(i, o))
                            if i > 0:
                                s.wait_ge(S.h["sto"], B["sto"] + 320 * i - 320 + (o + 1) * 16)
                            S.inc("scl", s.copy(qt[:, o * 512:(o + 1) * 512], psA[o % 2][:, :]), 1)
                        for o in range(NCH):
                            s.wait_ge(S.h["tns"], k_mm(i, o))
                            if i > 0:
                                s.wait_ge(S.h["sto"], B["sto"] + 320 * i - 192 + (o + 1) * 16)
                            S.inc("scl", s.copy(kt[:, o * 512:(o + 1) * 512], psA[o % 2][:, :]), 1)

                @block.vector
                def _(v):
                    for i in range(NT):
                        v.wait_ge(S.h["tns"], ab_mm(i))
                        v.tensor_copy(abx[:, 0:512], ps_ab[:, 0:512])
                        S.inc("vec", v.tensor_copy(abx[:, 512:1024], ps_ab[:, 512:1024]), 1)
                        v.wait_ge(S.h["ld0"], ld_done(i))
                        for c in range(NCH):
                            v.tensor_tensor(tmpf[:, :], xsv(c), abx[:, 0:512], ALU.mult)
                            inst = v.tensor_tensor(xn[:, c * 512:(c + 1) * 512], tmpf[:, :],
                                                   abx[:, 512:1024], ALU.add)
                        S.inc("vec", inst, 1)
                        for u in range(4):
                            v.wait_ge(S.h["tns"], v_mm(i, u))
                            if i > 0 or u >= 2:
                                pi, pu = (i, u - 2) if u >= 2 else (i - 1, u + 2)
                                v.wait_ge(S.h["sto"], B["sto"] + 320 * pi + 256 + (pu + 1) * 16)
                            S.inc("vec", v.tensor_copy(vt[:, (u % 2) * 1024:(u % 2 + 1) * 1024],
                                                       psV[u % 2][:, :]), 1)

            assert S.v["tns"] == B["tns"] + 21 * NT
            assert S.v["vec"] == B["vec"] + 6 * NT
            assert S.v["scl"] == B["scl"] + 16 * NT
            assert S.v["sto"] == B["sto"] + 320 * NT
        A_store_done = S.v["sto"]

        with ExitStack() as c2:
            qh = [c2.enter_context(nc.sbuf_tensor(f"at_qh{i}", [64, T], BF16)) for i in range(2)]
            kh = [c2.enter_context(nc.sbuf_tensor(f"at_kh{i}", [64, T], BF16)) for i in range(2)]
            vh = [c2.enter_context(nc.sbuf_tensor(f"at_vh{i}", [128, 32 * 65], BF16)) for i in range(2)]
            pb = [c2.enter_context(nc.sbuf_tensor(f"at_p{i}", [128, 1024], BF16)) for i in range(4)]
            attn = c2.enter_context(nc.sbuf_tensor("at_attn", [64, 1024], BF16))
            recs = c2.enter_context(nc.sbuf_tensor("at_recs", [1, 512], F32))
            recx = c2.enter_context(nc.sbuf_tensor("at_recx", [64, 512], F32))
            atok = c2.enter_context(nc.sbuf_tensor("at_atok", [128, 512], BF16))
            mask = c2.enter_context(nc.sbuf_tensor("at_mask", [128, 1920], F32))
            ps_sc = [c2.enter_context(nc.psum_tensor(f"at_sc{i}", [128, 1024], F32)) for i in range(2)]
            ps_at = [c2.enter_context(nc.psum_tensor(f"at_at{i}", [65, 512], F32)) for i in range(2)]
            ps_rec = c2.enter_context(nc.psum_tensor("at_rec", [64, 512], F32))
            ps_trb = c2.enter_context(nc.psum_tensor("at_tr", [128, 128], BF16))
            ps_tr = [ps_trb[:, 0:64], ps_trb[:, 64:128]]

            with nc.Block() as block:
                @block.gpsimd
                def _(g):
                    g.memset(mask[:, :], 1.0)
                    g.affine_select(out=mask[:, :], in_=mask[:, :], compare_op=ALU.is_ge,
                                    fill=0.0, base=-896, pattern=[[1, 1920]], channel_multiplier=-1)
                    g.memset(vh[0][:, :], 1.0)
                    g.memset(vh[1][:, :], 1.0)

            B = S.base("ld0", "tns", "vec", "scl", "sto")
            KV = [8 * (qb + 1) for qb in range(4)]

            sc_t = {}; pv_t = {}; exp_t = {}; msk_t = {}; rec_t = {}; rex_t = {}
            rcx_t = {}; attn_t = {}; trm_t = {}; tre_t = {}
            tv = B["tns"]; sv = B["scl"]; vv = B["vec"]
            for h in range(H):
                for qb in range(4):
                    nk = KV[qb]
                    for j in range(nk // 2):
                        tv += 1; sc_t[(h, qb, 2 * j)] = tv
                        tv += 1; sc_t[(h, qb, 2 * j + 1)] = tv
                        if j >= 1:
                            tv += 2; pv_t[(h, qb, 2 * j - 2)] = tv
                            tv += 2; pv_t[(h, qb, 2 * j - 1)] = tv
                    tv += 2; pv_t[(h, qb, nk - 2)] = tv
                    tv += 2; pv_t[(h, qb, nk - 1)] = tv
                    for nn in range(2):
                        tv += 1; rex_t[(h, qb, nn)] = tv
                    for j in range(8):
                        tv += 1; trm_t[(h, qb, j)] = tv
                    for kc in range(nk):
                        sv += 1; exp_t[(h, qb, kc)] = sv
                    for kc in range(qb * 8, nk):
                        vv += 1; msk_t[(h, qb, kc)] = vv
                    for nn in range(2):
                        vv += 1; rec_t[(h, qb, nn)] = vv
                        vv += 1; rcx_t[(h, qb, nn)] = vv
                        vv += 1; attn_t[(h, qb, nn)] = vv
                    for j in range(8):
                        vv += 1; tre_t[(h, qb, j)] = vv
            T_END, S_END, V_END = tv, sv, vv

            def prev_exp(h, qb, kc):
                idx = kc - 2
                while idx < 0:
                    if qb > 0:
                        qb -= 1; idx += KV[qb]
                    elif h > 0:
                        h -= 1; qb = 3; idx += KV[3]
                    else:
                        return None
                return exp_t[(h, qb, idx)]

            def prev_pv(h, qb, kc):
                idx = kc - 4
                while idx < 0:
                    if qb > 0:
                        qb -= 1; idx += KV[qb]
                    elif h > 0:
                        h -= 1; qb = 3; idx += KV[3]
                    else:
                        return None
                return pv_t[(h, qb, idx)]

            with nc.Block() as block:
                @block.sync
                def _(sync):
                    sync.wait_ge(S.h["sto"], A_store_done)

                    def head_loads(h):
                        S.inc("ld0", sync.dma_start(out=qh[h % 2][:, :],
                                                    in_=q_scr[h * 64:(h + 1) * 64, :]), 16)
                        S.inc("ld0", sync.dma_start(out=kh[h % 2][:, :],
                                                    in_=k_scr[h * 64:(h + 1) * 64, :]), 16)
                        S.inc("ld0", sync.dma_start(
                            out=vh[h % 2].rearrange("p (k w) -> p k w", w=65)[:, :, 0:64],
                            in_=v_scr[:, h * 64:(h + 1) * 64].rearrange("(k p) w -> p k w", p=128)), 16)

                    head_loads(0)
                    head_loads(1)
                    for h in range(H):
                        for qb in range(4):
                            sync.wait_ge(S.h["vec"], tre_t[(h, qb, 7)])
                            S.inc("sto", sync.dma_start(
                                out=a_scr[(h * T + qb * 1024):(h * T + (qb + 1) * 1024), :]
                                    .rearrange("(j p) w -> p j w", p=128),
                                in_=atok.rearrange("p (j w) -> p j w", w=64)), 16)
                        if h + 2 < H:
                            sync.wait_ge(S.h["tns"], pv_t[(h, 3, KV[3] - 1)])
                            head_loads(h + 2)
                    sync.wait_ge(S.h["sto"], S.v["sto"])

                @block.tensor
                def _(t):
                    for h in range(H):
                        t.wait_ge(S.h["ld0"], B["ld0"] + 48 * (h + 1))
                        for qb in range(4):
                            nk = KV[qb]

                            def emit_sc(kc):
                                pe = prev_exp(h, qb, kc)
                                if pe is not None:
                                    t.wait_ge(S.h["scl"], pe)
                                for nn in range(2):
                                    inst = t.matmul(
                                        ps_sc[kc % 2][:, nn * 512:(nn + 1) * 512],
                                        kh[h % 2][:, kc * 128:(kc + 1) * 128],
                                        qh[h % 2][:, qb * 1024 + nn * 512: qb * 1024 + (nn + 1) * 512],
                                        start=True, stop=True)
                                S.inc("tns", inst, 1)

                            def emit_pv(pkc):
                                t.wait_ge(S.h["scl"], exp_t[(h, qb, pkc)])
                                if pkc >= qb * 8:
                                    t.wait_ge(S.h["vec"], msk_t[(h, qb, pkc)])
                                for nn in range(2):
                                    inst = t.matmul(ps_at[nn][:, :],
                                                    vh[h % 2][:, pkc * 65:(pkc + 1) * 65],
                                                    pb[pkc % 4][:, nn * 512:(nn + 1) * 512],
                                                    start=(pkc == 0), stop=(pkc == nk - 1))
                                    S.inc("tns", inst, 1)

                            for j in range(nk // 2):
                                emit_sc(2 * j)
                                emit_sc(2 * j + 1)
                                if j >= 1:
                                    emit_pv(2 * j - 2)
                                    emit_pv(2 * j - 1)
                            emit_pv(nk - 2)
                            emit_pv(nk - 1)
                            for nn in range(2):
                                t.wait_ge(S.h["vec"], rec_t[(h, qb, nn)])
                                if nn == 1:
                                    t.wait_ge(S.h["vec"], rcx_t[(h, qb, 0)])
                                S.inc("tns", t.matmul(ps_rec[:, :], onescol[0:1, 0:64],
                                                      recs[:, :], start=True, stop=True), 1)
                            t.wait_ge(S.h["vec"], attn_t[(h, qb, 1)])
                            for j in range(8):
                                if j >= 2:
                                    t.wait_ge(S.h["vec"], tre_t[(h, qb, j - 2)])
                                elif (h, qb) != (0, 0):
                                    ph, pqb = (h, qb - 1) if qb > 0 else (h - 1, 3)
                                    t.wait_ge(S.h["vec"], tre_t[(ph, pqb, j + 6)])
                                S.inc("tns", t.transpose(ps_tr[j % 2],
                                                         attn[:, j * 128:(j + 1) * 128],
                                                         ident_bf[0:64, 0:64]), 1)

                @block.scalar
                def _(s):
                    for h in range(H):
                        for qb in range(4):
                            nk = KV[qb]
                            for kc in range(nk):
                                s.wait_ge(S.h["tns"], sc_t[(h, qb, kc)])
                                ppv = prev_pv(h, qb, kc)
                                if ppv is not None:
                                    s.wait_ge(S.h["tns"], ppv)
                                S.inc("scl", s.activation(pb[kc % 4][:, :], ps_sc[kc % 2][:, :],
                                                          AF.Exp, scale=0.125), 1)

                @block.vector
                def _(v):
                    for h in range(H):
                        for qb in range(4):
                            nk = KV[qb]
                            for kc in range(qb * 8, nk):
                                v.wait_ge(S.h["scl"], exp_t[(h, qb, kc)])
                                dd = kc * 128 - qb * 1024
                                S.inc("vec", v.tensor_tensor(
                                    pb[kc % 4][:, :], pb[kc % 4][:, :],
                                    mask[:, 896 - dd:896 - dd + 1024], ALU.mult), 1)
                            for nn in range(2):
                                v.wait_ge(S.h["tns"], pv_t[(h, qb, nk - 1)])
                                S.inc("vec", v.reciprocal(recs[:, :], ps_at[nn][64:65, :]), 1)
                                v.wait_ge(S.h["tns"], rex_t[(h, qb, nn)])
                                S.inc("vec", v.tensor_copy(recx[:, :], ps_rec[:, :]), 1)
                                S.inc("vec", v.tensor_tensor(attn[:, nn * 512:(nn + 1) * 512],
                                                             ps_at[nn][0:64, :], recx[:, :],
                                                             ALU.mult), 1)
                            for j in range(8):
                                v.wait_ge(S.h["tns"], trm_t[(h, qb, j)])
                                if (h, qb) != (0, 0) and j == 0:
                                    n_prev = h * 4 + qb
                                    v.wait_ge(S.h["sto"], B["sto"] + n_prev * 16)
                                S.inc("vec", v.tensor_copy(atok[:, j * 64:(j + 1) * 64],
                                                           ps_tr[j % 2]), 1)

            assert S.v["tns"] == T_END, (S.v["tns"], T_END)
            assert S.v["scl"] == S_END
            assert S.v["vec"] == V_END
        B_store_done = S.v["sto"]

        with ExitStack() as c2:
            xs2 = [c2.enter_context(nc.sbuf_tensor(f"atc_xs{j}", [128, NCH * 512], F32))
                   for j in range(2)]
            at2 = [c2.enter_context(nc.sbuf_tensor(f"atc_at{j}", [128, NCH * 512], BF16))
                   for j in range(2)]
            psA = [c2.enter_context(nc.psum_tensor(f"atc_ps{i}", [128, 512], F32)) for i in range(2)]
            a_view = a_scr.rearrange("(s g) w -> s (g w)", g=16)

            B = S.base("ld0", "tns", "vec", "sto")
            ld_done = lambda i: B["ld0"] + 256 * (i + 1)
            wo_mm = lambda i, o: B["tns"] + 8 * i + (o + 1)
            upd = lambda i, o: B["vec"] + 8 * i + (o + 1)
            sto_done = lambda i: B["sto"] + 128 * (i + 1)

            with nc.Block() as block:
                @block.sync
                def _(sync):
                    sync.wait_ge(S.h["sto"], B_store_done)
                    for i in range(NT):
                        if i == 0:
                            todo = [0, 1]
                        elif i + 1 < NT:
                            todo = [i + 1]
                        else:
                            todo = []
                        for j in todo:
                            if j >= 2:
                                sync.wait_ge(S.h["sto"], sto_done(j - 2))
                                sync.wait_ge(S.h["tns"], wo_mm(j - 2, NCH - 1))
                            for c in range(NCH):
                                S.inc("ld0", sync.dma_start(
                                    out=at2[j % 2][:, c * 512:(c + 1) * 512],
                                    in_=a_view[j * 512:(j + 1) * 512, c * 128:(c + 1) * 128],
                                    transpose=True), 16)
                            for c in range(NCH):
                                S.inc("ld0", sync.dma_start(
                                    out=xs2[j % 2][:, c * 512:(c + 1) * 512],
                                    in_=scr_in[c * 128:(c + 1) * 128, j * 512:(j + 1) * 512]), 16)
                        for o in range(NCH):
                            sync.wait_ge(S.h["vec"], upd(i, o))
                            S.inc("sto", sync.dma_start(
                                out=scr_out[o * 128:(o + 1) * 128, i * 512:(i + 1) * 512],
                                in_=xs2[i % 2][:, o * 512:(o + 1) * 512]), 16)
                    sync.wait_ge(S.h["sto"], sto_done(NT - 1))

                @block.tensor
                def _(t):
                    for i in range(NT):
                        t.wait_ge(S.h["ld0"], B["ld0"] + 256 * (i + 1))
                        for o in range(NCH):
                            if o >= 2:
                                t.wait_ge(S.h["vec"], upd(i, o - 2))
                            elif i > 0:
                                t.wait_ge(S.h["vec"], upd(i - 1, o + 6))
                            for c in range(NCH):
                                inst = t.matmul(psA[o % 2][:, :],
                                                wo_s[:, c * C + o * 128: c * C + (o + 1) * 128],
                                                at2[i % 2][:, c * 512:(c + 1) * 512],
                                                start=(c == 0), stop=(c == NCH - 1))
                            S.inc("tns", inst, 1)

                @block.vector
                def _(v):
                    for i in range(NT):
                        for o in range(NCH):
                            v.wait_ge(S.h["tns"], wo_mm(i, o))
                            S.inc("vec", v.scalar_tensor_tensor(
                                xs2[i % 2][:, o * 512:(o + 1) * 512], psA[o % 2][:, :],
                                boT[:, o:o + 1], xs2[i % 2][:, o * 512:(o + 1) * 512],
                                ALU.add, ALU.add), 1)

            assert S.v["tns"] == B["tns"] + 8 * NT
            assert S.v["vec"] == B["vec"] + 8 * NT




def emit_prep_l1attn(nc, D, S):
    with nc.Block() as block:
        @block.gpsimd
        def _(g):
            S.marks["wqkv_bf"] = S.inc("cast", g.dma_start(out=D["wqkv_bf"], in_=D["WQKV"]), 16)
            S.marks["wproj_bf"] = S.inc("cast", g.dma_start(out=D["wproj_bf"], in_=D["WPROJ"]), 16)


def emit_l1attn(nc, D, S):
    scr_in, scr_out = D["scr_in"], D["scr_out"]
    ident, onescol, ones_bf = D["cm_ident"], D["cm_onescol"], D["cm_ones_bf"]

    with ExitStack() as ctx:
        a_sb = ctx.enter_context(nc.sbuf_tensor("la_a", [1, T], F32))
        b_sb = ctx.enter_context(nc.sbuf_tensor("la_b", [1, T], F32))
        _emit_ln_stats(nc, D, S, ctx, scr_in, a_sb, b_sb, "la")

        wqkvT = ctx.enter_context(nc.sbuf_tensor("la_wqkv", [128, NCH * 3072], BF16))
        wprojT = ctx.enter_context(nc.sbuf_tensor("la_wproj", [128, NCH * C], BF16))
        bps = ctx.enter_context(nc.sbuf_tensor("la_bps", [1, C], F32))
        bpT = ctx.enter_context(nc.sbuf_tensor("la_bpT", [128, 8], F32))
        ones2 = ctx.enter_context(nc.sbuf_tensor("la_ones2", [128, 2], BF16))
        ind2 = ctx.enter_context(nc.sbuf_tensor("la_ind2", [2, 128], BF16))
        ctx_s = ctx.enter_context(nc.sbuf_tensor("la_ctx", [128, 512], BF16))
        S_sb = ctx.enter_context(nc.sbuf_tensor("la_S", [1, 1024], F32))
        srecT = ctx.enter_context(nc.sbuf_tensor("la_srT", [128, 8], F32))

        with ExitStack() as c2:
            ps_bp = c2.enter_context(nc.psum_tensor("la_psbp", [128, 8], F32))
            with nc.Block() as block:
                @block.sync
                def _(sync):
                    sync.wait_ge(S.h["cast"], S.marks["wqkv_bf"])
                    for c in range(NCH):
                        S.inc("ld1", sync.dma_start(out=wqkvT[:, c * 3072:(c + 1) * 3072],
                                                    in_=D["wqkv_bf"][:, c * 128:(c + 1) * 128],
                                                    transpose=True), 16)
                    sync.wait_ge(S.h["cast"], S.marks["wproj_bf"])
                    for c in range(NCH):
                        S.inc("ld1", sync.dma_start(out=wprojT[:, c * C:(c + 1) * C],
                                                    in_=D["wproj_bf"][:, c * 128:(c + 1) * 128],
                                                    transpose=True), 16)
                    S.inc("ld1", sync.dma_start(out=bps[:, :], in_=D["b_proj"]), 16)

                @block.gpsimd
                def _(g):
                    g.memset(ones2[:, :], 0.0)
                    g.memset(ones2[0:64, 0:1], 1.0)
                    g.memset(ones2[64:128, 1:2], 1.0)
                    g.memset(ind2[:, :], 1.0)
                    g.affine_select(out=ind2[:, :], in_=ind2[:, :], compare_op=ALU.is_ge,
                                    fill=0.0, base=0, pattern=[[1, 128]], channel_multiplier=-64)
                    g.affine_select(out=ind2[:, :], in_=ind2[:, :], compare_op=ALU.is_ge,
                                    fill=0.0, base=63, pattern=[[-1, 128]], channel_multiplier=64)
                    g.memset(S_sb[:, :], 0.0)

                @block.tensor
                def _(t):
                    t.wait_ge(S.h["ld1"], S.v["ld1"])
                    for j in range(8):
                        t.transpose(ps_bp[:, j:j + 1], bps[0:1, j * 128:(j + 1) * 128],
                                    ident[0:1, 0:1])
            with nc.Block() as block:
                @block.vector
                def _(v):
                    v.tensor_copy(bpT[:, :], ps_bp[:, :])

        with ExitStack() as c2:
            xs = c2.enter_context(nc.sbuf_tensor("la_xs", [128, NCH * 512], F32))
            xn = c2.enter_context(nc.sbuf_tensor("la_xn", [128, NCH * 512], BF16))
            abx = c2.enter_context(nc.sbuf_tensor("la_abx", [128, 1024], F32))
            tmpf = c2.enter_context(nc.sbuf_tensor("la_tmpf", [128, 512], F32))
            ke_s = [c2.enter_context(nc.sbuf_tensor(f"la_ke{i}", [128, 1024], BF16)) for i in range(2)]
            ve_s = [c2.enter_context(nc.sbuf_tensor(f"la_ve{i}", [128, 1024], BF16)) for i in range(2)]
            ps_ab = c2.enter_context(nc.psum_tensor("la_psab", [128, 1024], F32))
            psK = c2.enter_context(nc.psum_tensor("la_psK", [128, 1024], F32))
            psV = c2.enter_context(nc.psum_tensor("la_psV", [128, 1024], F32))
            ps_one = c2.enter_context(nc.psum_tensor("la_pso", [1, 512], F32))
            ps_ctx = c2.enter_context(nc.psum_tensor("la_psctx", [128, 512], F32))

            B = S.base("ld0", "tns", "vec", "scl", "sto")
            ld_done = lambda i: B["ld0"] + 128 * (i + 1)
            ab_mm = lambda i: B["tns"] + 21 * i + 1
            k_mm = lambda i, u: B["tns"] + 21 * i + 1 + 5 * u + 1
            v_mm = lambda i, u: B["tns"] + 21 * i + 1 + 5 * u + 2
            one_mm = lambda i, u, nh: B["tns"] + 21 * i + 1 + 5 * u + 3 + nh
            ctx_mm = lambda i, u: B["tns"] + 21 * i + 1 + 5 * u + 5
            kexp = lambda i, u: B["scl"] + 8 * i + 2 * u + 1
            vevac = lambda i, u: B["scl"] + 8 * i + 2 * u + 2
            ab_evac = lambda i: B["vec"] + 10 * i + 1
            apl = lambda i: B["vec"] + 10 * i + 2
            sadd = lambda i, u, nh: B["vec"] + 10 * i + 2 + 2 * u + 1 + nh

            with nc.Block() as block:
                @block.sync
                def _(sync):
                    for c in range(NCH):
                        S.inc("ld0", sync.dma_start(
                            out=xs[:, c * 512:(c + 1) * 512],
                            in_=scr_in[c * 128:(c + 1) * 128, 0:512]), 16)
                    for i in range(NT - 1):
                        sync.wait_ge(S.h["vec"], apl(i))
                        for c in range(NCH):
                            S.inc("ld0", sync.dma_start(
                                out=xs[:, c * 512:(c + 1) * 512],
                                in_=scr_in[c * 128:(c + 1) * 128, (i + 1) * 512:(i + 2) * 512]), 16)

                @block.tensor
                def _(t):
                    for i in range(NT):
                        if i >= 1:
                            t.wait_ge(S.h["vec"], ab_evac(i - 1))
                        t.matmul(ps_ab[:, 0:512], onescol[0:1, :], a_sb[:, i * 512:(i + 1) * 512],
                                 start=True, stop=True)
                        inst = t.matmul(ps_ab[:, 512:1024], onescol[0:1, :],
                                        b_sb[:, i * 512:(i + 1) * 512], start=True, stop=True)
                        S.inc("tns", inst, 1)
                        t.wait_ge(S.h["vec"], apl(i))
                        for u in range(4):
                            if i > 0 or u > 0:
                                pi, pu = (i, u - 1) if u > 0 else (i - 1, 3)
                                t.wait_ge(S.h["scl"], kexp(pi, pu))
                            for c in range(NCH):
                                for nn in range(2):
                                    inst = t.matmul(
                                        psK[:, nn * 512:(nn + 1) * 512],
                                        xn[:, c * 512 + u * 128: c * 512 + (u + 1) * 128],
                                        wqkvT[:, c * 3072 + 1024 + nn * 512: c * 3072 + 1024 + (nn + 1) * 512],
                                        start=(c == 0), stop=(c == NCH - 1))
                            S.inc("tns", inst, 1)
                            if i > 0 or u > 0:
                                pi, pu = (i, u - 1) if u > 0 else (i - 1, 3)
                                t.wait_ge(S.h["scl"], vevac(pi, pu))
                            for c in range(NCH):
                                for nn in range(2):
                                    inst = t.matmul(
                                        psV[:, nn * 512:(nn + 1) * 512],
                                        xn[:, c * 512 + u * 128: c * 512 + (u + 1) * 128],
                                        wqkvT[:, c * 3072 + 2048 + nn * 512: c * 3072 + 2048 + (nn + 1) * 512],
                                        start=(c == 0), stop=(c == NCH - 1))
                            S.inc("tns", inst, 1)
                            t.wait_ge(S.h["scl"], kexp(i, u))
                            for nh in range(2):
                                if i > 0 or u > 0 or nh > 0:
                                    if nh > 0:
                                        t.wait_ge(S.h["vec"], sadd(i, u, 0))
                                    else:
                                        pi, pu = (i, u - 1) if u > 0 else (i - 1, 3)
                                        t.wait_ge(S.h["vec"], sadd(pi, pu, 1))
                                S.inc("tns", t.matmul(ps_one[:, :], ones_bf[:, :],
                                                      ke_s[u % 2][:, nh * 512:(nh + 1) * 512],
                                                      start=True, stop=True), 1)
                            t.wait_ge(S.h["scl"], vevac(i, u))
                            first = (i == 0 and u == 0)
                            last = (i == NT - 1 and u == 3)
                            for hh in range(H):
                                inst = t.matmul(
                                    ps_ctx[(hh % 2) * 64:(hh % 2) * 64 + 64,
                                           (hh // 2) * 64:(hh // 2) * 64 + 64],
                                    ke_s[u % 2][:, hh * 64:(hh + 1) * 64],
                                    ve_s[u % 2][:, hh * 64:(hh + 1) * 64],
                                    start=(first and hh == 0), stop=(last and hh == H - 1),
                                    skip_group_check=True)
                            S.inc("tns", inst, 1)

                @block.vector
                def _(v):
                    for i in range(NT):
                        v.wait_ge(S.h["tns"], ab_mm(i))
                        v.tensor_copy(abx[:, 0:512], ps_ab[:, 0:512])
                        S.inc("vec", v.tensor_copy(abx[:, 512:1024], ps_ab[:, 512:1024]), 1)
                        v.wait_ge(S.h["ld0"], ld_done(i))
                        for c in range(NCH):
                            v.tensor_tensor(tmpf[:, :], xs[:, c * 512:(c + 1) * 512],
                                            abx[:, 0:512], ALU.mult)
                            inst = v.tensor_tensor(xn[:, c * 512:(c + 1) * 512], tmpf[:, :],
                                                   abx[:, 512:1024], ALU.add)
                        S.inc("vec", inst, 1)
                        for u in range(4):
                            for nh in range(2):
                                v.wait_ge(S.h["tns"], one_mm(i, u, nh))
                                S.inc("vec", v.tensor_tensor(
                                    S_sb[:, nh * 512:(nh + 1) * 512],
                                    S_sb[:, nh * 512:(nh + 1) * 512],
                                    ps_one[:, :], ALU.add), 1)

                @block.scalar
                def _(s):
                    for i in range(NT):
                        for u in range(4):
                            s.wait_ge(S.h["tns"], k_mm(i, u))
                            if i > 0 or u >= 2:
                                pi, pu = (i, u - 2) if u >= 2 else (i - 1, u + 2)
                                s.wait_ge(S.h["tns"], ctx_mm(pi, pu))
                                s.wait_ge(S.h["tns"], one_mm(pi, pu, 1))
                            S.inc("scl", s.activation(ke_s[u % 2][:, :], psK[:, :], AF.Exp), 1)
                            s.wait_ge(S.h["tns"], v_mm(i, u))
                            S.inc("scl", s.copy(ve_s[u % 2][:, :], psV[:, :]), 1)

            assert S.v["tns"] == B["tns"] + 21 * NT
            assert S.v["vec"] == B["vec"] + 10 * NT
            assert S.v["scl"] == B["scl"] + 8 * NT

            with nc.Block() as block:
                @block.tensor
                def _(t):
                    for j in range(8):
                        t.transpose(ps_ab[:, j:j + 1], S_sb[0:1, j * 128:(j + 1) * 128],
                                    ident[0:1, 0:1])
            with nc.Block() as block:
                @block.vector
                def _(v):
                    v.reciprocal(srecT[:, :], ps_ab[:, 0:8])
            with nc.Block() as block:
                @block.vector
                def _(v):
                    for j in range(8):
                        v.tensor_scalar(ctx_s[:, j * 64:(j + 1) * 64],
                                        ps_ctx[:, j * 64:(j + 1) * 64],
                                        srecT[:, j:j + 1], None, ALU.mult)

        with ExitStack() as c2:
            xs2b = [c2.enter_context(nc.sbuf_tensor(f"lb_xs{j}", [128, NCH * 512], F32))
                    for j in range(2)]
            xn = c2.enter_context(nc.sbuf_tensor("lb_xn", [128, NCH * 512], BF16))
            abx = c2.enter_context(nc.sbuf_tensor("lb_abx", [128, 1024], F32))
            tmpf = c2.enter_context(nc.sbuf_tensor("lb_tmpf", [128, 512], F32))
            qe_s = c2.enter_context(nc.sbuf_tensor("lb_qe", [128, NCH * 512], BF16))
            an_s = c2.enter_context(nc.sbuf_tensor("lb_an", [128, NCH * 512], BF16))
            qS_sb = c2.enter_context(nc.sbuf_tensor("lb_qS", [2, T], F32))
            rc_bf = c2.enter_context(nc.sbuf_tensor("lb_rc", [2, T], BF16))
            ps_ab = c2.enter_context(nc.psum_tensor("lb_psab", [128, 1024], F32))
            P0 = c2.enter_context(nc.psum_tensor("lb_P0", [128, 512], F32))
            P1 = c2.enter_context(nc.psum_tensor("lb_P1", [128, 512], F32))
            PP = [P0, P1]
            ps_qS = c2.enter_context(nc.psum_tensor("lb_qSp", [2, 512], F32))
            ps_ex = c2.enter_context(nc.psum_tensor("lb_exp", [128, 512], F32))
            PA = [c2.enter_context(nc.psum_tensor(f"lb_PA{i}", [128, 512], F32)) for i in range(2)]

            B = S.base("ld0", "tns", "vec", "scl", "sto")
            ld_done = lambda i: B["ld0"] + 128 * (i + 1)
            ab_mm = lambda i: B["tns"] + 41 * i + 1
            q_mm = lambda i, o: B["tns"] + 41 * i + 1 + (o + 1)
            qs_mm = lambda i, o: B["tns"] + 41 * i + 9 + (o + 1)
            ex_mm = lambda i, o: B["tns"] + 41 * i + 17 + (o + 1)
            at_mm = lambda i, o: B["tns"] + 41 * i + 25 + (o + 1)
            pj_mm = lambda i, o: B["tns"] + 41 * i + 33 + (o + 1)
            exp2 = lambda i, o: B["scl"] + 8 * i + (o + 1)
            ab_evac = lambda i: B["vec"] + 35 * i + 1
            apl = lambda i: B["vec"] + 35 * i + 2
            qsev = lambda i, o: B["vec"] + 35 * i + 2 + (o + 1)
            rcp = lambda i: B["vec"] + 35 * i + 11
            nrm = lambda i, o: B["vec"] + 35 * i + 11 + (o + 1)
            aev = lambda i, o: B["vec"] + 35 * i + 19 + (o + 1)
            upd = lambda i, o: B["vec"] + 35 * i + 27 + (o + 1)
            sto_done = lambda i: B["sto"] + 128 * (i + 1)

            with nc.Block() as block:
                @block.sync
                def _(sync):
                    for i in range(NT):
                        if i == 0:
                            todo = [0, 1]
                        elif i + 1 < NT:
                            todo = [i + 1]
                        else:
                            todo = []
                        for j in todo:
                            if j >= 2:
                                sync.wait_ge(S.h["sto"], sto_done(j - 2))
                            for c in range(NCH):
                                S.inc("ld0", sync.dma_start(
                                    out=xs2b[j % 2][:, c * 512:(c + 1) * 512],
                                    in_=scr_in[c * 128:(c + 1) * 128, j * 512:(j + 1) * 512]), 16)
                        for o in range(NCH):
                            sync.wait_ge(S.h["vec"], upd(i, o))
                            S.inc("sto", sync.dma_start(
                                out=scr_out[o * 128:(o + 1) * 128, i * 512:(i + 1) * 512],
                                in_=xs2b[i % 2][:, o * 512:(o + 1) * 512]), 16)
                    sync.wait_ge(S.h["sto"], sto_done(NT - 1))

                @block.tensor
                def _(t):
                    for i in range(NT):
                        if i >= 1:
                            t.wait_ge(S.h["vec"], ab_evac(i - 1))
                        t.matmul(ps_ab[:, 0:512], onescol[0:1, :], a_sb[:, i * 512:(i + 1) * 512],
                                 start=True, stop=True)
                        inst = t.matmul(ps_ab[:, 512:1024], onescol[0:1, :],
                                        b_sb[:, i * 512:(i + 1) * 512], start=True, stop=True)
                        S.inc("tns", inst, 1)
                        t.wait_ge(S.h["vec"], apl(i))
                        for o in range(NCH):
                            if o >= 2:
                                t.wait_ge(S.h["scl"], exp2(i, o - 2))
                            elif i > 0:
                                t.wait_ge(S.h["vec"], upd(i - 1, o + 6))
                            for c in range(NCH):
                                inst = t.matmul(PP[o % 2][:, :],
                                                wqkvT[:, c * 3072 + o * 128: c * 3072 + (o + 1) * 128],
                                                xn[:, c * 512:(c + 1) * 512],
                                                start=(c == 0), stop=(c == NCH - 1))
                            S.inc("tns", inst, 1)
                        for o in range(NCH):
                            t.wait_ge(S.h["scl"], exp2(i, o))
                            if o >= 1:
                                t.wait_ge(S.h["vec"], qsev(i, o - 1))
                            elif i > 0:
                                t.wait_ge(S.h["vec"], qsev(i - 1, 7))
                            S.inc("tns", t.matmul(ps_qS[:, :], ones2[:, :],
                                                  qe_s[:, o * 512:(o + 1) * 512],
                                                  start=True, stop=True), 1)
                        for o in range(NCH):
                            t.wait_ge(S.h["vec"], rcp(i))
                            if o >= 1:
                                t.wait_ge(S.h["vec"], nrm(i, o - 1))
                            S.inc("tns", t.matmul(ps_ex[:, :], ind2[:, :],
                                                  rc_bf[:, o * 512:(o + 1) * 512],
                                                  start=True, stop=True), 1)
                        for o in range(NCH):
                            t.wait_ge(S.h["vec"], nrm(i, o))
                            if o >= 2:
                                t.wait_ge(S.h["vec"], aev(i, o - 2))
                            elif i > 0:
                                t.wait_ge(S.h["vec"], aev(i - 1, o + 6))
                            for ss in range(2):
                                inst = t.matmul(
                                    PA[o % 2][ss * 64:(ss + 1) * 64, :],
                                    ctx_s[ss * 64:(ss + 1) * 64, o * 64:(o + 1) * 64],
                                    qe_s[ss * 64:(ss + 1) * 64, o * 512:(o + 1) * 512],
                                    start=True, stop=True, skip_group_check=True)
                            S.inc("tns", inst, 1)
                        for o in range(NCH):
                            t.wait_ge(S.h["vec"], aev(i, 7))
                            if o < 2:
                                t.wait_ge(S.h["scl"], exp2(i, o + 6))
                            else:
                                t.wait_ge(S.h["vec"], upd(i, o - 2))
                            for c in range(NCH):
                                inst = t.matmul(PP[o % 2][:, :],
                                                wprojT[:, c * C + o * 128: c * C + (o + 1) * 128],
                                                an_s[:, c * 512:(c + 1) * 512],
                                                start=(c == 0), stop=(c == NCH - 1))
                            S.inc("tns", inst, 1)

                @block.scalar
                def _(s):
                    for i in range(NT):
                        for o in range(NCH):
                            s.wait_ge(S.h["tns"], q_mm(i, o))
                            if i > 0:
                                s.wait_ge(S.h["tns"], at_mm(i - 1, o))
                            S.inc("scl", s.activation(qe_s[:, o * 512:(o + 1) * 512],
                                                      PP[o % 2][:, :], AF.Exp), 1)

                @block.vector
                def _(v):
                    for i in range(NT):
                        v.wait_ge(S.h["tns"], ab_mm(i))
                        v.tensor_copy(abx[:, 0:512], ps_ab[:, 0:512])
                        S.inc("vec", v.tensor_copy(abx[:, 512:1024], ps_ab[:, 512:1024]), 1)
                        v.wait_ge(S.h["ld0"], ld_done(i))
                        for c in range(NCH):
                            v.tensor_tensor(tmpf[:, :], xs2b[i % 2][:, c * 512:(c + 1) * 512],
                                            abx[:, 0:512], ALU.mult)
                            inst = v.tensor_tensor(xn[:, c * 512:(c + 1) * 512], tmpf[:, :],
                                                   abx[:, 512:1024], ALU.add)
                        S.inc("vec", inst, 1)
                        for o in range(NCH):
                            v.wait_ge(S.h["tns"], qs_mm(i, o))
                            S.inc("vec", v.tensor_copy(qS_sb[:, o * 512:(o + 1) * 512],
                                                       ps_qS[:, :]), 1)
                        if i > 0:
                            v.wait_ge(S.h["tns"], ex_mm(i - 1, 7))
                        v.reciprocal(qS_sb[:, :], qS_sb[:, :])
                        S.inc("vec", v.tensor_copy(rc_bf[:, :], qS_sb[:, :]), 1)
                        for o in range(NCH):
                            v.wait_ge(S.h["tns"], ex_mm(i, o))
                            S.inc("vec", v.tensor_tensor(qe_s[:, o * 512:(o + 1) * 512],
                                                         qe_s[:, o * 512:(o + 1) * 512],
                                                         ps_ex[:, :], ALU.mult), 1)
                        for o in range(NCH):
                            v.wait_ge(S.h["tns"], at_mm(i, o))
                            S.inc("vec", v.tensor_copy(an_s[:, o * 512:(o + 1) * 512],
                                                       PA[o % 2][:, :]), 1)
                        for o in range(NCH):
                            v.wait_ge(S.h["tns"], pj_mm(i, o))
                            S.inc("vec", v.scalar_tensor_tensor(
                                xs2b[i % 2][:, o * 512:(o + 1) * 512], PP[o % 2][:, :],
                                bpT[:, o:o + 1], xs2b[i % 2][:, o * 512:(o + 1) * 512],
                                ALU.add, ALU.add), 1)

            assert S.v["tns"] == B["tns"] + 41 * NT
            assert S.v["vec"] == B["vec"] + 35 * NT
            assert S.v["scl"] == B["scl"] + 8 * NT





def emit_prep_l1ffn(nc, D, S):
    with nc.Block() as block:
        @block.gpsimd
        def _(g):
            S.marks["w1_bf"] = S.inc("cast", g.dma_start(out=D["w1_bf"], in_=D["W1"]), 16)
            S.marks["w2_bf"] = S.inc("cast", g.dma_start(out=D["w2_bf"], in_=D["W2"]), 16)
            S.marks["w3_bf"] = S.inc("cast", g.dma_start(out=D["w3_bf"], in_=D["W3"]), 16)


def emit_moe(nc, D, S, tag):
    scr_in, scr_out = D["scr_in"], D["scr_out"]
    ident, onescol = D["cm_ident"], D["cm_onescol"]
    onescol_bf = D["cm_onescol_bf"]
    NTT = 4
    TT = 1024

    with ExitStack() as ctx:
        ab_bf = ctx.enter_context(nc.sbuf_tensor(f"{tag}_abbf", [128, T], BF16))
        a_bf = ab_bf[0:1, :]
        b_bf = ab_bf[64:65, :]
        abT = ctx.enter_context(nc.sbuf_tensor(f"{tag}_abT", [128, 64], F32))
        WgT_s = ctx.enter_context(nc.sbuf_tensor(f"{tag}_wgT", [128, 64], F32))
        W0 = ctx.enter_context(nc.sbuf_tensor(f"{tag}_W0", [128, 8], F32))
        ind8 = ctx.enter_context(nc.sbuf_tensor(f"{tag}_ind8", [8, 1024], BF16))

        with ExitStack() as c2:
            a_sb = c2.enter_context(nc.sbuf_tensor(f"{tag}_a32", [1, T], F32))
            b_sb = c2.enter_context(nc.sbuf_tensor(f"{tag}_b32", [1, T], F32))
            _emit_ln_stats(nc, D, S, c2, scr_in, a_sb, b_sb, tag)
            ps_abT = c2.enter_context(nc.psum_tensor(f"{tag}_psabT", [128, 64], F32))
            with nc.Block() as block:
                @block.scalar
                def _(s):
                    s.copy(a_bf, a_sb[:, :])
                    s.copy(b_bf, b_sb[:, :])

                @block.tensor
                def _(t):
                    for tc in range(32):
                        t.transpose(ps_abT[:, tc:tc + 1],
                                    a_sb[0:1, tc * 128:(tc + 1) * 128], ident[0:1, 0:1])
                        t.transpose(ps_abT[:, 32 + tc:33 + tc],
                                    b_sb[0:1, tc * 128:(tc + 1) * 128], ident[0:1, 0:1])
            with nc.Block() as block:
                @block.vector
                def _(v):
                    v.tensor_copy(abT[:, :], ps_abT[:, :])

        with ExitStack() as c2:
            wg_sb = c2.enter_context(nc.sbuf_tensor(f"{tag}_wg", [8, 1024], F32))
            wgtmp = c2.enter_context(nc.sbuf_tensor(f"{tag}_wgt", [8, 1024], F32))
            wgs = c2.enter_context(nc.sbuf_tensor(f"{tag}_wgs", [8, 1], F32))
            wgsr = c2.enter_context(nc.sbuf_tensor(f"{tag}_wgsr", [1, 8], F32))
            ps_g = c2.enter_context(nc.psum_tensor(f"{tag}_psg", [128, 80], F32))
            B0 = S.base("ld1")
            with nc.Block() as block:
                @block.sync
                def _(sync):
                    S.inc("ld1", sync.dma_start(out=wg_sb[:, :], in_=D["Wg"]), 16)

                @block.gpsimd
                def _(g):
                    g.memset(ind8[:, :], 1.0)
                    g.affine_select(out=ind8[:, :], in_=ind8[:, :], compare_op=ALU.is_ge,
                                    fill=0.0, base=0, pattern=[[1, 1024]], channel_multiplier=-128)
                    g.affine_select(out=ind8[:, :], in_=ind8[:, :], compare_op=ALU.is_ge,
                                    fill=0.0, base=127, pattern=[[-1, 1024]], channel_multiplier=128)

                @block.scalar
                def _(s):
                    s.wait_ge(S.h["ld1"], B0["ld1"] + 16)
                    s.activation(wgtmp[:, :], wg_sb[:, :], AF.Copy, accum_out=wgs[:, 0:1])

            with nc.Block() as block:
                @block.tensor
                def _(t):
                    for j in range(8):
                        t.transpose(ps_g[:, j * 8:(j + 1) * 8],
                                    wg_sb[0:8, j * 128:(j + 1) * 128], ident[0:8, 0:8])
                    t.transpose(ps_g[0:1, 64:72], wgs[0:8, 0:1], ident[0:8, 0:8])
            with nc.Block() as block:
                @block.vector
                def _(v):
                    v.tensor_copy(WgT_s[:, :], ps_g[:, 0:64])
                    v.tensor_copy(wgsr[:, :], ps_g[0:1, 64:72])
            with nc.Block() as block:
                @block.tensor
                def _(t):
                    t.matmul(ps_g[:, 72:80], onescol[0:1, :], wgsr[:, :], start=True, stop=True)
            with nc.Block() as block:
                @block.vector
                def _(v):
                    v.tensor_copy(W0[:, :], ps_g[:, 72:80])

        with ExitStack() as c2:
            xs = c2.enter_context(nc.sbuf_tensor(f"{tag}_xs", [128, NCH * TT], F32))
            xn = c2.enter_context(nc.sbuf_tensor(f"{tag}_xn", [128, NCH * TT], BF16))
            abx = c2.enter_context(nc.sbuf_tensor(f"{tag}_abx", [128, 1024], F32))
            tmpf = c2.enter_context(nc.sbuf_tensor(f"{tag}_tmpf", [128, 512], F32))
            w1T = c2.enter_context(nc.sbuf_tensor(f"{tag}_w1T", [128, NCH * HFF], BF16))
            w2T = c2.enter_context(nc.sbuf_tensor(f"{tag}_w2T", [128, NCH * HFF], BF16))
            w3T = c2.enter_context(nc.sbuf_tensor(f"{tag}_w3T", [128, 16 * C], BF16))
            gbuf = c2.enter_context(nc.sbuf_tensor(f"{tag}_g", [128, 16 * TT], BF16))
            h1s = c2.enter_context(nc.sbuf_tensor(f"{tag}_h1s", [128, TT], BF16))
            wexp = c2.enter_context(nc.sbuf_tensor(f"{tag}_wex", [128, TT], BF16))
            w_tok = c2.enter_context(nc.sbuf_tensor(f"{tag}_wtok", [128, 64], F32))
            w_fm = c2.enter_context(nc.sbuf_tensor(f"{tag}_wfm", [8, 1024], BF16))
            gate_sb = c2.enter_context(nc.sbuf_tensor(f"{tag}_gt", [128, 64], F32))
            msk = c2.enter_context(nc.sbuf_tensor(f"{tag}_msk", [128, 64], F32))
            g2b = c2.enter_context(nc.sbuf_tensor(f"{tag}_g2", [128, 64], F32))
            egb = c2.enter_context(nc.sbuf_tensor(f"{tag}_eg", [128, 64], F32))
            wrb = c2.enter_context(nc.sbuf_tensor(f"{tag}_wr", [128, 64], F32))
            d1b = c2.enter_context(nc.sbuf_tensor(f"{tag}_d1", [128, 64], F32))
            m1b = c2.enter_context(nc.sbuf_tensor(f"{tag}_m1", [128, 8], F32))
            nm1b = c2.enter_context(nc.sbuf_tensor(f"{tag}_nm1", [128, 8], F32))
            m2b = c2.enter_context(nc.sbuf_tensor(f"{tag}_m2", [128, 8], F32))
            ddb = c2.enter_context(nc.sbuf_tensor(f"{tag}_dd", [128, 8], F32))
            edb = c2.enter_context(nc.sbuf_tensor(f"{tag}_ed", [128, 8], F32))
            wvb = c2.enter_context(nc.sbuf_tensor(f"{tag}_wv", [128, 8], F32))
            PB1 = c2.enter_context(nc.psum_tensor(f"{tag}_PB1", [128, 1024], F32))
            PB2 = c2.enter_context(nc.psum_tensor(f"{tag}_PB2", [128, 1024], F32))
            PB3 = c2.enter_context(nc.psum_tensor(f"{tag}_PB3", [128, 1024], F32))
            psG = c2.enter_context(nc.psum_tensor(f"{tag}_psG", [128, 64], F32))
            psT = c2.enter_context(nc.psum_tensor(f"{tag}_psT", [128, 128], F32))

            B = S.base("ld0", "tns", "vec", "scl", "sto", "aux")
            TNS_T, SCL_T, VEC_T, AUX_T = 346, 130, 211, 16
            ab_mm = lambda i, r: B["tns"] + TNS_T * i + (r + 1)
            gate_mm = lambda i, u: B["tns"] + TNS_T * i + 2 + (u + 1)
            wT_mm = lambda i, u: B["tns"] + TNS_T * i + 10 + (u + 1)
            eb_t = lambda i, e: B["tns"] + TNS_T * i + 18 + 41 * e
            wexp_mm = lambda i, e: eb_t(i, e) + 1
            h1_mm = lambda i, e, hh: eb_t(i, e) + 1 + 2 * hh + 1
            h2_mm = lambda i, e, hh: eb_t(i, e) + 1 + 2 * hh + 2
            y_mm = lambda i, e, d: eb_t(i, e) + 33 + (d + 1)
            eg_s = lambda i: B["scl"] + SCL_T * i + 1
            ed_s = lambda i: B["scl"] + SCL_T * i + 2
            silu_s = lambda i, e, hh: B["scl"] + SCL_T * i + 2 + 16 * e + hh + 1
            ab_ev = lambda i, r: B["vec"] + VEC_T * i + (r + 1)
            apl = lambda i: B["vec"] + VEC_T * i + 3
            gva = lambda i: B["aux"] + AUX_T * i + 1
            g_sta = lambda i: B["aux"] + AUX_T * i + 10
            gvb = lambda i: B["aux"] + AUX_T * i + 16
            wfm_ev = lambda i, u: B["vec"] + VEC_T * i + 3 + (u + 1)
            eb_v = lambda i, e: B["vec"] + VEC_T * i + 11 + 25 * e
            wexp_ev = lambda i, e: eb_v(i, e) + 1
            gtt_v = lambda i, e, hh: eb_v(i, e) + 1 + (hh + 1)
            yev_v = lambda i, e, d: eb_v(i, e) + 17 + (d + 1)
            sto_done = lambda i: B["sto"] + 128 * (i + 1)
            ld_done = lambda i: B["ld0"] + 128 * (i + 1)
            wld = {}

            with nc.Block() as block:
                @block.sync
                def _(sync):
                    for i in range(NTT):
                        if i >= 1:
                            sync.wait_ge(S.h["sto"], sto_done(i - 1))
                        for c in range(NCH):
                            S.inc("ld0", sync.dma_start(
                                out=xs[:, c * TT:(c + 1) * TT],
                                in_=scr_in[c * 128:(c + 1) * 128, i * TT:(i + 1) * TT]), 16)
                        for e in range(NE):
                            if (i, e) != (0, 0):
                                pi, pe = (i, e - 1) if e > 0 else (i - 1, NE - 1)
                                sync.wait_ge(S.h["tns"], h2_mm(pi, pe, 15))
                            for c in range(NCH):
                                inst = sync.dma_start(
                                    out=w1T[:, c * HFF:(c + 1) * HFF],
                                    in_=D["w1_bf"][e * HFF:(e + 1) * HFF, c * 128:(c + 1) * 128],
                                    transpose=True)
                                S.inc("ld1", inst, 16)
                                inst = sync.dma_start(
                                    out=w2T[:, c * HFF:(c + 1) * HFF],
                                    in_=D["w2_bf"][e * HFF:(e + 1) * HFF, c * 128:(c + 1) * 128],
                                    transpose=True)
                                S.inc("ld1", inst, 16)
                            wld[(i, e, "w12")] = S.v["ld1"]
                            if (i, e) != (0, 0):
                                pi, pe = (i, e - 1) if e > 0 else (i - 1, NE - 1)
                                sync.wait_ge(S.h["tns"], y_mm(pi, pe, 7))
                            for hh in range(16):
                                inst = sync.dma_start(
                                    out=w3T[:, hh * C:(hh + 1) * C],
                                    in_=D["w3_bf"][e * C:(e + 1) * C, hh * 128:(hh + 1) * 128],
                                    transpose=True)
                                S.inc("ld1", inst, 16)
                            wld[(i, e, "w3")] = S.v["ld1"]
                        for o in range(NCH):
                            sync.wait_ge(S.h["vec"], yev_v(i, NE - 1, o))
                            S.inc("sto", sync.dma_start(
                                out=scr_out[o * 128:(o + 1) * 128, i * TT:(i + 1) * TT],
                                in_=xs[:, o * TT:(o + 1) * TT]), 16)
                    sync.wait_ge(S.h["sto"], sto_done(NTT - 1))

                @block.tensor
                def _(t):
                    for i in range(NTT):
                        if i >= 1:
                            t.wait_ge(S.h["vec"], wexp_ev(i - 1, NE - 1))
                        for r in range(2):
                            if r == 1:
                                t.wait_ge(S.h["vec"], ab_ev(i, 0))
                            t.matmul(PB1[:, 0:512], onescol_bf[0:1, :],
                                     a_bf[:, i * TT + r * 512: i * TT + (r + 1) * 512],
                                     start=True, stop=True)
                            inst = t.matmul(PB1[:, 512:1024], onescol_bf[64:65, :],
                                            b_bf[:, i * TT + r * 512: i * TT + (r + 1) * 512],
                                            start=True, stop=True)
                            S.inc("tns", inst, 1)
                        t.wait_ge(S.h["ld0"], ld_done(i))
                        if i > 0:
                            t.wait_ge(S.h["aux"], gva(i - 1))
                        for u in range(8):
                            for c in range(NCH):
                                inst = t.matmul(psG[:, u * 8:(u + 1) * 8],
                                                xs[:, c * TT + u * 128: c * TT + (u + 1) * 128],
                                                WgT_s[:, c * 8:(c + 1) * 8],
                                                start=(u == 0 and c == 0),
                                                stop=(c == NCH - 1), skip_group_check=True)
                            S.inc("tns", inst, 1)
                        t.wait_ge(S.h["aux"], gvb(i))
                        for u in range(8):
                            if (i, u) != (0, 0):
                                pi, pu = (i, u - 1) if u > 0 else (i - 1, 7)
                                t.wait_ge(S.h["vec"], wfm_ev(pi, pu))
                            S.inc("tns", t.transpose(psT[0:8, :], w_tok[:, u * 8:(u + 1) * 8],
                                                     ident[:, :]), 1)
                        for e in range(NE):
                            if e > 0:
                                t.wait_ge(S.h["vec"], wexp_ev(i, e - 1))
                            else:
                                t.wait_ge(S.h["vec"], ab_ev(i, 1))
                            t.wait_ge(S.h["vec"], wfm_ev(i, 7))
                            t.matmul(PB1[:, 0:512], ind8[:, e * 128:(e + 1) * 128],
                                     w_fm[:, 0:512], start=True, stop=True)
                            inst = t.matmul(PB1[:, 512:1024], ind8[:, e * 128:(e + 1) * 128],
                                            w_fm[:, 512:1024], start=True, stop=True)
                            S.inc("tns", inst, 1)
                            t.wait_ge(S.h["ld1"], wld[(i, e, "w12")])
                            if e == 0:
                                t.wait_ge(S.h["vec"], apl(i))
                            for hh in range(16):
                                if (e, hh) != (0, 0):
                                    ph = (e, hh - 1) if hh > 0 else (e - 1, 15)
                                    t.wait_ge(S.h["scl"], silu_s(i, ph[0], ph[1]))
                                elif i > 0:
                                    t.wait_ge(S.h["scl"], silu_s(i - 1, NE - 1, 15))
                                if hh == 0:
                                    if e > 0:
                                        t.wait_ge(S.h["vec"], yev_v(i, e - 1, 7))
                                    elif i > 0:
                                        t.wait_ge(S.h["vec"], yev_v(i - 1, NE - 1, 7))
                                for c in range(NCH):
                                    for nn in range(2):
                                        inst = t.matmul(
                                            PB2[:, nn * 512:(nn + 1) * 512],
                                            w1T[:, c * HFF + hh * 128: c * HFF + (hh + 1) * 128],
                                            xn[:, c * TT + nn * 512: c * TT + (nn + 1) * 512],
                                            start=(c == 0), stop=(c == NCH - 1))
                                S.inc("tns", inst, 1)
                                if (e, hh) != (0, 0):
                                    ph = (e, hh - 1) if hh > 0 else (e - 1, 15)
                                    t.wait_ge(S.h["vec"], gtt_v(i, ph[0], ph[1]))
                                elif i > 0:
                                    t.wait_ge(S.h["vec"], gtt_v(i - 1, NE - 1, 15))
                                for c in range(NCH):
                                    for nn in range(2):
                                        inst = t.matmul(
                                            PB3[:, nn * 512:(nn + 1) * 512],
                                            w2T[:, c * HFF + hh * 128: c * HFF + (hh + 1) * 128],
                                            xn[:, c * TT + nn * 512: c * TT + (nn + 1) * 512],
                                            start=(c == 0), stop=(c == NCH - 1))
                                S.inc("tns", inst, 1)
                            t.wait_ge(S.h["ld1"], wld[(i, e, "w3")])
                            t.wait_ge(S.h["vec"], gtt_v(i, e, 15))
                            for d in range(NCH):
                                PY = PB2 if d % 2 == 0 else PB3
                                if d >= 2:
                                    t.wait_ge(S.h["vec"], yev_v(i, e, d - 2))
                                for hh in range(16):
                                    for nn in range(2):
                                        inst = t.matmul(
                                            PY[:, nn * 512:(nn + 1) * 512],
                                            w3T[:, hh * C + d * 128: hh * C + (d + 1) * 128],
                                            gbuf[:, hh * TT + nn * 512: hh * TT + (nn + 1) * 512],
                                            start=(hh == 0), stop=(hh == 15))
                                S.inc("tns", inst, 1)

                @block.scalar
                def _(s):
                    for i in range(NTT):
                        s.wait_ge(S.h["aux"], g_sta(i))
                        S.inc("scl", s.activation(egb[:, :], d1b[:, :], AF.Exp), 1)
                        S.inc("scl", s.activation(edb[:, :], ddb[:, :], AF.Exp), 1)
                        for e in range(NE):
                            for hh in range(16):
                                s.wait_ge(S.h["tns"], h1_mm(i, e, hh))
                                if (e, hh) != (0, 0):
                                    ph = (e, hh - 1) if hh > 0 else (e - 1, 15)
                                    s.wait_ge(S.h["vec"], gtt_v(i, ph[0], ph[1]))
                                elif i > 0:
                                    s.wait_ge(S.h["vec"], gtt_v(i - 1, NE - 1, 15))
                                S.inc("scl", s.activation(h1s[:, :], PB2[:, :], AF.Silu), 1)

                @block.vector
                def _(v):
                    for i in range(NTT):
                        v.wait_ge(S.h["ld0"], ld_done(i))
                        if i > 0:
                            v.wait_ge(S.h["tns"], h2_mm(i - 1, NE - 1, 15))
                        for r in range(2):
                            v.wait_ge(S.h["tns"], ab_mm(i, r))
                            v.tensor_copy(abx[:, 0:512], PB1[:, 0:512])
                            S.inc("vec", v.tensor_copy(abx[:, 512:1024], PB1[:, 512:1024]), 1)
                            for c in range(NCH):
                                v.tensor_tensor(tmpf[:, 0:512],
                                                xs[:, c * TT + r * 512: c * TT + (r + 1) * 512],
                                                abx[:, 0:512], ALU.mult)
                                inst = v.tensor_tensor(
                                    xn[:, c * TT + r * 512: c * TT + (r + 1) * 512],
                                    tmpf[:, 0:512], abx[:, 512:1024], ALU.add)
                        S.inc("vec", inst, 1)
                        v.wait_ge(S.h["tns"], gate_mm(i, 7))

                        def step(inst):
                            S.inc("aux", inst, 1)
                            v.wait_ge(S.h["aux"], S.v["aux"])
                            return inst

                        def g3(buf):
                            return buf[:, :].rearrange("p (u e) -> p u e", e=8)

                        def bc_u(buf):
                            return bass.AP(buf, 0, [[8, 128], [1, 8], [0, 8]])

                        aT_v = bass.AP(abT, i * 8, [[64, 128], [1, 8], [0, 8]])
                        bT_v = bass.AP(abT, 32 + i * 8, [[64, 128], [1, 8], [0, 8]])
                        W0_v = bass.AP(W0, 0, [[8, 128], [0, 8], [1, 8]])
                        step(v.tensor_tensor(g3(gate_sb), g3(psG), aT_v, ALU.mult))
                        step(v.tensor_tensor(g3(d1b), W0_v, bT_v, ALU.mult))
                        step(v.tensor_tensor(gate_sb[:, :], gate_sb[:, :], d1b[:, :], ALU.add))
                        step(v.reduce_max(m1b[:, :].rearrange("p (u e) -> p u e", e=1),
                                          g3(gate_sb), axis=AX.X))
                        step(v.tensor_tensor(g3(msk), g3(gate_sb), bc_u(m1b), ALU.is_equal))
                        step(v.scalar_tensor_tensor(g2b[:, :], msk[:, :], -1e9, gate_sb[:, :],
                                                    ALU.mult, ALU.add))
                        step(v.reduce_max(m2b[:, :].rearrange("p (u e) -> p u e", e=1),
                                          g3(g2b), axis=AX.X))
                        step(v.tensor_tensor(g3(d1b), g3(gate_sb), bc_u(m1b), ALU.subtract))
                        step(v.tensor_scalar(nm1b[:, :], m1b[:, :], -1.0, None, ALU.mult))
                        step(v.tensor_tensor(ddb[:, :], m2b[:, :], nm1b[:, :], ALU.add))
                        v.wait_ge(S.h["scl"], ed_s(i))
                        step(v.tensor_tensor(g3(msk), g3(gate_sb), bc_u(m2b), ALU.is_ge))
                        step(v.tensor_tensor(wrb[:, :], egb[:, :], msk[:, :], ALU.mult))
                        step(v.tensor_scalar(edb[:, :], edb[:, :], 1.0, None, ALU.add))
                        step(v.reciprocal(wvb[:, :], edb[:, :]))
                        step(v.tensor_scalar(wvb[:, :], wvb[:, :], 0.5, None, ALU.mult))
                        step(v.tensor_tensor(g3(w_tok), g3(wrb), bc_u(wvb), ALU.mult))
                        for u in range(8):
                            v.wait_ge(S.h["tns"], wT_mm(i, u))
                            S.inc("vec", v.tensor_copy(w_fm[:, u * 128:(u + 1) * 128],
                                                       psT[0:8, :]), 1)
                        for e in range(NE):
                            v.wait_ge(S.h["tns"], wexp_mm(i, e))
                            S.inc("vec", v.tensor_copy(wexp[:, :], PB1[:, :]), 1)
                            for hh in range(16):
                                v.wait_ge(S.h["scl"], silu_s(i, e, hh))
                                v.wait_ge(S.h["tns"], h2_mm(i, e, hh))
                                v.tensor_tensor(gbuf[:, hh * TT:(hh + 1) * TT], h1s[:, :],
                                                PB3[:, :], ALU.mult)
                                S.inc("vec", v.tensor_tensor(gbuf[:, hh * TT:(hh + 1) * TT],
                                                             gbuf[:, hh * TT:(hh + 1) * TT],
                                                             wexp[:, :], ALU.mult), 1)
                            for d in range(NCH):
                                PY = PB2 if d % 2 == 0 else PB3
                                v.wait_ge(S.h["tns"], y_mm(i, e, d))
                                S.inc("vec", v.tensor_tensor(xs[:, d * TT:(d + 1) * TT],
                                                             xs[:, d * TT:(d + 1) * TT],
                                                             PY[:, :], ALU.add), 1)

            assert S.v["tns"] == B["tns"] + TNS_T * NTT, (S.v["tns"], B["tns"] + TNS_T * NTT)
            assert S.v["scl"] == B["scl"] + SCL_T * NTT
            assert S.v["vec"] == B["vec"] + VEC_T * NTT, (S.v["vec"], B["vec"] + VEC_T * NTT)
            assert S.v["aux"] == B["aux"] + AUX_T * NTT


STAGES = {
    "common": dict(emit=emit_common, io={}),
    "entry": dict(emit=emit_entry, io={"x_in": ((T, C), F32, "in")}),
    "exit": dict(emit=emit_exit, io={"x_out": ((T, C), F32, "out")}),
    "prep_l0ffn": dict(emit=emit_prep_l0ffn, io={
        "W_r": ((C, C), F32, "in"), "W_k": ((C, C), F32, "in"), "W_v": ((C, C), F32, "in"),
        "wr_bf": ((C, C), BF16, "internal"), "wk_bf": ((C, C), BF16, "internal"),
        "wv_bf": ((C, C), BF16, "internal")}),
    "s1": dict(emit=lambda nc, D, S: emit_l0ffn(nc, D, S, "f1"), io={
        "mu_r": ((1, C), F32, "in"), "mu_k": ((1, C), F32, "in"),
        "wr_bf": ((C, C), BF16, "internal"), "wk_bf": ((C, C), BF16, "internal"),
        "wv_bf": ((C, C), BF16, "internal")}),
    "s3": dict(emit=lambda nc, D, S: emit_l0ffn(nc, D, S, "f2"), io={
        "mu_r": ((1, C), F32, "in"), "mu_k": ((1, C), F32, "in"),
        "wr_bf": ((C, C), BF16, "internal"), "wk_bf": ((C, C), BF16, "internal"),
        "wv_bf": ((C, C), BF16, "internal")}),
    "prep_l0attn": dict(emit=emit_prep_l0attn, io={
        "WQ": ((C, C), F32, "in"), "WK": ((C, C), F32, "in"),
        "WV": ((C, C), F32, "in"), "WO": ((C, C), F32, "in"),
        "wq_bf": ((C, C), BF16, "internal"), "wk_bf2": ((C, C), BF16, "internal"),
        "wv_bf2": ((C, C), BF16, "internal"), "wo_bf": ((C, C), BF16, "internal")}),
    "prep_l1ffn": dict(emit=emit_prep_l1ffn, io={
        "W1": ((NE * HFF, C), F32, "in"), "W2": ((NE * HFF, C), F32, "in"),
        "W3": ((NE * C, HFF), F32, "in"),
        "w1_bf": ((NE * HFF, C), BF16, "internal"), "w2_bf": ((NE * HFF, C), BF16, "internal"),
        "w3_bf": ((NE * C, HFF), BF16, "internal")}),
    "s4": dict(emit=lambda nc, D, S: emit_moe(nc, D, S, "m1"), io={
        "Wg": ((NE, C), F32, "in"),
        "w1_bf": ((NE * HFF, C), BF16, "internal"), "w2_bf": ((NE * HFF, C), BF16, "internal"),
        "w3_bf": ((NE * C, HFF), BF16, "internal")}),
    "s6": dict(emit=lambda nc, D, S: emit_moe(nc, D, S, "m2"), io={
        "Wg": ((NE, C), F32, "in"),
        "w1_bf": ((NE * HFF, C), BF16, "internal"), "w2_bf": ((NE * HFF, C), BF16, "internal"),
        "w3_bf": ((NE * C, HFF), BF16, "internal")}),
    "prep_l1attn": dict(emit=emit_prep_l1attn, io={
        "WQKV": ((3072, C), F32, "in"), "WPROJ": ((C, C), F32, "in"),
        "wqkv_bf": ((3072, C), BF16, "internal"), "wproj_bf": ((C, C), BF16, "internal")}),
    "s5": dict(emit=emit_l1attn, io={
        "b_proj": ((1, C), F32, "in"),
        "wqkv_bf": ((3072, C), BF16, "internal"), "wproj_bf": ((C, C), BF16, "internal")}),
    "s2": dict(emit=emit_l0attn, io={
        "bo": ((1, C), F32, "in"),
        "wq_bf": ((C, C), BF16, "internal"), "wk_bf2": ((C, C), BF16, "internal"),
        "wv_bf2": ((C, C), BF16, "internal"), "wo_bf": ((C, C), BF16, "internal"),
        "q_scr": ((C, T), BF16, "internal"), "k_scr": ((C, T), BF16, "internal"),
        "v_scr": ((T, C), BF16, "internal"), "a_scr": ((16 * T, 64), BF16, "internal")}),
}





def build_program(stage_names, scr_mode="internal", uses_scr=True, scr_need="rw"):
    nc = bass.Bass("TRN2", target_bir_lowering=False)
    io = {}
    for sn in stage_names:
        for name, spec in STAGES[sn].get("io", {}).items():
            io[name] = spec
    D = {}
    for name, (shape, dtype, kind) in io.items():
        k = {"in": "ExternalInput", "out": "ExternalOutput", "internal": "Internal"}[kind]
        D[name] = nc.dram_tensor(name, list(shape), dtype, kind=k).ap()
    if uses_scr:
        if scr_mode == "internal":
            scr = nc.dram_tensor("scr", [C, T], F32, kind="Internal").ap()
            D["scr_in"] = scr
            D["scr_out"] = scr
        else:
            if "r" in scr_need:
                D["scr_in"] = nc.dram_tensor("scr_in", [C, T], F32, kind="ExternalInput").ap()
            if "w" in scr_need:
                D["scr_out"] = nc.dram_tensor("scr_out", [C, T], F32, kind="ExternalOutput").ap()
    with ExitStack() as stack:
        for name, (shape, dtype) in COMMON_SBUF.items():
            D[name] = stack.enter_context(nc.sbuf_tensor(name, shape, dtype))
        S = Sems(nc, stack, SEM_NAMES)
        for sn in stage_names:
            STAGES[sn]["emit"](nc, D, S)
    return nc


FUSED_STAGES = ["common", "prep_l0ffn", "entry", "prep_l0attn", "prep_l1attn", "prep_l1ffn",
                "s1", "s2", "s3", "s4", "s5", "s6", "exit"]


def kernel(**inputs):
    f32 = np.float32
    x = np.ascontiguousarray(np.asarray(inputs["L0_attn_x"], dtype=f32))
    n = 8

    def arr(name, shape=None):
        a = np.ascontiguousarray(np.asarray(inputs[name], dtype=f32))
        return a.reshape(shape) if shape is not None else a

    wmap = dict(W_r=arr("L0_ffn_W_r"), W_k=arr("L0_ffn_W_k"), W_v=arr("L0_ffn_W_v"),
                mu_r=arr("L0_ffn_mu_r", (1, C)), mu_k=arr("L0_ffn_mu_k", (1, C)),
                WQ=arr("L0_attn_Wq"), WK=arr("L0_attn_Wk"), WV=arr("L0_attn_Wv"),
                WO=arr("L0_attn_Wo"), bo=arr("L0_attn_bo", (1, C)),
                Wg=arr("L1_ffn_Wg"), W1=arr("L1_ffn_W1", (NE * HFF, C)),
                W2=arr("L1_ffn_W2", (NE * HFF, C)), W3=arr("L1_ffn_W3", (NE * C, HFF)),
                WQKV=arr("L1_attn_w_qkv"), WPROJ=arr("L1_attn_w_proj"),
                b_proj=arr("L1_attn_b_proj", (1, C)))

    nc = build_program(FUSED_STAGES, scr_mode="internal")
    in_maps = [dict(x_in=np.ascontiguousarray(x[i]), **wmap) for i in range(n)]
    res = run_bass_kernel_spmd(nc, in_maps, core_ids=list(range(n)))
    return np.stack([res.results[i]["x_out"] for i in range(n)], axis=0)
